# version 07
# baseline (speedup 1.0000x reference)
import numpy as np
import concourse.bass as bass
import concourse.mybir as mybir
from concourse.bass_utils import run_bass_kernel_spmd

F32 = mybir.dt.float32
BF16 = mybir.dt.bfloat16
I32 = mybir.dt.int32
AF = mybir.ActivationFunctionType
ALU = mybir.AluOpType
AX = mybir.AxisListType.X

D = 1024
HID = 4096
TOK = 2048
TFULL = 4096
CG = 8
HG = 32
NQT = 32
FLAG = 77777
POLL_CAP = 50000000
LN_EPS = 1e-5
RMS_EPS = 1e-6
TB4 = [(0, 512), (512, 512), (1024, 512), (1536, 512)]
TB8 = [(512 * i, 512) for i in range(8)]

PHASES = 99
DBG = False
LAST_RES = None

ENGS = ("tensor", "vector", "scalar", "gpsimd", "sync")


class Sem:
    def __init__(self, nc, name):
        self.h = nc.alloc_semaphore(name)
        self.n = 0

    def inc(self, by=1):
        self.n += by
        return (self.h, by)

    def now(self):
        return (self.h, self.n)


class Sched:
    def __init__(self):
        self.ops = {e: [] for e in ENGS}

    def on(self, eng, fn, waits=(), incs=()):
        waits = tuple(w for w in waits if w is not None)
        self.ops[eng].append((waits, fn, tuple(incs)))

    def runner(self, name):
        ops = self.ops[name]

        def f(e):
            for waits, fn, incs in ops:
                for (h, v) in waits:
                    e.wait_ge(h, v)
                inst = fn(e)
                for (h, by) in incs:
                    inst = inst.then_inc(h, by)

        return f


class SegAlloc:
    def __init__(self, nc, base, size, tag):
        self.nc = nc
        self.base = base
        self.size = size
        self.ptr = base
        self.tag = tag
        self.uid = 0

    def alloc(self, shape, dt, name=None):
        per = int(np.prod(shape[1:])) * (2 if dt == BF16 else 4)
        per = (per + 31) & ~31
        assert self.ptr + per <= self.base + self.size, (
            f"seg {self.tag} overflow: {name} needs {per}B at "
            f"{self.ptr - self.base}, cap {self.size}")
        off = self.ptr
        self.ptr = off + per
        self.uid += 1
        return self.nc.alloc_sbuf_tensor_at(
            name or f"{self.tag}{self.uid}", shape, dt, offset=off)

    def mark(self):
        return self.ptr

    def reset(self, m=None):
        self.ptr = self.base if m is None else m


def build_program():
    PH = PHASES
    dbg_outs = {}

    def dbg(name, shape, dt=F32):
        dbg_outs[name] = nc.dram_tensor(name, shape, dt,
                                        kind="ExternalOutput").ap()
        return dbg_outs[name]
    nc = bass.Bass("TRN2", target_bir_lowering=False)

    def din(name, shape, dt=F32):
        return nc.dram_tensor(name, shape, dt, kind="ExternalInput").ap()

    xin = din("xin", [D, TOK + 1])
    wkT = din("wkT", [CG, D, 128])
    wvT = din("wvT", [CG, D, 128])
    wrT = din("wrT", [CG, D, 128])
    woTa = din("woTa", [CG, D, 128])
    veca = din("veca", [128, 40])
    w1T = din("w1T", [HG, D, 128])
    w2T = din("w2T", [HG, D, 128])
    h1T = din("h1T", [HG, HID, 128])
    h2T = din("h2T", [HG, HID, 128])
    w3T = din("w3T", [CG, HID, 128])
    ffs = din("ffs", [128, 4 * HG + CG])
    wqT = din("wqT", [4, D, 128])
    wkTb = din("wkTb", [4, D, 128])
    wvTb = din("wvTb", [4, D, 128])
    woTb = din("woTb", [CG, D, 128])
    vecb = din("vecb", [128, 28])
    dmask = din("dmask", [128, 4 * 512])
    dmask01 = din("dmask01", [128, 4 * 512])
    nonce = din("nonce", [1, 1], I32)
    czero = din("czero", [128, 2])
    out_fm = nc.dram_tensor("out_fm", [D, TOK], F32, kind="ExternalOutput").ap()

    def shared(name, shape, dt):
        return nc.dram_tensor(name, shape, dt, kind="Internal",
                              addr_space="Shared")

    sh_carry = [shared(f"sh_carry{d}", [128, 2], F32) for d in range(CG)]
    sh_xn = [shared(f"sh_xn{s}", [D, TOK], BF16) for s in range(2)]
    sh_O = [shared(f"sh_O{s}", [512, TFULL], BF16) for s in range(2)]
    sh_fl = shared("sh_fl", [1, 16], I32)

    s_wld = Sem(nc, "s_wld")
    s_xld = Sem(nc, "s_xld")
    s_tn = Sem(nc, "s_tn")
    s_vc = Sem(nc, "s_vc")
    s_sc = Sem(nc, "s_sc")
    s_gp = Sem(nc, "s_gp")
    s_tr = Sem(nc, "s_tr")
    s_out = Sem(nc, "s_out")
    s_junk = Sem(nc, "s_junk")
    s_exA = Sem(nc, "s_exA")
    s_exB = Sem(nc, "s_exB")
    exA = [0]
    exB = [0]

    vec_a = nc.alloc_sbuf_tensor("vec_a", [128, 40], F32)
    mix1m = nc.alloc_sbuf_tensor("mix1m", [128, 24], F32)
    ffs_s = nc.alloc_sbuf_tensor("ffs_s", [128, 4 * HG + CG], F32)
    vec_b = nc.alloc_sbuf_tensor("vec_b", [128, 28], F32)
    bq_sc = nc.alloc_sbuf_tensor("bq_sc", [128, 4], F32)
    bq_st = [nc.alloc_sbuf_tensor(f"bq_st{p}", [128, 1], F32) for p in range(4)]
    bk_st = [nc.alloc_sbuf_tensor(f"bk_st{p}", [128, 1], F32) for p in range(4)]
    bv_st = [nc.alloc_sbuf_tensor(f"bv_st{p}", [128, 1], F32) for p in range(4)]
    bo_st = [nc.alloc_sbuf_tensor(f"bo_st{c}", [128, 1], F32) for c in range(CG)]
    ew_d = nc.alloc_sbuf_tensor("ew_d", [128, 3 * CG], F32)
    ew_w = [nc.alloc_sbuf_tensor(f"ew_w{d}", [128, 1], F32) for d in range(CG)]
    ones_col = nc.alloc_sbuf_tensor("ones_col", [128, 1], F32)
    ones_row = nc.alloc_sbuf_tensor("ones_row", [1, 128], F32)
    flagv = nc.alloc_sbuf_tensor("flagv", [1, 1], I32)
    flagr = nc.alloc_sbuf_tensor("flagr", [1, 1], I32)
    eps1 = nc.alloc_sbuf_tensor("eps1", [1, 1], F32)
    eps2 = nc.alloc_sbuf_tensor("eps2", [1, 1], F32)
    srow = nc.alloc_sbuf_tensor("srow", [1, 512], F32)
    mrow = nc.alloc_sbuf_tensor("mrow", [1, 512], F32)
    vrow = nc.alloc_sbuf_tensor("vrow", [1, 512], F32)
    brow = nc.alloc_sbuf_tensor("brow", [1, 512], F32)
    sq_t = [nc.alloc_sbuf_tensor(f"sq{i}", [128, 512], F32) for i in range(2)]
    carry_sb = nc.alloc_sbuf_tensor("carry_sb", [128, 2 * CG], F32)
    carry_out = nc.alloc_sbuf_tensor("carry_out", [128, 2 * CG], F32)
    o_acc = nc.alloc_sbuf_tensor("o_acc", [128, 128], F32)
    o_bf = nc.alloc_sbuf_tensor("o_bf", [128, 128], BF16)
    mneg_t = [nc.alloc_sbuf_tensor(f"mneg{h}", [128, 1], F32) for h in range(2)]
    mx8 = [nc.alloc_sbuf_tensor(f"mx8_{h}", [128, 8], F32) for h in range(2)]
    lc8 = [nc.alloc_sbuf_tensor(f"lc8_{h}", [128, 8], F32) for h in range(2)]
    lsum = [nc.alloc_sbuf_tensor(f"lsum{h}", [128, 1], F32) for h in range(2)]
    lsc0 = nc.alloc_sbuf_tensor("lsc0", [128, 8], F32)
    lsc = [lsc0, lsc0]
    rl_t = [nc.alloc_sbuf_tensor(f"rl{h}", [128, 1], F32) for h in range(2)]

    SEGX_SZ = 64 * 1024
    base0 = (nc.sbuf_base + 31) & ~31
    avail = nc.sbuf_top - base0 - 64
    segX = SegAlloc(nc, base0, SEGX_SZ, "sx")
    segY = SegAlloc(nc, base0 + SEGX_SZ, avail - SEGX_SZ, "sy")

    pb = [nc.alloc_psum_tensor(f"pb{i}", [128, 512], F32) for i in range(8)]
    bank_free = {i: None for i in range(8)}

    sch = Sched()

    def mm_group(bank, mms, waits=()):
        w = [x for x in waits if x is not None]
        if bank_free[bank] is not None:
            w.append(bank_free[bank])
        n = len(mms)
        for i, (lhsT, rhs, out) in enumerate(mms):
            def f(t, lhsT=lhsT, rhs=rhs, out=out, st=(i == 0), sp=(i == n - 1)):
                return t.matmul(out, lhsT, rhs, start=st, stop=sp)
            sch.on("tensor", f, waits=w if i == 0 else ())

    def t_fence():
        sch.on("tensor", lambda t: (t.drain(), t.nop())[1], incs=(s_tn.inc(),))
        return s_tn.now()

    sch.on("gpsimd", lambda g: g.dma_start(out=vec_a[:, :], in_=veca),
           incs=(s_xld.inc(16),))
    sch.on("gpsimd", lambda g: g.dma_start(out=ffs_s[:, :], in_=ffs),
           incs=(s_xld.inc(16),))
    sch.on("gpsimd", lambda g: g.dma_start(out=vec_b[:, :], in_=vecb),
           incs=(s_xld.inc(16),))
    sch.on("gpsimd", lambda g: g.dma_start(out=flagv[:, :], in_=nonce),
           incs=(s_xld.inc(16),))
    setup_ld = s_xld.now()
    regs = {}

    def fsetup_regs(g):
        for r in ("nonce", "pid", "fv", "cnt", "t1", "t2"):
            regs[r] = g.alloc_register("R" + r)
        g.reg_load(regs["nonce"], flagv[0:1, 0:1])
        g.reg_mov(regs["pid"], g.partition_id())
        g.reg_mod(regs["pid"], regs["pid"], 2)
        return g.nop()
    sch.on("gpsimd", fsetup_regs, waits=(setup_ld,))

    sch.on("vector", lambda v: v.memset(ones_col[:, :], 1.0),
           incs=(s_vc.inc(),))
    sch.on("vector", lambda v: v.memset(eps1[:, :], LN_EPS),
           incs=(s_vc.inc(),))
    sch.on("vector", lambda v: v.memset(eps2[:, :], RMS_EPS),
           incs=(s_vc.inc(),))
    sch.on("vector", lambda v: v.memset(ones_row[:, :], 1.0),
           incs=(s_vc.inc(),))
    sch.on("vector", lambda v: v.memset(mx8[0][:, :], 0.0),
           incs=(s_vc.inc(),))
    sch.on("vector", lambda v: v.memset(mx8[1][:, :], 0.0),
           incs=(s_vc.inc(),))
    sch.on("vector",
           lambda v: v.tensor_scalar(mix1m[:, 0:24], vec_a[:, 16:40], -1.0, 1.0,
                                     ALU.mult, ALU.add),
           waits=(setup_ld,), incs=(s_vc.inc(),))
    for p in range(4):
        sch.on("vector", lambda v, p=p: v.tensor_scalar_mul(
            bq_st[p][:, :], vec_b[:, p:p + 1], 0.125), incs=(s_vc.inc(),))
        sch.on("vector", lambda v, p=p: v.tensor_copy(bk_st[p][:, :],
                                                      vec_b[:, 4 + p:5 + p]),
               incs=(s_vc.inc(),))
        sch.on("vector", lambda v, p=p: v.tensor_copy(bv_st[p][:, :],
                                                      vec_b[:, 8 + p:9 + p]),
               incs=(s_vc.inc(),))
    for c in range(CG):
        sch.on("vector", lambda v, c=c: v.tensor_copy(bo_st[c][:, :],
                                                      vec_b[:, 20 + c:21 + c]),
               incs=(s_vc.inc(),))
    setup_done = s_vc.now()

    def emit_ln_block(x_tiles, col0, W, out_tiles, ocol0, src_ready=()):
        mm_group(6, [(ones_col[:, 0:1], x_tiles[c][:, col0:col0 + W],
                      pb[6][0:1, 0:W]) for c in range(CG)], waits=src_ready)
        sq_guard = {}
        for c in range(CG):
            w = list(src_ready)
            if c >= 2:
                w.append((s_tn.h, sq_guard[c - 2]))
            def fsq(s, c=c, col0=col0, W=W):
                return s.activation(sq_t[c % 2][:, 0:W],
                                    x_tiles[c][:, col0:col0 + W], AF.Square)
            sch.on("scalar", fsq, waits=w, incs=(s_sc.inc(),))
            sqv = s_sc.now()
            bw = [sqv]
            if c == 0 and bank_free[7] is not None:
                bw.append(bank_free[7])
            def fm(t, c=c, W=W):
                return t.matmul(pb[7][0:1, 0:W], ones_col[:, 0:1],
                                sq_t[c % 2][:, 0:W], start=(c == 0),
                                stop=(c == CG - 1))
            sch.on("tensor", fm, waits=bw)
            sch.on("tensor", lambda t: (t.drain(), t.nop())[1],
                   incs=(s_tn.inc(),))
            sq_guard[c] = s_tn.n
        stats_tok = (s_tn.h, s_tn.n)
        sch.on("vector", lambda v, W=W: v.tensor_scalar_mul(
            mrow[0:1, 0:W], pb[6][0:1, 0:W], 1.0 / D), waits=(stats_tok,),
            incs=(s_vc.inc(),))
        sch.on("vector", lambda v, W=W: v.tensor_mul(
            vrow[0:1, 0:W], mrow[0:1, 0:W], mrow[0:1, 0:W]),
            waits=(s_vc.now(),), incs=(s_vc.inc(),))
        sch.on("vector", lambda v, W=W: v.scalar_tensor_tensor(
            vrow[0:1, 0:W], pb[7][0:1, 0:W], 1.0 / D, vrow[0:1, 0:W],
            ALU.mult, ALU.subtract), waits=(s_vc.now(),),
            incs=(s_vc.inc(),))
        var_tok = s_vc.now()
        sch.on("scalar", lambda s, W=W: s.activation(
            srow[0:1, 0:W], vrow[0:1, 0:W], AF.Sqrt, bias=eps1[0:1, 0:1]),
            waits=(var_tok,), incs=(s_sc.inc(),))
        sch.on("vector", lambda v, W=W: v.reciprocal(vrow[0:1, 0:W],
                                                     srow[0:1, 0:W]),
               waits=(s_sc.now(),), incs=(s_vc.inc(),))
        sch.on("vector", lambda v, W=W: v.scalar_tensor_tensor(
            brow[0:1, 0:W], mrow[0:1, 0:W], -1.0, vrow[0:1, 0:W],
            ALU.mult, ALU.mult), waits=(s_vc.now(),), incs=(s_vc.inc(),))
        rows_tok = s_vc.now()
        bank_free[6] = rows_tok
        bank_free[7] = rows_tok
        mm_group(6, [(ones_row[0:1, :], vrow[0:1, 0:W], pb[6][:, 0:W])],
                 waits=(rows_tok,))
        mm_group(7, [(ones_row[0:1, :], brow[0:1, 0:W], pb[7][:, 0:W])])
        bc_tok = t_fence()
        for c in range(CG):
            def fa(v, c=c, col0=col0, W=W):
                return v.tensor_mul(sq_t[c % 2][:, 0:W],
                                    x_tiles[c][:, col0:col0 + W], pb[6][:, 0:W])
            def fb(v, c=c, ocol0=ocol0, W=W):
                return v.tensor_add(out_tiles[c][:, ocol0:ocol0 + W],
                                    sq_t[c % 2][:, 0:W], pb[7][:, 0:W])
            sch.on("vector", fa,
                   waits=(bc_tok,) if c == 0 else (s_vc.now(),),
                   incs=(s_vc.inc(),))
            sch.on("vector", fb, waits=(s_vc.now(),), incs=(s_vc.inc(),))
        apply_tok = s_vc.now()
        bank_free[6] = apply_tok
        bank_free[7] = apply_tok
        return apply_tok

    xk_b = [segY.alloc([128, TOK], BF16, f"xk{c}") for c in range(CG)]
    xv_b = [segY.alloc([128, TOK], BF16, f"xv{c}") for c in range(CG)]
    xr_b = [segY.alloc([128, TOK], BF16, f"xr{c}") for c in range(CG)]
    rwkv_b = [segY.alloc([128, TOK], BF16, f"rw{c}") for c in range(CG)]
    ln_bf = [segX.alloc([128, TOK + 1], BF16, f"ln{c}") for c in range(CG)]
    x_blk = [segX.alloc([128, 512], F32, f"xblk{c}") for c in range(CG)]
    mixtmp = segX.alloc([128, 512], F32, "mixtmp")

    LN1_BLOCKS = [(0, 512), (512, 512), (1024, 512), (1536, 512), (1537, 512)]
    prev_apply = setup_done
    for (col0, W) in LN1_BLOCKS:
        for c in range(CG):
            def fl(g, c=c, col0=col0, W=W):
                return g.dma_start(out=x_blk[c][:, 0:W],
                                   in_=xin[128 * c:128 * (c + 1),
                                           col0:col0 + W])
            sch.on("gpsimd", fl, waits=(prev_apply,) if c == 0 else (),
                   incs=(s_xld.inc(16),))
        ld_tok = s_xld.now()
        prev_apply = emit_ln_block(x_blk, 0, W, ln_bf, col0,
                                   src_ready=(ld_tok,) if col0 else
                                   (ld_tok, setup_done))
    ln_done = prev_apply

    for (mi, out_t, tmcol, omcol) in ((0, xk_b, 16, 0), (1, xv_b, 24, 8),
                                      (2, xr_b, 32, 16)):
        for c in range(CG):
            for (t0, W) in TB4:
                def f1(v, c=c, t0=t0, W=W, tmcol=tmcol):
                    return v.tensor_scalar_mul(
                        mixtmp[:, 0:W], ln_bf[c][:, 1 + t0:1 + t0 + W],
                        vec_a[:, tmcol + c:tmcol + c + 1])
                def f2(v, c=c, t0=t0, W=W, omcol=omcol, out_t=out_t):
                    return v.scalar_tensor_tensor(
                        out_t[c][:, t0:t0 + W], ln_bf[c][:, t0:t0 + W],
                        mix1m[:, omcol + c:omcol + c + 1], mixtmp[:, 0:W],
                        ALU.mult, ALU.add)
                first = (mi == 0 and c == 0 and t0 == 0)
                sch.on("vector", f1,
                       waits=(ln_done,) if first else (s_vc.now(),),
                       incs=(s_vc.inc(),))
                sch.on("vector", f2, waits=(s_vc.now(),),
                       incs=(s_vc.inc(),))
    mix_done = s_vc.now()
    if DBG:
        d_ln0 = dbg("d_ln0", [128, TOK + 1], BF16)
        d_xk0 = dbg("d_xk0", [128, TOK], BF16)
        d_xr0 = dbg("d_xr0", [128, TOK], BF16)
        sch.on("sync", lambda sy: sy.dma_start(out=d_ln0, in_=ln_bf[0][:, :]),
               waits=(mix_done,), incs=(s_out.inc(16),))
        sch.on("sync", lambda sy: sy.dma_start(out=d_xk0, in_=xk_b[0][:, :]),
               incs=(s_out.inc(16),))
        sch.on("sync", lambda sy: sy.dma_start(out=d_xr0, in_=xr_b[0][:, :]),
               incs=(s_out.inc(16),))

    last_done = mix_done
    segX.reset()
    sr_d = segX.alloc([128, TOK], F32, "sr_d")
    ek_d = segX.alloc([128, TOK + 1], F32, "ek_d")
    ekv_d = segX.alloc([128, TOK + 1], F32, "ekv_d")
    aa_d = segX.alloc([128, TOK + 1], F32, "aa_d")
    bb_d = segX.alloc([128, TOK + 1], F32, "bb_d")
    ew_big = segX.alloc([128, TOK + 1], F32, "ew_big")
    wk_blk = [segX.alloc([128, D], BF16, f"wkb{i}") for i in range(2)]
    wv_blk = [segX.alloc([128, D], BF16, f"wvb{i}") for i in range(2)]
    wr_blk = [segX.alloc([128, D], BF16, f"wrb{i}") for i in range(2)]

    wblk_free = {}

    wsems = {}

    def wsem(key):
        if key not in wsems:
            wsems[key] = Sem(nc, "sw_" + "_".join(str(k) for k in key))
        return wsems[key]

    def load_wblk(dst, src_ap, idx, free_key, groups=CG):
        sem = wsem(free_key)
        def fl(g, dst=dst, src_ap=src_ap, idx=idx, groups=groups):
            return g.dma_start(
                out=dst[:, 0:groups * 128].rearrange("p (c j) -> p c j",
                                                     c=groups),
                in_=src_ap[idx].rearrange("(c p) j -> p c j", p=128))
        sch.on("gpsimd", fl, waits=(wblk_free.get(free_key),),
               incs=(sem.inc(16),))
        return sem.now()

    def parity_open(g):
        return regs["pid"]

    def emit_poll(g, col, arm):
        sem, cntr = (s_exA, exA) if arm == 0 else (s_exB, exB)
        cntr[0] += 16
        g.dma_start(out=flagr[:, :],
                    in_=sh_fl.ap()[0:1, col:col + 1]).then_inc(sem.h, 16)
        g.wait_ge(sem.h, cntr[0])
        g.reg_mov(regs["cnt"], 0)

        def poll():
            g.dma_start(out=flagr[:, :],
                        in_=sh_fl.ap()[0:1, col:col + 1]).then_inc(s_junk.h, 16)
            g.reg_add(regs["cnt"], regs["cnt"], 1)
            g.reg_load(regs["fv"], flagr[0:1, 0:1])
            g.reg_alu(regs["t1"], regs["fv"], regs["nonce"], ALU.not_equal)
            g.reg_alu(regs["t2"], regs["cnt"], POLL_CAP, ALU.is_lt)
            g.reg_alu(regs["t1"], regs["t1"], regs["t2"], ALU.logical_and)
            return regs["t1"]

        g.reg_load(regs["fv"], flagr[0:1, 0:1])
        g.reg_alu(regs["t1"], regs["fv"], regs["nonce"], ALU.not_equal)
        with g.If_ne(regs["t1"], 0):
            with g.While(poll):
                g.nop()

    for k in (("r", 0), ("r", 1), ("k", 0), ("k", 1), ("v", 0), ("v", 1)):
        wblk_free[k] = mix_done

    prev_y_tok = None
    for d in range(CG if PH >= 2 else 0):
        buf = d % 2
        wr_tok = load_wblk(wr_blk[buf], wrT, d, ("r", buf))
        wk_tok = load_wblk(wk_blk[buf], wkT, d, ("k", buf))
        wv_tok = load_wblk(wv_blk[buf], wvT, d, ("v", buf))
        for bi, (t0, W) in enumerate(TB4):
            bank = bi % 2
            mm_group(bank, [(wr_blk[buf][:, 128 * c:128 * (c + 1)],
                             xr_b[c][:, t0:t0 + W], pb[bank][:, 0:W])
                            for c in range(CG)], waits=(wr_tok, mix_done))
            ptok = t_fence()
            def fsg(s, bank=bank, t0=t0, W=W):
                return s.activation(sr_d[:, t0:t0 + W], pb[bank][:, 0:W],
                                    AF.Sigmoid)
            sch.on("scalar", fsg, waits=(ptok, prev_y_tok),
                   incs=(s_sc.inc(),))
            bank_free[bank] = s_sc.now()
        sr_tok = s_sc.now()
        for bi, (t0, W) in enumerate(TB4):
            bank = 2 + bi % 2
            mm_group(bank, [(wk_blk[buf][:, 128 * c:128 * (c + 1)],
                             xk_b[c][:, t0:t0 + W], pb[bank][:, 0:W])
                            for c in range(CG)], waits=(wk_tok,))
            ptok = t_fence()
            def fek(s, bank=bank, t0=t0, W=W):
                return s.activation(ek_d[:, 1 + t0:1 + t0 + W],
                                    pb[bank][:, 0:W], AF.Exp)
            sch.on("scalar", fek, waits=(ptok, prev_y_tok),
                   incs=(s_sc.inc(),))
            bank_free[bank] = s_sc.now()
        ek_tok = s_sc.now()
        def fcarry_recv(g, d=d):
            pid = parity_open(g)
            with g.If_eq(pid, 0):
                exA[0] += 16
                g.dma_start(out=carry_sb[:, 2 * d:2 * d + 2],
                            in_=czero).then_inc(s_exA.h, 16)
                g.wait_ge(s_exA.h, exA[0])
            with g.Else():
                emit_poll(g, d, 1)
                exB[0] += 16
                g.dma_start(out=carry_sb[:, 2 * d:2 * d + 2],
                            in_=sh_carry[d].ap()).then_inc(s_exB.h, 16)
                g.wait_ge(s_exB.h, exB[0])
            return g.nop()
        sch.on("gpsimd", fcarry_recv, incs=(s_gp.inc(),))
        carry_tok = s_gp.now()
        for bi, (t0, W) in enumerate(TB4):
            bank = 4 + bi % 2
            mm_group(bank, [(wv_blk[buf][:, 128 * c:128 * (c + 1)],
                             xv_b[c][:, t0:t0 + W], pb[bank][:, 0:W])
                            for c in range(CG)], waits=(wv_tok,))
            ptok = t_fence()
            def fekv(v, bank=bank, t0=t0, W=W):
                return v.tensor_mul(ekv_d[:, 1 + t0:1 + t0 + W],
                                    pb[bank][:, 0:W],
                                    ek_d[:, 1 + t0:1 + t0 + W])
            sch.on("vector", fekv, waits=(ptok, ek_tok), incs=(s_vc.inc(),))
            bank_free[bank] = s_vc.now()
        sch.on("vector", lambda v, d=d: v.tensor_copy(
            ekv_d[:, 0:1], carry_sb[:, 2 * d:2 * d + 1]), waits=(carry_tok,),
            incs=(s_vc.inc(),))
        sch.on("vector", lambda v, d=d: v.tensor_copy(
            ek_d[:, 0:1], carry_sb[:, 2 * d + 1:2 * d + 2]),
            incs=(s_vc.inc(),))
        cpy_tok = s_vc.now()
        sch.on("scalar", lambda s, d=d: s.activation(
            ew_d[:, 3 * d:3 * d + 1], vec_a[:, d:d + 1], AF.Exp),
            waits=(setup_done,), incs=(s_sc.inc(),))
        sch.on("scalar", lambda s, d=d: s.activation(
            ew_d[:, 3 * d + 1:3 * d + 2], ew_d[:, 3 * d:3 * d + 1],
            AF.Exp, scale=-1.0), waits=(s_sc.now(),), incs=(s_sc.inc(),))
        sch.on("scalar", lambda s, d=d: s.activation(
            ew_d[:, 3 * d + 2:3 * d + 3], vec_a[:, 8 + d:9 + d], AF.Exp),
            incs=(s_sc.inc(),))
        sch.on("scalar", lambda s, d=d: s.activation(
            ew_big[:, 0:TOK], sr_d[:, 0:TOK], AF.Identity, scale=0.0,
            bias=ew_d[:, 3 * d + 1:3 * d + 2]), waits=(s_sc.now(), sr_tok),
            incs=(s_sc.inc(),))
        sch.on("scalar", lambda s, d=d: s.activation(
            ew_big[:, TOK:TOK + 1], sr_d[:, 0:1], AF.Identity, scale=0.0,
            bias=ew_d[:, 3 * d + 1:3 * d + 2]), incs=(s_sc.inc(),))
        ewb_tok = s_sc.now()
        sch.on("vector", lambda v: v.tensor_tensor_scan(
            aa_d[:, :], ew_big[:, :], ekv_d[:, :], 0.0, ALU.mult, ALU.add),
            waits=(ewb_tok, cpy_tok), incs=(s_vc.inc(),))
        sch.on("vector", lambda v: v.tensor_tensor_scan(
            bb_d[:, :], ew_big[:, :], ek_d[:, :], 0.0, ALU.mult, ALU.add),
            incs=(s_vc.inc(),))
        scans_tok = s_vc.now()
        sch.on("vector", lambda v, d=d: v.tensor_copy(
            carry_out[:, 2 * d:2 * d + 1], aa_d[:, TOK:TOK + 1]),
            waits=(scans_tok,), incs=(s_vc.inc(),))
        sch.on("vector", lambda v, d=d: v.tensor_copy(
            carry_out[:, 2 * d + 1:2 * d + 2], bb_d[:, TOK:TOK + 1]),
            incs=(s_vc.inc(),))
        scan_tok = s_vc.now()
        def fcarry_send(g, d=d):
            pid = parity_open(g)
            with g.If_eq(pid, 0):
                exA[0] += 16
                g.dma_start(out=sh_carry[d].ap(),
                            in_=carry_out[:, 2 * d:2 * d + 2]
                            ).then_inc(s_exA.h, 16)
                g.wait_ge(s_exA.h, exA[0])
                exA[0] += 16
                g.dma_start(out=sh_fl.ap()[0:1, d:d + 1],
                            in_=flagv[:, :]).then_inc(s_exA.h, 16)
            with g.Else():
                g.nop()
            return g.nop()
        sch.on("gpsimd", fcarry_send, waits=(scan_tok,))
        if DBG and d in (0, 1, CG - 1):
            sfx = str(d) if d != CG - 1 else "7"
            for nm, tt in ((f"d_aa{sfx}", aa_d), (f"d_bb{sfx}", bb_d),
                           (f"d_ewb{sfx}", ew_big), (f"d_srg{sfx}", sr_d),
                           (f"d_ekg{sfx}", ek_d), (f"d_ekvg{sfx}", ekv_d)):
                a = dbg(nm, [128, tt.shape[1]])
                def fdmp(sy, a=a, tt=tt):
                    return sy.dma_start(out=a, in_=tt[:, :])
                sch.on("sync", fdmp, waits=(scan_tok,), incs=(s_out.inc(16),))
            sch.on("vector", lambda v: v.nop(), waits=(s_out.now(),))
        sch.on("vector", lambda v, d=d: v.scalar_tensor_tensor(
            aa_d[:, 0:TOK], ekv_d[:, 1:TOK + 1], ew_d[:, 3 * d + 2:3 * d + 3],
            aa_d[:, 0:TOK], ALU.mult, ALU.add), waits=(scan_tok,),
            incs=(s_vc.inc(),))
        sch.on("vector", lambda v, d=d: v.scalar_tensor_tensor(
            bb_d[:, 0:TOK], ek_d[:, 1:TOK + 1], ew_d[:, 3 * d + 2:3 * d + 3],
            bb_d[:, 0:TOK], ALU.mult, ALU.add), incs=(s_vc.inc(),))
        sch.on("vector", lambda v: v.reciprocal(ew_big[:, 0:TOK],
                                                bb_d[:, 0:TOK]),
               waits=(s_vc.now(),), incs=(s_vc.inc(),))
        sch.on("vector", lambda v: v.tensor_mul(aa_d[:, 0:TOK],
                                                aa_d[:, 0:TOK],
                                                ew_big[:, 0:TOK]),
               waits=(s_vc.now(),), incs=(s_vc.inc(),))
        sch.on("vector", lambda v, d=d: v.tensor_mul(
            rwkv_b[d][:, :], aa_d[:, 0:TOK], sr_d[:, :]),
            waits=(s_vc.now(),), incs=(s_vc.inc(),))
        prev_y_tok = s_vc.now()
    rwkv_done = s_vc.now()
    if DBG and PH >= 2:
        for dd in range(CG):
            ap = dbg(f"d_rw{dd}", [128, TOK], BF16)
            def fdr(sy, ap=ap, dd=dd):
                return sy.dma_start(out=ap, in_=rwkv_b[dd][:, :])
            sch.on("sync", fdr, incs=(s_out.inc(16),))
    if PH >= 2:
        last_done = rwkv_done

    segX.reset()
    x1 = [segX.alloc([128, TOK], F32, f"x1_{c}") for c in range(CG)]
    segY.reset()
    woa_blk = [segY.alloc([128, D], BF16, f"woab{i}") for i in range(2)]
    xi_blk = segY.alloc([128, 512], F32, "xi_blk")
    wblk_free[("woa", 0)] = rwkv_done
    wblk_free[("woa", 1)] = rwkv_done
    xi_free = rwkv_done
    for cg in range(CG if PH >= 3 else 0):
        buf = cg % 2
        wtok = load_wblk(woa_blk[buf], woTa, cg, ("woa", buf))
        for bi, (t0, W) in enumerate(TB4):
            bank = bi % 2
            mm_group(bank, [(woa_blk[buf][:, 128 * dd:128 * (dd + 1)],
                             rwkv_b[dd][:, t0:t0 + W], pb[bank][:, 0:W])
                            for dd in range(CG)], waits=(wtok, rwkv_done))
            ptok = t_fence()
            def fxl(g, cg=cg, t0=t0, W=W):
                return g.dma_start(out=xi_blk[:, 0:W],
                                   in_=xin[128 * cg:128 * (cg + 1),
                                           1 + t0:1 + t0 + W])
            sch.on("gpsimd", fxl, waits=(xi_free,), incs=(s_xld.inc(16),))
            xl_tok = s_xld.now()
            def fadd(v, cg=cg, t0=t0, W=W, bank=bank):
                return v.tensor_add(x1[cg][:, t0:t0 + W], pb[bank][:, 0:W],
                                    xi_blk[:, 0:W])
            sch.on("vector", fadd, waits=(ptok, xl_tok), incs=(s_vc.inc(),))
            bank_free[bank] = s_vc.now()
            xi_free = s_vc.now()
        wblk_free[("woa", buf)] = s_vc.now()
    x1_tok = s_vc.now()
    if PH >= 3:
        last_done = x1_tok

    def emit_ffn(ready_tok, tag):
        segY.reset()
        mark = segY.mark()
        lnb = [segY.alloc([128, 512], BF16, f"lnb{c}{tag}") for c in range(CG)]
        hbuf = [segY.alloc([128, 512], BF16, f"hb{h}{tag}") for h in range(HG)]
        gbuf = [segY.alloc([128, 512], BF16, f"gb{h}{tag}") for h in range(HG)]
        sA = segY.alloc([128, 512], F32, f"sA{tag}")
        w12 = [segY.alloc([128, D], BF16, f"w12_{i}{tag}") for i in range(4)]
        wbig = [segY.alloc([128, HID], BF16, f"wbig{i}{tag}") for i in range(2)]
        wbig_free = {0: ready_tok, 1: ready_tok}
        w12_free = {0: ready_tok, 1: ready_tok}
        last_tok = ready_tok
        for (t0, W) in TB4:
            ln_tok = emit_ln_block(x1, t0, W, lnb, 0, src_ready=(last_tok,))
            for h in range(HG):
                b0, b1 = (0, 1) if h % 2 == 0 else (2, 3)
                wb1, wb2 = w12[2 * (h % 2)], w12[2 * (h % 2) + 1]
                def fl1(g, wb1=wb1, h=h):
                    return g.dma_start(
                        out=wb1[:, :].rearrange("p (c j) -> p c j", c=CG),
                        in_=w1T[h].rearrange("(c p) j -> p c j", p=128))
                def fl2(g, wb2=wb2, h=h):
                    return g.dma_start(
                        out=wb2[:, :].rearrange("p (c j) -> p c j", c=CG),
                        in_=w2T[h].rearrange("(c p) j -> p c j", p=128))
                sem12 = wsem(("w12", h % 2))
                sch.on("gpsimd", fl1, waits=(w12_free.get(h % 2),),
                       incs=(sem12.inc(16),))
                sch.on("gpsimd", fl2, incs=(sem12.inc(16),))
                wl_tok = sem12.now()
                mm_group(b0, [(wb1[:, 128 * c:128 * (c + 1)], lnb[c][:, 0:W],
                               pb[b0][:, 0:W]) for c in range(CG)],
                         waits=(wl_tok, ln_tok))
                mm_group(b1, [(wb2[:, 128 * c:128 * (c + 1)], lnb[c][:, 0:W],
                               pb[b1][:, 0:W]) for c in range(CG)])
                ptok = t_fence()
                w12_free[h % 2] = ptok
                def fsilu(s, b0=b0, h=h, W=W):
                    return s.activation(sA[:, 0:W], pb[b0][:, 0:W], AF.Silu,
                                        scale=ffs_s[:, h:h + 1])
                sch.on("scalar", fsilu, waits=(ptok, s_vc.now()),
                       incs=(s_sc.inc(),))
                si_tok = s_sc.now()
                bank_free[b0] = si_tok
                def fh(v, b1=b1, h=h, W=W):
                    return v.scalar_tensor_tensor(
                        hbuf[h][:, 0:W], pb[b1][:, 0:W],
                        ffs_s[:, HG + h:HG + h + 1], sA[:, 0:W],
                        ALU.mult, ALU.mult)
                sch.on("vector", fh, waits=(si_tok,), incs=(s_vc.inc(),))
                bank_free[b1] = s_vc.now()
            stage_in_tok = s_vc.now()
            src, dst = hbuf, gbuf
            for (wT, scol) in ((h1T, 2 * HG), (h2T, 3 * HG)):
                for og in range(HG):
                    bank = og % 4
                    wb = wbig[og % 2]
                    def flw(g, wb=wb, wT=wT, og=og):
                        return g.dma_start(
                            out=wb[:, :].rearrange("p (c j) -> p c j", c=HG),
                            in_=wT[og].rearrange("(c p) j -> p c j", p=128))
                    semb = wsem(("wbig", og % 2))
                    sch.on("gpsimd", flw, waits=(wbig_free.get(og % 2),),
                           incs=(semb.inc(16),))
                    wl_tok = semb.now()
                    mm_group(bank, [(wb[:, 128 * k:128 * (k + 1)],
                                     src[k][:, 0:W], pb[bank][:, 0:W])
                                    for k in range(HG)],
                             waits=(wl_tok, stage_in_tok))
                    ptok = t_fence()
                    wbig_free[og % 2] = ptok
                    def fg(s, bank=bank, og=og, W=W, scol=scol, dst=dst):
                        return s.activation(
                            dst[og][:, 0:W], pb[bank][:, 0:W], AF.Gelu,
                            scale=ffs_s[:, scol + og:scol + og + 1])
                    sch.on("scalar", fg, waits=(ptok,), incs=(s_sc.inc(),))
                    bank_free[bank] = s_sc.now()
                src, dst = dst, src
                stage_in_tok = s_sc.now()
            for cg in range(CG):
                bank = cg % 4
                wb = wbig[cg % 2]
                def flw3(g, wb=wb, cg=cg):
                    return g.dma_start(
                        out=wb[:, :].rearrange("p (c j) -> p c j", c=HG),
                        in_=w3T[cg].rearrange("(c p) j -> p c j", p=128))
                semb = wsem(("wbig", cg % 2))
                sch.on("gpsimd", flw3, waits=(wbig_free.get(cg % 2),),
                       incs=(semb.inc(16),))
                wl_tok = semb.now()
                mm_group(bank, [(wb[:, 128 * k:128 * (k + 1)],
                                 src[k][:, 0:W], pb[bank][:, 0:W])
                                for k in range(HG)],
                         waits=(wl_tok, stage_in_tok))
                ptok = t_fence()
                wbig_free[cg % 2] = ptok
                def fx(v, bank=bank, cg=cg, t0=t0, W=W):
                    return v.scalar_tensor_tensor(
                        x1[cg][:, t0:t0 + W], pb[bank][:, 0:W],
                        ffs_s[:, 4 * HG + cg:4 * HG + cg + 1],
                        x1[cg][:, t0:t0 + W], ALU.mult, ALU.add)
                sch.on("vector", fx, waits=(ptok,), incs=(s_vc.inc(),))
                bank_free[bank] = s_vc.now()
            last_tok = s_vc.now()
        segY.reset(mark)
        return last_tok

    if PH >= 4:
        x2_tok = emit_ffn(x1_tok, "f1")
        last_done = x2_tok
    else:
        x2_tok = x1_tok

    segY.reset()
    markY0 = segY.mark()
    xn_loc = [segY.alloc([128, 512], BF16, f"xnl{c}") for c in range(CG)]
    t_f32 = [segY.alloc([128, 512], F32, f"t3_{c}") for c in range(CG)]
    last = x2_tok
    for bi, (t0, W) in enumerate(TB4 if PH >= 5 else []):
        ln_tok = emit_ln_block(x1, t0, W, t_f32, 0, src_ready=(last,))
        sq_guard = {}
        for c in range(CG):
            w = [ln_tok] if c == 0 else []
            if c >= 2:
                w.append((s_tn.h, sq_guard[c - 2]))
            def fsq(s, c=c, W=W):
                return s.activation(sq_t[c % 2][:, 0:W], t_f32[c][:, 0:W],
                                    AF.Square)
            sch.on("scalar", fsq, waits=w, incs=(s_sc.inc(),))
            sqv = s_sc.now()
            bw = [sqv]
            if c == 0 and bank_free[7] is not None:
                bw.append(bank_free[7])
            def fm(t, c=c, W=W):
                return t.matmul(pb[7][0:1, 0:W], ones_col[:, 0:1],
                                sq_t[c % 2][:, 0:W], start=(c == 0),
                                stop=(c == CG - 1))
            sch.on("tensor", fm, waits=bw)
            sch.on("tensor", lambda t: (t.drain(), t.nop())[1],
                   incs=(s_tn.inc(),))
            sq_guard[c] = s_tn.n
        rms_tok = (s_tn.h, s_tn.n)
        sch.on("scalar", lambda s, W=W: s.activation(
            srow[0:1, 0:W], pb[7][0:1, 0:W], AF.Sqrt, bias=eps2[0:1, 0:1],
            scale=1.0 / D), waits=(rms_tok,), incs=(s_sc.inc(),))
        sch.on("vector", lambda v, W=W: v.reciprocal(vrow[0:1, 0:W],
                                                     srow[0:1, 0:W]),
               waits=(s_sc.now(),), incs=(s_vc.inc(),))
        rr_tok = s_vc.now()
        bank_free[7] = rr_tok
        mm_group(7, [(ones_row[0:1, :], vrow[0:1, 0:W], pb[7][:, 0:W])],
                 waits=(rr_tok,))
        bc_tok = t_fence()
        for c in range(CG):
            def fa(v, c=c, W=W):
                return v.tensor_mul(sq_t[c % 2][:, 0:W], t_f32[c][:, 0:W],
                                    pb[7][:, 0:W])
            def fb(v, c=c, W=W):
                return v.tensor_scalar_mul(xn_loc[c][:, 0:W],
                                           sq_t[c % 2][:, 0:W],
                                           vec_b[:, 12 + c:13 + c])
            sch.on("vector", fa,
                   waits=(bc_tok,) if c == 0 else (s_vc.now(),),
                   incs=(s_vc.inc(),))
            sch.on("vector", fb, waits=(s_vc.now(),), incs=(s_vc.inc(),))
        xn_tok = s_vc.now()
        bank_free[7] = xn_tok
        def fship(g, t0=t0, W=W):
            pid = parity_open(g)
            with g.If_eq(pid, 0):
                exA[0] += CG * 16
                for c in range(CG):
                    g.dma_start(
                        out=sh_xn[0].ap()[128 * c:128 * (c + 1), t0:t0 + W],
                        in_=xn_loc[c][:, 0:W]).then_inc(s_exA.h, 16)
                g.wait_ge(s_exA.h, exA[0])
            with g.Else():
                exB[0] += CG * 16
                for c in range(CG):
                    g.dma_start(
                        out=sh_xn[1].ap()[128 * c:128 * (c + 1), t0:t0 + W],
                        in_=xn_loc[c][:, 0:W]).then_inc(s_exB.h, 16)
                g.wait_ge(s_exB.h, exB[0])
            return g.nop()
        sch.on("gpsimd", fship, waits=(xn_tok,), incs=(s_gp.inc(),))
        last = s_gp.now()

    def fxflag(g):
        pid = parity_open(g)
        with g.If_eq(pid, 0):
            exA[0] += 16
            g.dma_start(out=sh_fl.ap()[0:1, 8:9],
                        in_=flagv[:, :]).then_inc(s_exA.h, 16)
            emit_poll(g, 9, 0)
        with g.Else():
            exB[0] += 16
            g.dma_start(out=sh_fl.ap()[0:1, 9:10],
                        in_=flagv[:, :]).then_inc(s_exB.h, 16)
            emit_poll(g, 8, 1)
        return g.nop()
    if PH >= 5:
        sch.on("gpsimd", fxflag, incs=(s_gp.inc(),))
        last_done = s_gp.now()
    xn_ready = s_gp.now()

    segY.reset(markY0)
    o_fm = [segY.alloc([128, TFULL], BF16, f"ofm{p}") for p in range(4)]
    markY1 = segY.mark()
    q_fm = segY.alloc([128, TFULL], BF16, "q_fm")
    k_fm = segY.alloc([128, TFULL], BF16, "k_fm")
    v_fm = segY.alloc([128, TFULL], BF16, "v_fm")
    v_tok = segY.alloc([128, TFULL], BF16, "v_tok")
    xn_st = [segY.alloc([128, 512], BF16, f"xst{c}") for c in range(CG)]
    wq_blk = segY.alloc([128, D], BF16, "wq_blk")
    wkb_blk = segY.alloc([128, D], BF16, "wkb_blk")
    wvb_blk = segY.alloc([128, D], BF16, "wvb_blk")
    p_bf = [segY.alloc([128, 512], BF16, f"pbf{h}") for h in range(2)]
    pt_bf = [segY.alloc([128, 512], BF16, f"ptb{h}") for h in range(2)]
    mask01_s = segY.alloc([128, 4 * 512], BF16, "mask01_s")
    s_msk = wsem(("mask",))
    sch.on("gpsimd", lambda g: g.dma_start(out=mask01_s[:, :], in_=dmask01),
           waits=(x2_tok,), incs=(s_msk.inc(16),))
    mask_ld = s_msk.now()

    s_vtr = wsem(("vtr",))
    s_otr = wsem(("otr",))
    s_ptr = [wsem(("ptr", i)) for i in range(2)]
    xnst_free = None
    vt_free = None
    p_free = [None, None]
    pt_free = [None, None]
    obf_free = None
    ofm_tok = []
    wblk_free[("wq", 0)] = x2_tok
    wblk_free[("wkb", 0)] = x2_tok
    wblk_free[("wvb", 0)] = x2_tok
    for p in range(4 if PH >= 6 else 0):
        tq_tok = load_wblk(wq_blk, wqT, p, ("wq", 0))
        tk_tok = load_wblk(wkb_blk, wkTb, p, ("wkb", 0))
        tv_tok = load_wblk(wvb_blk, wvTb, p, ("wvb", 0))
        pw_toks = {"q": tq_tok, "k": tk_tok, "v": tv_tok}
        for bi, (t0, W) in enumerate(TB8):
            slot = 0 if t0 < TOK else 1
            c0 = t0 % TOK
            for c in range(CG):
                def fxn(g, c=c, slot=slot, c0=c0, W=W):
                    return g.dma_start(
                        out=xn_st[c][:, 0:W],
                        in_=sh_xn[slot].ap()[128 * c:128 * (c + 1),
                                             c0:c0 + W])
                w = (xn_ready, xnst_free) if c == 0 else ()
                sch.on("gpsimd", fxn, waits=w, incs=(s_xld.inc(16),))
            xl_tok = s_xld.now()
            for (bank, wb, dstt, kind) in ((0, wq_blk, q_fm, "q"),
                                           (1, wkb_blk, k_fm, "k"),
                                           (2, wvb_blk, v_fm, "v")):
                mm_group(bank, [(wb[:, 128 * c:128 * (c + 1)],
                                 xn_st[c][:, 0:W], pb[bank][:, 0:W])
                                for c in range(CG)],
                         waits=(pw_toks[kind], xl_tok))
                ptok = t_fence()
                if kind == "q":
                    def fe(s, bank=bank, dstt=dstt, t0=t0, W=W, p=p):
                        return s.activation(dstt[:, t0:t0 + W],
                                            pb[bank][:, 0:W], AF.Identity,
                                            bias=bq_st[p][:, 0:1],
                                            scale=0.125)
                elif kind == "k":
                    def fe(s, bank=bank, dstt=dstt, t0=t0, W=W, p=p):
                        return s.activation(dstt[:, t0:t0 + W],
                                            pb[bank][:, 0:W], AF.Identity,
                                            bias=bk_st[p][:, 0:1])
                else:
                    def fe(s, bank=bank, dstt=dstt, t0=t0, W=W, p=p):
                        return s.activation(dstt[:, t0:t0 + W],
                                            pb[bank][:, 0:W], AF.Identity,
                                            bias=bv_st[p][:, 0:1])
                sch.on("scalar", fe, waits=(ptok,), incs=(s_sc.inc(),))
                bank_free[bank] = s_sc.now()
            xnst_free = s_sc.now()
        proj_tok = s_sc.now()
        wblk_free[("wq", 0)] = proj_tok
        wblk_free[("wkb", 0)] = proj_tok
        wblk_free[("wvb", 0)] = proj_tok
        if DBG and p == 0:
            for nm, tt in (("d_qfm", q_fm), ("d_kfm", k_fm), ("d_vfm", v_fm),
                           ("d_bqst", bq_st[0]), ("d_bqsc", bq_sc)):
                a = dbg(nm, [128, tt.shape[1]],
                        BF16 if tt.dtype == BF16 else F32)
                def fdq(sy, a=a, tt=tt):
                    return sy.dma_start(out=a, in_=tt[:, :])
                sch.on("sync", fdq, waits=(proj_tok,), incs=(s_out.inc(16),))
        for tj in range(TFULL // 128):
            def ftr(s, tj=tj):
                return s.dma_start(out=v_tok[:, 128 * tj:128 * (tj + 1)],
                                   in_=v_fm[:, 128 * tj:128 * (tj + 1)],
                                   transpose=True)
            w = (proj_tok, vt_free) if tj == 0 else ()
            sch.on("sync", ftr, waits=w, incs=(s_vtr.inc(16),))
        vtr_tok = s_vtr.now()

        if PH >= 6 and PH < 7:
            last_done = vtr_tok
        for qi in range(NQT if PH >= 7 else 0):
            nblk = qi // 4 + 1
            for h2 in range(2):
                r0 = 64 * h2
                for j in range(nblk):
                    mm_group(j, [(q_fm[r0:r0 + 64, 128 * qi:128 * (qi + 1)],
                                  k_fm[r0:r0 + 64, 512 * j:512 * (j + 1)],
                                  pb[j][:, 0:512])],
                             waits=(proj_tok,) if j == 0 else ())
                sc_tok = t_fence()
                for j in range(nblk):
                    def fmx(v, h2=h2, j=j):
                        return v.reduce_max(mx8[h2][:, j:j + 1],
                                            pb[j][:, 0:512], axis=AX)
                    sch.on("vector", fmx, waits=(sc_tok,) if j == 0 else (),
                           incs=(s_vc.inc(),))
                def fcmb(v, h2=h2, nblk=nblk):
                    return v.reduce_max(mneg_t[h2][:, :],
                                        mx8[h2][:, 0:nblk], axis=AX,
                                        negate=True)
                sch.on("vector", fcmb, waits=(s_vc.now(),),
                       incs=(s_vc.inc(),))
                mstat_tok = s_vc.now()
                pv_started = False
                for j in range(nblk):
                    diag = (j == nblk - 1)
                    slot = j % 2
                    def fpexp(s, h2=h2, j=j, slot=slot):
                        return s.activation(p_bf[slot][:, 0:512],
                                            pb[j][:, 0:512], AF.Exp,
                                            bias=mneg_t[h2][:, 0:1],
                                            accum_out=lc8[h2][:, j:j + 1])
                    sch.on("scalar", fpexp,
                           waits=(mstat_tok, p_free[slot]),
                           incs=(s_sc.inc(),))
                    pexp_tok = s_sc.now()
                    bank_free[j] = pexp_tok
                    if diag:
                        mv = qi % 4
                        def fpm(v, slot=slot, mv=mv):
                            return v.tensor_mul(
                                p_bf[slot][:, 0:512], p_bf[slot][:, 0:512],
                                mask01_s[:, 512 * mv:512 * (mv + 1)])
                        sch.on("vector", fpm, waits=(pexp_tok, mask_ld),
                               incs=(s_vc.inc(),))
                        def fls(v, h2=h2, j=j, slot=slot):
                            return v.reduce_sum(lc8[h2][:, j:j + 1],
                                                p_bf[slot][:, 0:512], axis=AX)
                        sch.on("vector", fls, waits=(s_vc.now(),),
                               incs=(s_vc.inc(),))
                        pexp_tok = s_vc.now()
                    for u in range(4):
                        def ftp(sy, slot=slot, u=u):
                            return sy.dma_start(
                                out=pt_bf[slot][:, 128 * u:128 * (u + 1)],
                                in_=p_bf[slot][:, 128 * u:128 * (u + 1)],
                                transpose=True)
                        w = (pexp_tok, pt_free[slot]) if u == 0 else ()
                        sch.on("sync", ftp, waits=w,
                               incs=(s_ptr[slot].inc(16),))
                    p_free[slot] = s_ptr[slot].now()
                    ptr_tok = s_ptr[slot].now()
                    mms = [(pt_bf[slot][:, 128 * u:128 * (u + 1)],
                            v_tok[:, 128 * (4 * j + u) + r0:
                                  128 * (4 * j + u) + r0 + 64],
                            pb[0][:, 64 * h2:64 * h2 + 64]) for u in range(4)]
                    w = [ptr_tok, vtr_tok]
                    if j == 0:
                        w.append(pexp_tok if not diag else pexp_tok)
                    n = len(mms)
                    for i, (lhsT, rhs, outp) in enumerate(mms):
                        def fpv(t, lhsT=lhsT, rhs=rhs, outp=outp,
                                st=(j == 0 and i == 0),
                                sp=(j == nblk - 1 and i == n - 1)):
                            return t.matmul(outp, lhsT, rhs, start=st, stop=sp)
                        sch.on("tensor", fpv, waits=w if i == 0 else ())
                pv_tok = t_fence()
                pt_free[0] = pv_tok
                pt_free[1] = pv_tok
                def flsum(s, h2=h2, nblk=nblk):
                    return s.activation(lsc[h2][:, 0:nblk],
                                        lc8[h2][:, 0:nblk], AF.Identity,
                                        accum_out=lsum[h2][:, :])
                sch.on("scalar", flsum, waits=(pv_tok,), incs=(s_sc.inc(),))
                lsum_tok = s_sc.now()
                def frl(v, h2=h2):
                    return v.reciprocal(rl_t[h2][:, :], lsum[h2][:, :])
                sch.on("vector", frl, waits=(lsum_tok,), incs=(s_vc.inc(),))
                rl_tok = s_vc.now()
                def fOev(s, h2=h2):
                    return s.activation(o_bf[:, 64 * h2:64 * h2 + 64],
                                        pb[0][:, 64 * h2:64 * h2 + 64],
                                        AF.Identity, scale=rl_t[h2][:, 0:1])
                sch.on("scalar", fOev, waits=(rl_tok, obf_free),
                       incs=(s_sc.inc(),))
                bank_free[0] = s_sc.now()
            fin_tok = s_sc.now()
            def ftro(s, p=p, qi=qi):
                return s.dma_start(out=o_fm[p][:, 128 * qi:128 * (qi + 1)],
                                   in_=o_bf[:, :], transpose=True)
            sch.on("sync", ftro, waits=(fin_tok,), incs=(s_otr.inc(16),))
            obf_free = s_otr.now()
        vt_free = s_tn.now()
        if DBG and p == 0 and PH >= 7:
            for nm, tt in (("d_vtok", v_tok), ("d_ofm0", o_fm[0])):
                a = dbg(nm, [128, tt.shape[1]], BF16)
                def fdv(sy, a=a, tt=tt):
                    return sy.dma_start(out=a, in_=tt[:, :])
                sch.on("sync", fdv, waits=(s_otr.now(),),
                       incs=(s_out.inc(16),))
        ofm_tok.append(s_otr.now())
        if PH >= 7:
            last_done = s_otr.now()

    segY.reset(markY1)
    o_full = [segY.alloc([128, TOK], BF16, f"ofl{g}") for g in range(CG)]
    wob_blk = [segY.alloc([128, D], BF16, f"wob{i}") for i in range(2)]
    xnm_st = [segY.alloc([128, 512], BF16, f"xnm{c}") for c in range(CG)]
    osum = segY.alloc([128, 512], F32, "osum")

    flash_done = s_tn.now()

    def foship(g):
        pid = parity_open(g)
        with g.If_eq(pid, 0):
            exA[0] += 4 * 16
            for p in range(4):
                g.dma_start(out=sh_O[0].ap()[128 * p:128 * (p + 1), :],
                            in_=o_fm[p][:, :]).then_inc(s_exA.h, 16)
            g.wait_ge(s_exA.h, exA[0])
            exA[0] += 16
            g.dma_start(out=sh_fl.ap()[0:1, 10:11],
                        in_=flagv[:, :]).then_inc(s_exA.h, 16)
            emit_poll(g, 11, 0)
            exA[0] += CG * 16
            for gg in range(CG):
                g.dma_start(
                    out=o_full[gg][:, :],
                    in_=sh_O[gg // 4].ap()[128 * (gg % 4):128 * (gg % 4 + 1),
                                           0:TOK]).then_inc(s_exA.h, 16)
            g.wait_ge(s_exA.h, exA[0])
        with g.Else():
            exB[0] += 4 * 16
            for p in range(4):
                g.dma_start(out=sh_O[1].ap()[128 * p:128 * (p + 1), :],
                            in_=o_fm[p][:, :]).then_inc(s_exB.h, 16)
            g.wait_ge(s_exB.h, exB[0])
            exB[0] += 16
            g.dma_start(out=sh_fl.ap()[0:1, 11:12],
                        in_=flagv[:, :]).then_inc(s_exB.h, 16)
            emit_poll(g, 10, 1)
            exB[0] += CG * 16
            for gg in range(CG):
                g.dma_start(
                    out=o_full[gg][:, :],
                    in_=sh_O[gg // 4].ap()[128 * (gg % 4):128 * (gg % 4 + 1),
                                           TOK:TFULL]).then_inc(s_exB.h, 16)
            g.wait_ge(s_exB.h, exB[0])
        return g.nop()
    if PH >= 8:
        sch.on("gpsimd", foship, waits=tuple(ofm_tok) + (flash_done,),
               incs=(s_gp.inc(),))
    o_ready = s_gp.now()

    wblk_free[("wob", 0)] = flash_done
    wblk_free[("wob", 1)] = flash_done
    xnm_free = None
    for bi, (t0, W) in enumerate(TB4 if PH >= 8 else []):
        def fxnm(g, t0=t0, W=W):
            pid = parity_open(g)
            with g.If_eq(pid, 0):
                exA[0] += CG * 16
                for c in range(CG):
                    g.dma_start(
                        out=xnm_st[c][:, 0:W],
                        in_=sh_xn[0].ap()[128 * c:128 * (c + 1),
                                          t0:t0 + W]).then_inc(s_exA.h, 16)
                g.wait_ge(s_exA.h, exA[0])
            with g.Else():
                exB[0] += CG * 16
                for c in range(CG):
                    g.dma_start(
                        out=xnm_st[c][:, 0:W],
                        in_=sh_xn[1].ap()[128 * c:128 * (c + 1),
                                          t0:t0 + W]).then_inc(s_exB.h, 16)
                g.wait_ge(s_exB.h, exB[0])
            return g.nop()
        sch.on("gpsimd", fxnm, waits=(o_ready, xnm_free), incs=(s_gp.inc(),))
        xnm_tok = s_gp.now()
        for cg in range(CG):
            buf = cg % 2
            wtok = load_wblk(wob_blk[buf], woTb, cg, ("wob", buf))
            bank = cg % 4
            mm_group(bank, [(wob_blk[buf][:, 128 * dd:128 * (dd + 1)],
                             o_full[dd][:, t0:t0 + W], pb[bank][:, 0:W])
                            for dd in range(CG)], waits=(wtok, o_ready))
            ptok = t_fence()
            wblk_free[("wob", buf)] = ptok
            def fbo(s, bank=bank, cg=cg, W=W):
                return s.activation(osum[:, 0:W], pb[bank][:, 0:W], AF.Identity,
                                    bias=bo_st[cg][:, 0:1])
            sch.on("scalar", fbo, waits=(ptok, s_vc.now()),
                       incs=(s_sc.inc(),))
            bo_tok = s_sc.now()
            bank_free[bank] = bo_tok
            def fr1(v, cg=cg, W=W):
                return v.tensor_add(osum[:, 0:W], osum[:, 0:W],
                                    xnm_st[cg][:, 0:W])
            def fr2(v, cg=cg, t0=t0, W=W):
                return v.tensor_add(x1[cg][:, t0:t0 + W],
                                    x1[cg][:, t0:t0 + W], osum[:, 0:W])
            sch.on("vector", fr1, waits=(bo_tok, xnm_tok),
                   incs=(s_vc.inc(),))
            sch.on("vector", fr2, waits=(s_vc.now(),), incs=(s_vc.inc(),))
        xnm_free = s_vc.now()
    x3_tok = s_vc.now()
    if PH >= 8:
        last_done = x3_tok

    if PH >= 9:
        x4_tok = emit_ffn(x3_tok, "f2")
        last_done = x4_tok
    else:
        x4_tok = last_done

    for cg in range(CG):
        def fo2(s, cg=cg):
            return s.dma_start(out=out_fm[128 * cg:128 * (cg + 1), :],
                               in_=x1[cg][:, :])
        sch.on("sync", fo2, waits=(last_done,) if cg == 0 else (),
               incs=(s_out.inc(16),))
    sch.on("sync", lambda s: s.nop(), waits=(s_out.now(),))

    with nc.Block() as block:
        block.tensor(sch.runner("tensor"))
        block.vector(sch.runner("vector"))
        block.scalar(sch.runner("scalar"))
        block.gpsimd(sch.runner("gpsimd"))
        block.sync(sch.runner("sync"))

    return nc


def _blk(wt, og):
    i, o = wt.shape
    return np.ascontiguousarray(wt.reshape(i, og, 128).transpose(1, 0, 2))


def _col(v):
    return np.ascontiguousarray(v.reshape(-1, 128).T)


def kernel(**inp):
    x = np.asarray(inp["L0_attn_a_x"], np.float32)
    g = lambda n: np.asarray(inp[n], np.float32)

    base = dict(
        wkT=_blk(g("L0_attn_a_Wk").T, CG),
        wvT=_blk(g("L0_attn_a_Wv").T, CG),
        wrT=_blk(g("L0_attn_a_Wr").T, CG),
        woTa=_blk(g("L0_attn_a_Wo").T, CG),
        veca=np.concatenate([_col(g("L0_attn_a_time_decay")),
                             _col(g("L0_attn_a_time_first")),
                             _col(g("L0_attn_a_time_mix_k")),
                             _col(g("L0_attn_a_time_mix_v")),
                             _col(g("L0_attn_a_time_mix_r"))], axis=1),
        w1T=_blk(g("L0_ffn_w1_w").T, HG),
        w2T=_blk(g("L0_ffn_w2_w").T, HG),
        h1T=_blk(g("L0_ffn_h1_w").T, HG),
        h2T=_blk(g("L0_ffn_h2_w").T, HG),
        w3T=_blk(g("L0_ffn_w3_w").T, CG),
        ffs=np.concatenate([_col(g("L0_ffn_w1_s")), _col(g("L0_ffn_w2_s")),
                            _col(g("L0_ffn_h1_s")), _col(g("L0_ffn_h2_s")),
                            _col(g("L0_ffn_w3_s"))], axis=1),
        woTb=_blk(g("L0_attn_b_Wo").T, CG),
    )
    dmask = np.zeros((128, 4 * 512), np.float32)
    for mv in range(4):
        keep = np.arange(512)[None, :] <= (128 * mv + np.arange(128)[:, None])
        dmask[:, 512 * mv:512 * (mv + 1)] = np.where(keep, 0.0, -1e30)
    base["dmask"] = dmask
    base["dmask01"] = (dmask == 0.0).astype(np.float32)
    base["czero"] = np.zeros((128, 2), np.float32)
    base["nonce"] = np.array([[np.int64(np.random.randint(1, 2**31 - 1))]],
                             dtype=np.int32)

    in_maps = []
    for b in range(4):
        for hh in range(2):
            m = dict(base)
            t0 = TOK * hh
            xin = np.empty((D, TOK + 1), np.float32)
            if hh == 0:
                xin[:, 0] = 0.0
            else:
                xin[:, 0] = x[b, TOK - 1, :]
            xin[:, 1:] = x[b, t0:t0 + TOK, :].T
            m["xin"] = np.ascontiguousarray(xin)
            hs = slice(512 * hh, 512 * (hh + 1))
            m["wqT"] = _blk(g("L0_attn_b_Wq")[hs, :].T, 4)
            m["wkTb"] = _blk(g("L0_attn_b_Wk")[hs, :].T, 4)
            m["wvTb"] = _blk(g("L0_attn_b_Wv")[hs, :].T, 4)
            m["vecb"] = np.concatenate(
                [_col(g("L0_attn_b_bq")[hs]), _col(g("L0_attn_b_bk")[hs]),
                 _col(g("L0_attn_b_bv")[hs]), _col(g("L0_attn_b_norm_w")),
                 _col(g("L0_attn_b_bo"))], axis=1)
            in_maps.append(m)

    nc = build_program()
    res = run_bass_kernel_spmd(nc, in_maps, core_ids=list(range(8)))
    global LAST_RES
    LAST_RES = res

    out = np.empty((4, TFULL, D), np.float32)
    for b in range(4):
        for hh in range(2):
            out[b, TOK * hh:TOK * (hh + 1), :] = \
                res.results[2 * b + hh]["out_fm"].T
    return out


# version 08
# speedup vs baseline: 1.0242x; 1.0242x over previous
import numpy as np
import concourse.bass as bass
import concourse.mybir as mybir
from concourse.bass_utils import run_bass_kernel_spmd

F32 = mybir.dt.float32
BF16 = mybir.dt.bfloat16
I32 = mybir.dt.int32
AF = mybir.ActivationFunctionType
ALU = mybir.AluOpType
AX = mybir.AxisListType.X

D = 1024
HID = 4096
TOK = 2048
TFULL = 4096
CG = 8
HG = 32
NQT = 32
FLAG = 77777
POLL_CAP = 50000000
LN_EPS = 1e-5
RMS_EPS = 1e-6
TB4 = [(0, 512), (512, 512), (1024, 512), (1536, 512)]
TB8 = [(512 * i, 512) for i in range(8)]

PHASES = 99
DBG = False
LAST_RES = None

ENGS = ("tensor", "vector", "scalar", "gpsimd", "sync")


class Sem:
    def __init__(self, nc, name):
        self.h = nc.alloc_semaphore(name)
        self.n = 0

    def inc(self, by=1):
        self.n += by
        return (self.h, by)

    def now(self):
        return (self.h, self.n)


class Sched:
    def __init__(self):
        self.ops = {e: [] for e in ENGS}

    def on(self, eng, fn, waits=(), incs=()):
        waits = tuple(w for w in waits if w is not None)
        self.ops[eng].append((waits, fn, tuple(incs)))

    def runner(self, name):
        ops = self.ops[name]

        def f(e):
            for waits, fn, incs in ops:
                for (h, v) in waits:
                    e.wait_ge(h, v)
                inst = fn(e)
                for (h, by) in incs:
                    inst = inst.then_inc(h, by)

        return f


class SegAlloc:
    def __init__(self, nc, base, size, tag):
        self.nc = nc
        self.base = base
        self.size = size
        self.ptr = base
        self.tag = tag
        self.uid = 0

    def alloc(self, shape, dt, name=None):
        per = int(np.prod(shape[1:])) * (2 if dt == BF16 else 4)
        per = (per + 31) & ~31
        assert self.ptr + per <= self.base + self.size, (
            f"seg {self.tag} overflow: {name} needs {per}B at "
            f"{self.ptr - self.base}, cap {self.size}")
        off = self.ptr
        self.ptr = off + per
        self.uid += 1
        return self.nc.alloc_sbuf_tensor_at(
            name or f"{self.tag}{self.uid}", shape, dt, offset=off)

    def mark(self):
        return self.ptr

    def reset(self, m=None):
        self.ptr = self.base if m is None else m


def build_program():
    PH = PHASES
    dbg_outs = {}

    def dbg(name, shape, dt=F32):
        dbg_outs[name] = nc.dram_tensor(name, shape, dt,
                                        kind="ExternalOutput").ap()
        return dbg_outs[name]
    nc = bass.Bass("TRN2", target_bir_lowering=False)

    def din(name, shape, dt=F32):
        return nc.dram_tensor(name, shape, dt, kind="ExternalInput").ap()

    xin = din("xin", [D, TOK + 1])
    wkT = din("wkT", [CG, D, 128])
    wvT = din("wvT", [CG, D, 128])
    wrT = din("wrT", [CG, D, 128])
    woTa = din("woTa", [CG, D, 128])
    veca = din("veca", [128, 40])
    w1T = din("w1T", [HG, D, 128])
    w2T = din("w2T", [HG, D, 128])
    h1T = din("h1T", [HG, HID, 128])
    h2T = din("h2T", [HG, HID, 128])
    w3T = din("w3T", [CG, HID, 128])
    ffs = din("ffs", [128, 4 * HG + CG])
    wqT = din("wqT", [4, D, 128])
    wkTb = din("wkTb", [4, D, 128])
    wvTb = din("wvTb", [4, D, 128])
    woTb = din("woTb", [CG, D, 128])
    vecb = din("vecb", [128, 28])
    dmask = din("dmask", [128, 4 * 512])
    dmask01 = din("dmask01", [128, 4 * 512])
    nonce = din("nonce", [1, 1], I32)
    czero = din("czero", [128, 2])
    out_fm = nc.dram_tensor("out_fm", [D, TOK], F32, kind="ExternalOutput").ap()

    def shared(name, shape, dt):
        return nc.dram_tensor(name, shape, dt, kind="Internal",
                              addr_space="Shared")

    sh_carry = [shared(f"sh_carry{d}", [128, 2], F32) for d in range(CG)]
    sh_xn = [shared(f"sh_xn{s}", [D, TOK], BF16) for s in range(2)]
    sh_O = [shared(f"sh_O{s}", [512, TFULL], BF16) for s in range(2)]
    sh_fl = shared("sh_fl", [1, 16], I32)

    s_wld = Sem(nc, "s_wld")
    s_xld = Sem(nc, "s_xld")
    s_tn = Sem(nc, "s_tn")
    s_vc = Sem(nc, "s_vc")
    s_sc = Sem(nc, "s_sc")
    s_gp = Sem(nc, "s_gp")
    s_tr = Sem(nc, "s_tr")
    s_out = Sem(nc, "s_out")
    s_junk = Sem(nc, "s_junk")
    s_exA = Sem(nc, "s_exA")
    s_exB = Sem(nc, "s_exB")
    exA = [0]
    exB = [0]

    vec_a = nc.alloc_sbuf_tensor("vec_a", [128, 40], F32)
    mix1m = nc.alloc_sbuf_tensor("mix1m", [128, 24], F32)
    ffs_s = nc.alloc_sbuf_tensor("ffs_s", [128, 4 * HG + CG], F32)
    vec_b = nc.alloc_sbuf_tensor("vec_b", [128, 28], F32)
    bq_sc = nc.alloc_sbuf_tensor("bq_sc", [128, 4], F32)
    bq_st = [nc.alloc_sbuf_tensor(f"bq_st{p}", [128, 1], F32) for p in range(4)]
    bk_st = [nc.alloc_sbuf_tensor(f"bk_st{p}", [128, 1], F32) for p in range(4)]
    bv_st = [nc.alloc_sbuf_tensor(f"bv_st{p}", [128, 1], F32) for p in range(4)]
    bo_st = [nc.alloc_sbuf_tensor(f"bo_st{c}", [128, 1], F32) for c in range(CG)]
    ew_d = nc.alloc_sbuf_tensor("ew_d", [128, 3 * CG], F32)
    ew_w = [nc.alloc_sbuf_tensor(f"ew_w{d}", [128, 1], F32) for d in range(CG)]
    ones_col = nc.alloc_sbuf_tensor("ones_col", [128, 1], F32)
    ones_row = nc.alloc_sbuf_tensor("ones_row", [1, 128], F32)
    flagv = nc.alloc_sbuf_tensor("flagv", [1, 1], I32)
    flagr = nc.alloc_sbuf_tensor("flagr", [1, 1], I32)
    eps1 = nc.alloc_sbuf_tensor("eps1", [1, 1], F32)
    eps2 = nc.alloc_sbuf_tensor("eps2", [1, 1], F32)
    srow = nc.alloc_sbuf_tensor("srow", [1, 512], F32)
    mrow = nc.alloc_sbuf_tensor("mrow", [1, 512], F32)
    vrow = nc.alloc_sbuf_tensor("vrow", [1, 512], F32)
    brow = nc.alloc_sbuf_tensor("brow", [1, 512], F32)
    sq_t = [nc.alloc_sbuf_tensor(f"sq{i}", [128, 512], F32) for i in range(2)]
    carry_sb = nc.alloc_sbuf_tensor("carry_sb", [128, 2 * CG], F32)
    carry_out = nc.alloc_sbuf_tensor("carry_out", [128, 2 * CG], F32)
    o_acc = nc.alloc_sbuf_tensor("o_acc", [128, 128], F32)
    o_bf = nc.alloc_sbuf_tensor("o_bf", [128, 128], BF16)
    mneg_t = [nc.alloc_sbuf_tensor(f"mneg{h}", [128, 1], F32) for h in range(2)]
    mx8 = [nc.alloc_sbuf_tensor(f"mx8_{h}", [128, 8], F32) for h in range(2)]
    lc8 = [nc.alloc_sbuf_tensor(f"lc8_{h}", [128, 8], F32) for h in range(2)]
    lsum = [nc.alloc_sbuf_tensor(f"lsum{h}", [128, 1], F32) for h in range(2)]
    lsc0 = nc.alloc_sbuf_tensor("lsc0", [128, 8], F32)
    lsc = [lsc0, lsc0]
    rl_t = [nc.alloc_sbuf_tensor(f"rl{h}", [128, 1], F32) for h in range(2)]

    SEGX_SZ = 64 * 1024
    base0 = (nc.sbuf_base + 31) & ~31
    avail = nc.sbuf_top - base0 - 64
    segX = SegAlloc(nc, base0, SEGX_SZ, "sx")
    segY = SegAlloc(nc, base0 + SEGX_SZ, avail - SEGX_SZ, "sy")

    pb = [nc.alloc_psum_tensor(f"pb{i}", [128, 512], F32) for i in range(8)]
    bank_free = {i: None for i in range(8)}

    sch = Sched()

    def mm_group(bank, mms, waits=()):
        w = [x for x in waits if x is not None]
        if bank_free[bank] is not None:
            w.append(bank_free[bank])
        n = len(mms)
        for i, (lhsT, rhs, out) in enumerate(mms):
            def f(t, lhsT=lhsT, rhs=rhs, out=out, st=(i == 0), sp=(i == n - 1)):
                return t.matmul(out, lhsT, rhs, start=st, stop=sp)
            sch.on("tensor", f, waits=w if i == 0 else ())

    def t_fence():
        sch.on("tensor", lambda t: (t.drain(), t.nop())[1], incs=(s_tn.inc(),))
        return s_tn.now()

    sch.on("gpsimd", lambda g: g.dma_start(out=vec_a[:, :], in_=veca),
           incs=(s_xld.inc(16),))
    sch.on("gpsimd", lambda g: g.dma_start(out=ffs_s[:, :], in_=ffs),
           incs=(s_xld.inc(16),))
    sch.on("gpsimd", lambda g: g.dma_start(out=vec_b[:, :], in_=vecb),
           incs=(s_xld.inc(16),))
    sch.on("gpsimd", lambda g: g.dma_start(out=flagv[:, :], in_=nonce),
           incs=(s_xld.inc(16),))
    setup_ld = s_xld.now()
    regs = {}

    def fsetup_regs(g):
        for r in ("nonce", "pid", "fv", "cnt", "t1", "t2"):
            regs[r] = g.alloc_register("R" + r)
        g.reg_load(regs["nonce"], flagv[0:1, 0:1])
        g.reg_mov(regs["pid"], g.partition_id())
        g.reg_mod(regs["pid"], regs["pid"], 2)
        return g.nop()
    sch.on("gpsimd", fsetup_regs, waits=(setup_ld,))

    sch.on("vector", lambda v: v.memset(ones_col[:, :], 1.0),
           incs=(s_vc.inc(),))
    sch.on("vector", lambda v: v.memset(eps1[:, :], LN_EPS),
           incs=(s_vc.inc(),))
    sch.on("vector", lambda v: v.memset(eps2[:, :], RMS_EPS),
           incs=(s_vc.inc(),))
    sch.on("vector", lambda v: v.memset(ones_row[:, :], 1.0),
           incs=(s_vc.inc(),))
    sch.on("vector", lambda v: v.memset(mx8[0][:, :], 0.0),
           incs=(s_vc.inc(),))
    sch.on("vector", lambda v: v.memset(mx8[1][:, :], 0.0),
           incs=(s_vc.inc(),))
    sch.on("vector",
           lambda v: v.tensor_scalar(mix1m[:, 0:24], vec_a[:, 16:40], -1.0, 1.0,
                                     ALU.mult, ALU.add),
           waits=(setup_ld,), incs=(s_vc.inc(),))
    for p in range(4):
        sch.on("vector", lambda v, p=p: v.tensor_scalar_mul(
            bq_st[p][:, :], vec_b[:, p:p + 1], 0.125), incs=(s_vc.inc(),))
        sch.on("vector", lambda v, p=p: v.tensor_copy(bk_st[p][:, :],
                                                      vec_b[:, 4 + p:5 + p]),
               incs=(s_vc.inc(),))
        sch.on("vector", lambda v, p=p: v.tensor_copy(bv_st[p][:, :],
                                                      vec_b[:, 8 + p:9 + p]),
               incs=(s_vc.inc(),))
    for c in range(CG):
        sch.on("vector", lambda v, c=c: v.tensor_copy(bo_st[c][:, :],
                                                      vec_b[:, 20 + c:21 + c]),
               incs=(s_vc.inc(),))
    setup_done = s_vc.now()

    def emit_ln_block(x_tiles, col0, W, out_tiles, ocol0, src_ready=()):
        mm_group(6, [(ones_col[:, 0:1], x_tiles[c][:, col0:col0 + W],
                      pb[6][0:1, 0:W]) for c in range(CG)], waits=src_ready)
        sq_guard = {}
        for c in range(CG):
            w = list(src_ready)
            if c >= 2:
                w.append((s_tn.h, sq_guard[c - 2]))
            def fsq(s, c=c, col0=col0, W=W):
                return s.activation(sq_t[c % 2][:, 0:W],
                                    x_tiles[c][:, col0:col0 + W], AF.Square)
            sch.on("scalar", fsq, waits=w, incs=(s_sc.inc(),))
            sqv = s_sc.now()
            bw = [sqv]
            if c == 0 and bank_free[7] is not None:
                bw.append(bank_free[7])
            def fm(t, c=c, W=W):
                return t.matmul(pb[7][0:1, 0:W], ones_col[:, 0:1],
                                sq_t[c % 2][:, 0:W], start=(c == 0),
                                stop=(c == CG - 1))
            sch.on("tensor", fm, waits=bw)
            sch.on("tensor", lambda t: (t.drain(), t.nop())[1],
                   incs=(s_tn.inc(),))
            sq_guard[c] = s_tn.n
        stats_tok = (s_tn.h, s_tn.n)
        sch.on("vector", lambda v, W=W: v.tensor_scalar_mul(
            mrow[0:1, 0:W], pb[6][0:1, 0:W], 1.0 / D), waits=(stats_tok,),
            incs=(s_vc.inc(),))
        sch.on("vector", lambda v, W=W: v.tensor_mul(
            vrow[0:1, 0:W], mrow[0:1, 0:W], mrow[0:1, 0:W]),
            waits=(s_vc.now(),), incs=(s_vc.inc(),))
        sch.on("vector", lambda v, W=W: v.scalar_tensor_tensor(
            vrow[0:1, 0:W], pb[7][0:1, 0:W], 1.0 / D, vrow[0:1, 0:W],
            ALU.mult, ALU.subtract), waits=(s_vc.now(),),
            incs=(s_vc.inc(),))
        var_tok = s_vc.now()
        sch.on("scalar", lambda s, W=W: s.activation(
            srow[0:1, 0:W], vrow[0:1, 0:W], AF.Sqrt, bias=eps1[0:1, 0:1]),
            waits=(var_tok,), incs=(s_sc.inc(),))
        sch.on("vector", lambda v, W=W: v.reciprocal(vrow[0:1, 0:W],
                                                     srow[0:1, 0:W]),
               waits=(s_sc.now(),), incs=(s_vc.inc(),))
        sch.on("vector", lambda v, W=W: v.scalar_tensor_tensor(
            brow[0:1, 0:W], mrow[0:1, 0:W], -1.0, vrow[0:1, 0:W],
            ALU.mult, ALU.mult), waits=(s_vc.now(),), incs=(s_vc.inc(),))
        rows_tok = s_vc.now()
        bank_free[6] = rows_tok
        bank_free[7] = rows_tok
        mm_group(6, [(ones_row[0:1, :], vrow[0:1, 0:W], pb[6][:, 0:W])],
                 waits=(rows_tok,))
        mm_group(7, [(ones_row[0:1, :], brow[0:1, 0:W], pb[7][:, 0:W])])
        bc_tok = t_fence()
        for c in range(CG):
            def fa(v, c=c, col0=col0, W=W):
                return v.tensor_mul(sq_t[c % 2][:, 0:W],
                                    x_tiles[c][:, col0:col0 + W], pb[6][:, 0:W])
            def fb(v, c=c, ocol0=ocol0, W=W):
                return v.tensor_add(out_tiles[c][:, ocol0:ocol0 + W],
                                    sq_t[c % 2][:, 0:W], pb[7][:, 0:W])
            sch.on("vector", fa,
                   waits=(bc_tok,) if c == 0 else (s_vc.now(),),
                   incs=(s_vc.inc(),))
            sch.on("vector", fb, waits=(s_vc.now(),), incs=(s_vc.inc(),))
        apply_tok = s_vc.now()
        bank_free[6] = apply_tok
        bank_free[7] = apply_tok
        return apply_tok

    xk_b = [segY.alloc([128, TOK], BF16, f"xk{c}") for c in range(CG)]
    xv_b = [segY.alloc([128, TOK], BF16, f"xv{c}") for c in range(CG)]
    xr_b = [segY.alloc([128, TOK], BF16, f"xr{c}") for c in range(CG)]
    rwkv_b = [segY.alloc([128, TOK], BF16, f"rw{c}") for c in range(CG)]
    ln_bf = [segX.alloc([128, TOK + 1], BF16, f"ln{c}") for c in range(CG)]
    x_blk = [segX.alloc([128, 512], F32, f"xblk{c}") for c in range(CG)]
    mixtmp = segX.alloc([128, 512], F32, "mixtmp")

    LN1_BLOCKS = [(0, 512), (512, 512), (1024, 512), (1536, 512), (1537, 512)]
    prev_apply = setup_done
    for (col0, W) in LN1_BLOCKS:
        for c in range(CG):
            def fl(g, c=c, col0=col0, W=W):
                return g.dma_start(out=x_blk[c][:, 0:W],
                                   in_=xin[128 * c:128 * (c + 1),
                                           col0:col0 + W])
            sch.on("gpsimd", fl, waits=(prev_apply,) if c == 0 else (),
                   incs=(s_xld.inc(16),))
        ld_tok = s_xld.now()
        prev_apply = emit_ln_block(x_blk, 0, W, ln_bf, col0,
                                   src_ready=(ld_tok,) if col0 else
                                   (ld_tok, setup_done))
    ln_done = prev_apply

    for (mi, out_t, tmcol, omcol) in ((0, xk_b, 16, 0), (1, xv_b, 24, 8),
                                      (2, xr_b, 32, 16)):
        for c in range(CG):
            for (t0, W) in TB4:
                def f1(v, c=c, t0=t0, W=W, tmcol=tmcol):
                    return v.tensor_scalar_mul(
                        mixtmp[:, 0:W], ln_bf[c][:, 1 + t0:1 + t0 + W],
                        vec_a[:, tmcol + c:tmcol + c + 1])
                def f2(v, c=c, t0=t0, W=W, omcol=omcol, out_t=out_t):
                    return v.scalar_tensor_tensor(
                        out_t[c][:, t0:t0 + W], ln_bf[c][:, t0:t0 + W],
                        mix1m[:, omcol + c:omcol + c + 1], mixtmp[:, 0:W],
                        ALU.mult, ALU.add)
                first = (mi == 0 and c == 0 and t0 == 0)
                sch.on("vector", f1,
                       waits=(ln_done,) if first else (s_vc.now(),),
                       incs=(s_vc.inc(),))
                sch.on("vector", f2, waits=(s_vc.now(),),
                       incs=(s_vc.inc(),))
    mix_done = s_vc.now()
    if DBG:
        d_ln0 = dbg("d_ln0", [128, TOK + 1], BF16)
        d_xk0 = dbg("d_xk0", [128, TOK], BF16)
        d_xr0 = dbg("d_xr0", [128, TOK], BF16)
        sch.on("sync", lambda sy: sy.dma_start(out=d_ln0, in_=ln_bf[0][:, :]),
               waits=(mix_done,), incs=(s_out.inc(16),))
        sch.on("sync", lambda sy: sy.dma_start(out=d_xk0, in_=xk_b[0][:, :]),
               incs=(s_out.inc(16),))
        sch.on("sync", lambda sy: sy.dma_start(out=d_xr0, in_=xr_b[0][:, :]),
               incs=(s_out.inc(16),))

    last_done = mix_done
    segX.reset()
    sr_d = segX.alloc([128, TOK], F32, "sr_d")
    ek_d = segX.alloc([128, TOK + 1], F32, "ek_d")
    ekv_d = segX.alloc([128, TOK + 1], F32, "ekv_d")
    aa_d = segX.alloc([128, TOK + 1], F32, "aa_d")
    bb_d = segX.alloc([128, TOK + 1], F32, "bb_d")
    ew_big = segX.alloc([128, TOK + 1], F32, "ew_big")
    wk_blk = [segX.alloc([128, D], BF16, f"wkb{i}") for i in range(2)]
    wv_blk = [segX.alloc([128, D], BF16, f"wvb{i}") for i in range(2)]
    wr_blk = [segX.alloc([128, D], BF16, f"wrb{i}") for i in range(2)]

    wblk_free = {}

    wsems = {}

    def wsem(key):
        if key not in wsems:
            wsems[key] = Sem(nc, "sw_" + "_".join(str(k) for k in key))
        return wsems[key]

    def load_wblk(dst, src_ap, idx, free_key, groups=CG):
        sem = wsem(free_key)
        def fl(g, dst=dst, src_ap=src_ap, idx=idx, groups=groups):
            return g.dma_start(
                out=dst[:, 0:groups * 128].rearrange("p (c j) -> p c j",
                                                     c=groups),
                in_=src_ap[idx].rearrange("(c p) j -> p c j", p=128))
        sch.on("gpsimd", fl, waits=(wblk_free.get(free_key),),
               incs=(sem.inc(16),))
        return sem.now()

    def parity_open(g):
        return regs["pid"]

    def emit_poll(g, col, arm):
        sem, cntr = (s_exA, exA) if arm == 0 else (s_exB, exB)
        cntr[0] += 16
        g.dma_start(out=flagr[:, :],
                    in_=sh_fl.ap()[0:1, col:col + 1]).then_inc(sem.h, 16)
        g.wait_ge(sem.h, cntr[0])
        g.reg_mov(regs["cnt"], 0)

        def poll():
            g.dma_start(out=flagr[:, :],
                        in_=sh_fl.ap()[0:1, col:col + 1]).then_inc(s_junk.h, 16)
            g.drain()
            g.reg_add(regs["cnt"], regs["cnt"], 1)
            g.reg_load(regs["fv"], flagr[0:1, 0:1])
            g.reg_alu(regs["t1"], regs["fv"], regs["nonce"], ALU.not_equal)
            g.reg_alu(regs["t2"], regs["cnt"], POLL_CAP, ALU.is_lt)
            g.reg_alu(regs["t1"], regs["t1"], regs["t2"], ALU.logical_and)
            return regs["t1"]

        g.reg_load(regs["fv"], flagr[0:1, 0:1])
        g.reg_alu(regs["t1"], regs["fv"], regs["nonce"], ALU.not_equal)
        with g.If_ne(regs["t1"], 0):
            with g.While(poll):
                g.nop()

    for k in (("r", 0), ("r", 1), ("k", 0), ("k", 1), ("v", 0), ("v", 1)):
        wblk_free[k] = mix_done

    prev_y_tok = None
    for d in range(CG if PH >= 2 else 0):
        buf = d % 2
        wr_tok = load_wblk(wr_blk[buf], wrT, d, ("r", buf))
        wk_tok = load_wblk(wk_blk[buf], wkT, d, ("k", buf))
        wv_tok = load_wblk(wv_blk[buf], wvT, d, ("v", buf))
        for bi, (t0, W) in enumerate(TB4):
            bank = bi % 2
            mm_group(bank, [(wr_blk[buf][:, 128 * c:128 * (c + 1)],
                             xr_b[c][:, t0:t0 + W], pb[bank][:, 0:W])
                            for c in range(CG)], waits=(wr_tok, mix_done))
            ptok = t_fence()
            def fsg(s, bank=bank, t0=t0, W=W):
                return s.activation(sr_d[:, t0:t0 + W], pb[bank][:, 0:W],
                                    AF.Sigmoid)
            sch.on("scalar", fsg, waits=(ptok, prev_y_tok),
                   incs=(s_sc.inc(),))
            bank_free[bank] = s_sc.now()
        sr_tok = s_sc.now()
        for bi, (t0, W) in enumerate(TB4):
            bank = 2 + bi % 2
            mm_group(bank, [(wk_blk[buf][:, 128 * c:128 * (c + 1)],
                             xk_b[c][:, t0:t0 + W], pb[bank][:, 0:W])
                            for c in range(CG)], waits=(wk_tok,))
            ptok = t_fence()
            def fek(s, bank=bank, t0=t0, W=W):
                return s.activation(ek_d[:, 1 + t0:1 + t0 + W],
                                    pb[bank][:, 0:W], AF.Exp)
            sch.on("scalar", fek, waits=(ptok, prev_y_tok),
                   incs=(s_sc.inc(),))
            bank_free[bank] = s_sc.now()
        ek_tok = s_sc.now()
        def fcarry_recv(g, d=d):
            pid = parity_open(g)
            with g.If_eq(pid, 0):
                exA[0] += 16
                g.dma_start(out=carry_sb[:, 2 * d:2 * d + 2],
                            in_=czero).then_inc(s_exA.h, 16)
                g.wait_ge(s_exA.h, exA[0])
            with g.Else():
                emit_poll(g, d, 1)
                exB[0] += 16
                g.dma_start(out=carry_sb[:, 2 * d:2 * d + 2],
                            in_=sh_carry[d].ap()).then_inc(s_exB.h, 16)
                g.wait_ge(s_exB.h, exB[0])
            return g.nop()
        sch.on("gpsimd", fcarry_recv, incs=(s_gp.inc(),))
        carry_tok = s_gp.now()
        for bi, (t0, W) in enumerate(TB4):
            bank = 4 + bi % 2
            mm_group(bank, [(wv_blk[buf][:, 128 * c:128 * (c + 1)],
                             xv_b[c][:, t0:t0 + W], pb[bank][:, 0:W])
                            for c in range(CG)], waits=(wv_tok,))
            ptok = t_fence()
            def fekv(v, bank=bank, t0=t0, W=W):
                return v.tensor_mul(ekv_d[:, 1 + t0:1 + t0 + W],
                                    pb[bank][:, 0:W],
                                    ek_d[:, 1 + t0:1 + t0 + W])
            sch.on("vector", fekv, waits=(ptok, ek_tok), incs=(s_vc.inc(),))
            bank_free[bank] = s_vc.now()
        sch.on("vector", lambda v, d=d: v.tensor_copy(
            ekv_d[:, 0:1], carry_sb[:, 2 * d:2 * d + 1]), waits=(carry_tok,),
            incs=(s_vc.inc(),))
        sch.on("vector", lambda v, d=d: v.tensor_copy(
            ek_d[:, 0:1], carry_sb[:, 2 * d + 1:2 * d + 2]),
            incs=(s_vc.inc(),))
        cpy_tok = s_vc.now()
        sch.on("scalar", lambda s, d=d: s.activation(
            ew_d[:, 3 * d:3 * d + 1], vec_a[:, d:d + 1], AF.Exp),
            waits=(setup_done,), incs=(s_sc.inc(),))
        sch.on("scalar", lambda s, d=d: s.activation(
            ew_d[:, 3 * d + 1:3 * d + 2], ew_d[:, 3 * d:3 * d + 1],
            AF.Exp, scale=-1.0), waits=(s_sc.now(),), incs=(s_sc.inc(),))
        sch.on("scalar", lambda s, d=d: s.activation(
            ew_d[:, 3 * d + 2:3 * d + 3], vec_a[:, 8 + d:9 + d], AF.Exp),
            incs=(s_sc.inc(),))
        sch.on("scalar", lambda s, d=d: s.activation(
            ew_big[:, 0:TOK], sr_d[:, 0:TOK], AF.Identity, scale=0.0,
            bias=ew_d[:, 3 * d + 1:3 * d + 2]), waits=(s_sc.now(), sr_tok),
            incs=(s_sc.inc(),))
        sch.on("scalar", lambda s, d=d: s.activation(
            ew_big[:, TOK:TOK + 1], sr_d[:, 0:1], AF.Identity, scale=0.0,
            bias=ew_d[:, 3 * d + 1:3 * d + 2]), incs=(s_sc.inc(),))
        ewb_tok = s_sc.now()
        sch.on("vector", lambda v: v.tensor_tensor_scan(
            aa_d[:, :], ew_big[:, :], ekv_d[:, :], 0.0, ALU.mult, ALU.add),
            waits=(ewb_tok, cpy_tok), incs=(s_vc.inc(),))
        sch.on("vector", lambda v: v.tensor_tensor_scan(
            bb_d[:, :], ew_big[:, :], ek_d[:, :], 0.0, ALU.mult, ALU.add),
            incs=(s_vc.inc(),))
        scans_tok = s_vc.now()
        sch.on("vector", lambda v, d=d: v.tensor_copy(
            carry_out[:, 2 * d:2 * d + 1], aa_d[:, TOK:TOK + 1]),
            waits=(scans_tok,), incs=(s_vc.inc(),))
        sch.on("vector", lambda v, d=d: v.tensor_copy(
            carry_out[:, 2 * d + 1:2 * d + 2], bb_d[:, TOK:TOK + 1]),
            incs=(s_vc.inc(),))
        scan_tok = s_vc.now()
        def fcarry_send(g, d=d):
            pid = parity_open(g)
            with g.If_eq(pid, 0):
                exA[0] += 16
                g.dma_start(out=sh_carry[d].ap(),
                            in_=carry_out[:, 2 * d:2 * d + 2]
                            ).then_inc(s_exA.h, 16)
                g.wait_ge(s_exA.h, exA[0])
                exA[0] += 16
                g.dma_start(out=sh_fl.ap()[0:1, d:d + 1],
                            in_=flagv[:, :]).then_inc(s_exA.h, 16)
            with g.Else():
                g.nop()
            return g.nop()
        sch.on("gpsimd", fcarry_send, waits=(scan_tok,))
        if DBG and d in (0, 1, CG - 1):
            sfx = str(d) if d != CG - 1 else "7"
            for nm, tt in ((f"d_aa{sfx}", aa_d), (f"d_bb{sfx}", bb_d),
                           (f"d_ewb{sfx}", ew_big), (f"d_srg{sfx}", sr_d),
                           (f"d_ekg{sfx}", ek_d), (f"d_ekvg{sfx}", ekv_d)):
                a = dbg(nm, [128, tt.shape[1]])
                def fdmp(sy, a=a, tt=tt):
                    return sy.dma_start(out=a, in_=tt[:, :])
                sch.on("sync", fdmp, waits=(scan_tok,), incs=(s_out.inc(16),))
            sch.on("vector", lambda v: v.nop(), waits=(s_out.now(),))
        sch.on("vector", lambda v, d=d: v.scalar_tensor_tensor(
            aa_d[:, 0:TOK], ekv_d[:, 1:TOK + 1], ew_d[:, 3 * d + 2:3 * d + 3],
            aa_d[:, 0:TOK], ALU.mult, ALU.add), waits=(scan_tok,),
            incs=(s_vc.inc(),))
        sch.on("vector", lambda v, d=d: v.scalar_tensor_tensor(
            bb_d[:, 0:TOK], ek_d[:, 1:TOK + 1], ew_d[:, 3 * d + 2:3 * d + 3],
            bb_d[:, 0:TOK], ALU.mult, ALU.add), incs=(s_vc.inc(),))
        sch.on("vector", lambda v: v.reciprocal(ew_big[:, 0:TOK],
                                                bb_d[:, 0:TOK]),
               waits=(s_vc.now(),), incs=(s_vc.inc(),))
        sch.on("vector", lambda v: v.tensor_mul(aa_d[:, 0:TOK],
                                                aa_d[:, 0:TOK],
                                                ew_big[:, 0:TOK]),
               waits=(s_vc.now(),), incs=(s_vc.inc(),))
        sch.on("vector", lambda v, d=d: v.tensor_mul(
            rwkv_b[d][:, :], aa_d[:, 0:TOK], sr_d[:, :]),
            waits=(s_vc.now(),), incs=(s_vc.inc(),))
        prev_y_tok = s_vc.now()
    rwkv_done = s_vc.now()
    if DBG and PH >= 2:
        for dd in range(CG):
            ap = dbg(f"d_rw{dd}", [128, TOK], BF16)
            def fdr(sy, ap=ap, dd=dd):
                return sy.dma_start(out=ap, in_=rwkv_b[dd][:, :])
            sch.on("sync", fdr, incs=(s_out.inc(16),))
    if PH >= 2:
        last_done = rwkv_done

    segX.reset()
    x1 = [segX.alloc([128, TOK], F32, f"x1_{c}") for c in range(CG)]
    segY.reset()
    woa_blk = [segY.alloc([128, D], BF16, f"woab{i}") for i in range(2)]
    xi_blk = segY.alloc([128, 512], F32, "xi_blk")
    wblk_free[("woa", 0)] = rwkv_done
    wblk_free[("woa", 1)] = rwkv_done
    xi_free = rwkv_done
    for cg in range(CG if PH >= 3 else 0):
        buf = cg % 2
        wtok = load_wblk(woa_blk[buf], woTa, cg, ("woa", buf))
        for bi, (t0, W) in enumerate(TB4):
            bank = bi % 2
            mm_group(bank, [(woa_blk[buf][:, 128 * dd:128 * (dd + 1)],
                             rwkv_b[dd][:, t0:t0 + W], pb[bank][:, 0:W])
                            for dd in range(CG)], waits=(wtok, rwkv_done))
            ptok = t_fence()
            def fxl(g, cg=cg, t0=t0, W=W):
                return g.dma_start(out=xi_blk[:, 0:W],
                                   in_=xin[128 * cg:128 * (cg + 1),
                                           1 + t0:1 + t0 + W])
            sch.on("gpsimd", fxl, waits=(xi_free,), incs=(s_xld.inc(16),))
            xl_tok = s_xld.now()
            def fadd(v, cg=cg, t0=t0, W=W, bank=bank):
                return v.tensor_add(x1[cg][:, t0:t0 + W], pb[bank][:, 0:W],
                                    xi_blk[:, 0:W])
            sch.on("vector", fadd, waits=(ptok, xl_tok), incs=(s_vc.inc(),))
            bank_free[bank] = s_vc.now()
            xi_free = s_vc.now()
        wblk_free[("woa", buf)] = s_vc.now()
    x1_tok = s_vc.now()
    if PH >= 3:
        last_done = x1_tok

    def emit_ffn(ready_tok, tag):
        segY.reset()
        mark = segY.mark()
        lnb = [segY.alloc([128, 512], BF16, f"lnb{c}{tag}") for c in range(CG)]
        hbuf = [segY.alloc([128, 512], BF16, f"hb{h}{tag}") for h in range(HG)]
        gbuf = [segY.alloc([128, 512], BF16, f"gb{h}{tag}") for h in range(HG)]
        sA = segY.alloc([128, 512], F32, f"sA{tag}")
        w12 = [segY.alloc([128, D], BF16, f"w12_{i}{tag}") for i in range(4)]
        wbig = [segY.alloc([128, HID], BF16, f"wbig{i}{tag}") for i in range(2)]
        wbig_free = {0: ready_tok, 1: ready_tok}
        w12_free = {0: ready_tok, 1: ready_tok}
        last_tok = ready_tok
        for (t0, W) in TB4:
            ln_tok = emit_ln_block(x1, t0, W, lnb, 0, src_ready=(last_tok,))
            for h in range(HG):
                b0, b1 = (0, 1) if h % 2 == 0 else (2, 3)
                wb1, wb2 = w12[2 * (h % 2)], w12[2 * (h % 2) + 1]
                def fl1(g, wb1=wb1, h=h):
                    return g.dma_start(
                        out=wb1[:, :].rearrange("p (c j) -> p c j", c=CG),
                        in_=w1T[h].rearrange("(c p) j -> p c j", p=128))
                def fl2(g, wb2=wb2, h=h):
                    return g.dma_start(
                        out=wb2[:, :].rearrange("p (c j) -> p c j", c=CG),
                        in_=w2T[h].rearrange("(c p) j -> p c j", p=128))
                sem12 = wsem(("w12", h % 2))
                sch.on("gpsimd", fl1, waits=(w12_free.get(h % 2),),
                       incs=(sem12.inc(16),))
                sch.on("gpsimd", fl2, incs=(sem12.inc(16),))
                wl_tok = sem12.now()
                mm_group(b0, [(wb1[:, 128 * c:128 * (c + 1)], lnb[c][:, 0:W],
                               pb[b0][:, 0:W]) for c in range(CG)],
                         waits=(wl_tok, ln_tok))
                mm_group(b1, [(wb2[:, 128 * c:128 * (c + 1)], lnb[c][:, 0:W],
                               pb[b1][:, 0:W]) for c in range(CG)])
                ptok = t_fence()
                w12_free[h % 2] = ptok
                def fsilu(s, b0=b0, h=h, W=W):
                    return s.activation(sA[:, 0:W], pb[b0][:, 0:W], AF.Silu,
                                        scale=ffs_s[:, h:h + 1])
                sch.on("scalar", fsilu, waits=(ptok, s_vc.now()),
                       incs=(s_sc.inc(),))
                si_tok = s_sc.now()
                bank_free[b0] = si_tok
                def fh(v, b1=b1, h=h, W=W):
                    return v.scalar_tensor_tensor(
                        hbuf[h][:, 0:W], pb[b1][:, 0:W],
                        ffs_s[:, HG + h:HG + h + 1], sA[:, 0:W],
                        ALU.mult, ALU.mult)
                sch.on("vector", fh, waits=(si_tok,), incs=(s_vc.inc(),))
                bank_free[b1] = s_vc.now()
            stage_in_tok = s_vc.now()
            src, dst = hbuf, gbuf
            for (wT, scol) in ((h1T, 2 * HG), (h2T, 3 * HG)):
                for og in range(HG):
                    bank = og % 4
                    wb = wbig[og % 2]
                    def flw(g, wb=wb, wT=wT, og=og):
                        return g.dma_start(
                            out=wb[:, :].rearrange("p (c j) -> p c j", c=HG),
                            in_=wT[og].rearrange("(c p) j -> p c j", p=128))
                    semb = wsem(("wbig", og % 2))
                    sch.on("gpsimd", flw, waits=(wbig_free.get(og % 2),),
                           incs=(semb.inc(16),))
                    wl_tok = semb.now()
                    mm_group(bank, [(wb[:, 128 * k:128 * (k + 1)],
                                     src[k][:, 0:W], pb[bank][:, 0:W])
                                    for k in range(HG)],
                             waits=(wl_tok, stage_in_tok))
                    ptok = t_fence()
                    wbig_free[og % 2] = ptok
                    def fg(s, bank=bank, og=og, W=W, scol=scol, dst=dst):
                        return s.activation(
                            dst[og][:, 0:W], pb[bank][:, 0:W], AF.Gelu,
                            scale=ffs_s[:, scol + og:scol + og + 1])
                    sch.on("scalar", fg, waits=(ptok,), incs=(s_sc.inc(),))
                    bank_free[bank] = s_sc.now()
                src, dst = dst, src
                stage_in_tok = s_sc.now()
            for cg in range(CG):
                bank = cg % 4
                wb = wbig[cg % 2]
                def flw3(g, wb=wb, cg=cg):
                    return g.dma_start(
                        out=wb[:, :].rearrange("p (c j) -> p c j", c=HG),
                        in_=w3T[cg].rearrange("(c p) j -> p c j", p=128))
                semb = wsem(("wbig", cg % 2))
                sch.on("gpsimd", flw3, waits=(wbig_free.get(cg % 2),),
                       incs=(semb.inc(16),))
                wl_tok = semb.now()
                mm_group(bank, [(wb[:, 128 * k:128 * (k + 1)],
                                 src[k][:, 0:W], pb[bank][:, 0:W])
                                for k in range(HG)],
                         waits=(wl_tok, stage_in_tok))
                ptok = t_fence()
                wbig_free[cg % 2] = ptok
                def fx(v, bank=bank, cg=cg, t0=t0, W=W):
                    return v.scalar_tensor_tensor(
                        x1[cg][:, t0:t0 + W], pb[bank][:, 0:W],
                        ffs_s[:, 4 * HG + cg:4 * HG + cg + 1],
                        x1[cg][:, t0:t0 + W], ALU.mult, ALU.add)
                sch.on("vector", fx, waits=(ptok,), incs=(s_vc.inc(),))
                bank_free[bank] = s_vc.now()
            last_tok = s_vc.now()
        segY.reset(mark)
        return last_tok

    if PH >= 4:
        x2_tok = emit_ffn(x1_tok, "f1")
        last_done = x2_tok
    else:
        x2_tok = x1_tok

    segY.reset()
    markY0 = segY.mark()
    xn_loc = [segY.alloc([128, 512], BF16, f"xnl{c}") for c in range(CG)]
    t_f32 = [segY.alloc([128, 512], F32, f"t3_{c}") for c in range(CG)]
    last = x2_tok
    for bi, (t0, W) in enumerate(TB4 if PH >= 5 else []):
        ln_tok = emit_ln_block(x1, t0, W, t_f32, 0, src_ready=(last,))
        sq_guard = {}
        for c in range(CG):
            w = [ln_tok] if c == 0 else []
            if c >= 2:
                w.append((s_tn.h, sq_guard[c - 2]))
            def fsq(s, c=c, W=W):
                return s.activation(sq_t[c % 2][:, 0:W], t_f32[c][:, 0:W],
                                    AF.Square)
            sch.on("scalar", fsq, waits=w, incs=(s_sc.inc(),))
            sqv = s_sc.now()
            bw = [sqv]
            if c == 0 and bank_free[7] is not None:
                bw.append(bank_free[7])
            def fm(t, c=c, W=W):
                return t.matmul(pb[7][0:1, 0:W], ones_col[:, 0:1],
                                sq_t[c % 2][:, 0:W], start=(c == 0),
                                stop=(c == CG - 1))
            sch.on("tensor", fm, waits=bw)
            sch.on("tensor", lambda t: (t.drain(), t.nop())[1],
                   incs=(s_tn.inc(),))
            sq_guard[c] = s_tn.n
        rms_tok = (s_tn.h, s_tn.n)
        sch.on("scalar", lambda s, W=W: s.activation(
            srow[0:1, 0:W], pb[7][0:1, 0:W], AF.Sqrt, bias=eps2[0:1, 0:1],
            scale=1.0 / D), waits=(rms_tok,), incs=(s_sc.inc(),))
        sch.on("vector", lambda v, W=W: v.reciprocal(vrow[0:1, 0:W],
                                                     srow[0:1, 0:W]),
               waits=(s_sc.now(),), incs=(s_vc.inc(),))
        rr_tok = s_vc.now()
        bank_free[7] = rr_tok
        mm_group(7, [(ones_row[0:1, :], vrow[0:1, 0:W], pb[7][:, 0:W])],
                 waits=(rr_tok,))
        bc_tok = t_fence()
        for c in range(CG):
            def fa(v, c=c, W=W):
                return v.tensor_mul(sq_t[c % 2][:, 0:W], t_f32[c][:, 0:W],
                                    pb[7][:, 0:W])
            def fb(v, c=c, W=W):
                return v.tensor_scalar_mul(xn_loc[c][:, 0:W],
                                           sq_t[c % 2][:, 0:W],
                                           vec_b[:, 12 + c:13 + c])
            sch.on("vector", fa,
                   waits=(bc_tok,) if c == 0 else (s_vc.now(),),
                   incs=(s_vc.inc(),))
            sch.on("vector", fb, waits=(s_vc.now(),), incs=(s_vc.inc(),))
        xn_tok = s_vc.now()
        bank_free[7] = xn_tok
        def fship(g, t0=t0, W=W):
            pid = parity_open(g)
            with g.If_eq(pid, 0):
                exA[0] += CG * 16
                for c in range(CG):
                    g.dma_start(
                        out=sh_xn[0].ap()[128 * c:128 * (c + 1), t0:t0 + W],
                        in_=xn_loc[c][:, 0:W]).then_inc(s_exA.h, 16)
                g.wait_ge(s_exA.h, exA[0])
            with g.Else():
                exB[0] += CG * 16
                for c in range(CG):
                    g.dma_start(
                        out=sh_xn[1].ap()[128 * c:128 * (c + 1), t0:t0 + W],
                        in_=xn_loc[c][:, 0:W]).then_inc(s_exB.h, 16)
                g.wait_ge(s_exB.h, exB[0])
            return g.nop()
        sch.on("gpsimd", fship, waits=(xn_tok,), incs=(s_gp.inc(),))
        last = s_gp.now()

    def fxflag(g):
        pid = parity_open(g)
        with g.If_eq(pid, 0):
            exA[0] += 16
            g.dma_start(out=sh_fl.ap()[0:1, 8:9],
                        in_=flagv[:, :]).then_inc(s_exA.h, 16)
            emit_poll(g, 9, 0)
        with g.Else():
            exB[0] += 16
            g.dma_start(out=sh_fl.ap()[0:1, 9:10],
                        in_=flagv[:, :]).then_inc(s_exB.h, 16)
            emit_poll(g, 8, 1)
        return g.nop()
    if PH >= 5:
        sch.on("gpsimd", fxflag, incs=(s_gp.inc(),))
        last_done = s_gp.now()
    xn_ready = s_gp.now()

    segY.reset(markY0)
    o_fm = [segY.alloc([128, TFULL], BF16, f"ofm{p}") for p in range(4)]
    markY1 = segY.mark()
    q_fm = segY.alloc([128, TFULL], BF16, "q_fm")
    k_fm = segY.alloc([128, TFULL], BF16, "k_fm")
    v_fm = segY.alloc([128, TFULL], BF16, "v_fm")
    v_tok = segY.alloc([128, TFULL], BF16, "v_tok")
    xn_st = [segY.alloc([128, 512], BF16, f"xst{c}") for c in range(CG)]
    wq_blk = segY.alloc([128, D], BF16, "wq_blk")
    wkb_blk = segY.alloc([128, D], BF16, "wkb_blk")
    wvb_blk = segY.alloc([128, D], BF16, "wvb_blk")
    p_bf = [segY.alloc([128, 512], BF16, f"pbf{h}") for h in range(2)]
    pt_bf = [segY.alloc([128, 512], BF16, f"ptb{h}") for h in range(2)]
    mask01_s = segY.alloc([128, 4 * 512], BF16, "mask01_s")
    s_msk = wsem(("mask",))
    sch.on("gpsimd", lambda g: g.dma_start(out=mask01_s[:, :], in_=dmask01),
           waits=(x2_tok,), incs=(s_msk.inc(16),))
    mask_ld = s_msk.now()

    s_vtr = wsem(("vtr",))
    s_otr = wsem(("otr",))
    s_ptr = [wsem(("ptr", i)) for i in range(2)]
    xnst_free = None
    vt_free = None
    p_free = [None, None]
    pt_free = [None, None]
    obf_free = None
    ofm_tok = []
    wblk_free[("wq", 0)] = x2_tok
    wblk_free[("wkb", 0)] = x2_tok
    wblk_free[("wvb", 0)] = x2_tok
    for p in range(4 if PH >= 6 else 0):
        tq_tok = load_wblk(wq_blk, wqT, p, ("wq", 0))
        tk_tok = load_wblk(wkb_blk, wkTb, p, ("wkb", 0))
        tv_tok = load_wblk(wvb_blk, wvTb, p, ("wvb", 0))
        pw_toks = {"q": tq_tok, "k": tk_tok, "v": tv_tok}
        for bi, (t0, W) in enumerate(TB8):
            slot = 0 if t0 < TOK else 1
            c0 = t0 % TOK
            for c in range(CG):
                def fxn(g, c=c, slot=slot, c0=c0, W=W):
                    return g.dma_start(
                        out=xn_st[c][:, 0:W],
                        in_=sh_xn[slot].ap()[128 * c:128 * (c + 1),
                                             c0:c0 + W])
                w = (xn_ready, xnst_free) if c == 0 else ()
                sch.on("gpsimd", fxn, waits=w, incs=(s_xld.inc(16),))
            xl_tok = s_xld.now()
            for (bank, wb, dstt, kind) in ((0, wq_blk, q_fm, "q"),
                                           (1, wkb_blk, k_fm, "k"),
                                           (2, wvb_blk, v_fm, "v")):
                mm_group(bank, [(wb[:, 128 * c:128 * (c + 1)],
                                 xn_st[c][:, 0:W], pb[bank][:, 0:W])
                                for c in range(CG)],
                         waits=(pw_toks[kind], xl_tok))
                ptok = t_fence()
                if kind == "q":
                    def fe(s, bank=bank, dstt=dstt, t0=t0, W=W, p=p):
                        return s.activation(dstt[:, t0:t0 + W],
                                            pb[bank][:, 0:W], AF.Identity,
                                            bias=bq_st[p][:, 0:1],
                                            scale=0.125)
                elif kind == "k":
                    def fe(s, bank=bank, dstt=dstt, t0=t0, W=W, p=p):
                        return s.activation(dstt[:, t0:t0 + W],
                                            pb[bank][:, 0:W], AF.Identity,
                                            bias=bk_st[p][:, 0:1])
                else:
                    def fe(s, bank=bank, dstt=dstt, t0=t0, W=W, p=p):
                        return s.activation(dstt[:, t0:t0 + W],
                                            pb[bank][:, 0:W], AF.Identity,
                                            bias=bv_st[p][:, 0:1])
                sch.on("scalar", fe, waits=(ptok,), incs=(s_sc.inc(),))
                bank_free[bank] = s_sc.now()
            xnst_free = s_sc.now()
        proj_tok = s_sc.now()
        wblk_free[("wq", 0)] = proj_tok
        wblk_free[("wkb", 0)] = proj_tok
        wblk_free[("wvb", 0)] = proj_tok
        if DBG and p == 0:
            for nm, tt in (("d_qfm", q_fm), ("d_kfm", k_fm), ("d_vfm", v_fm),
                           ("d_bqst", bq_st[0]), ("d_bqsc", bq_sc)):
                a = dbg(nm, [128, tt.shape[1]],
                        BF16 if tt.dtype == BF16 else F32)
                def fdq(sy, a=a, tt=tt):
                    return sy.dma_start(out=a, in_=tt[:, :])
                sch.on("sync", fdq, waits=(proj_tok,), incs=(s_out.inc(16),))
        for tj in range(TFULL // 128):
            def ftr(s, tj=tj):
                return s.dma_start(out=v_tok[:, 128 * tj:128 * (tj + 1)],
                                   in_=v_fm[:, 128 * tj:128 * (tj + 1)],
                                   transpose=True)
            w = (proj_tok, vt_free) if tj == 0 else ()
            sch.on("sync", ftr, waits=w, incs=(s_vtr.inc(16),))
        vtr_tok = s_vtr.now()

        if PH >= 6 and PH < 7:
            last_done = vtr_tok
        for qi in range(NQT if PH >= 7 else 0):
            nblk = qi // 4 + 1
            for h2 in range(2):
                r0 = 64 * h2
                for j in range(nblk):
                    mm_group(j, [(q_fm[r0:r0 + 64, 128 * qi:128 * (qi + 1)],
                                  k_fm[r0:r0 + 64, 512 * j:512 * (j + 1)],
                                  pb[j][:, 0:512])],
                             waits=(proj_tok,) if j == 0 else ())
                sc_tok = t_fence()
                for j in range(nblk):
                    def fmx(v, h2=h2, j=j):
                        return v.reduce_max(mx8[h2][:, j:j + 1],
                                            pb[j][:, 0:512], axis=AX)
                    sch.on("vector", fmx, waits=(sc_tok,) if j == 0 else (),
                           incs=(s_vc.inc(),))
                def fcmb(v, h2=h2, nblk=nblk):
                    return v.reduce_max(mneg_t[h2][:, :],
                                        mx8[h2][:, 0:nblk], axis=AX,
                                        negate=True)
                sch.on("vector", fcmb, waits=(s_vc.now(),),
                       incs=(s_vc.inc(),))
                mstat_tok = s_vc.now()
                pv_started = False
                for j in range(nblk):
                    diag = (j == nblk - 1)
                    slot = j % 2
                    def fpexp(s, h2=h2, j=j, slot=slot):
                        return s.activation(p_bf[slot][:, 0:512],
                                            pb[j][:, 0:512], AF.Exp,
                                            bias=mneg_t[h2][:, 0:1],
                                            accum_out=lc8[h2][:, j:j + 1])
                    sch.on("scalar", fpexp,
                           waits=(mstat_tok, p_free[slot]),
                           incs=(s_sc.inc(),))
                    pexp_tok = s_sc.now()
                    bank_free[j] = pexp_tok
                    if diag:
                        mv = qi % 4
                        def fpm(v, slot=slot, mv=mv):
                            return v.tensor_mul(
                                p_bf[slot][:, 0:512], p_bf[slot][:, 0:512],
                                mask01_s[:, 512 * mv:512 * (mv + 1)])
                        sch.on("vector", fpm, waits=(pexp_tok, mask_ld),
                               incs=(s_vc.inc(),))
                        def fls(v, h2=h2, j=j, slot=slot):
                            return v.reduce_sum(lc8[h2][:, j:j + 1],
                                                p_bf[slot][:, 0:512], axis=AX)
                        sch.on("vector", fls, waits=(s_vc.now(),),
                               incs=(s_vc.inc(),))
                        pexp_tok = s_vc.now()
                    for u in range(4):
                        def ftp(sy, slot=slot, u=u):
                            return sy.dma_start(
                                out=pt_bf[slot][:, 128 * u:128 * (u + 1)],
                                in_=p_bf[slot][:, 128 * u:128 * (u + 1)],
                                transpose=True)
                        w = (pexp_tok, pt_free[slot]) if u == 0 else ()
                        sch.on("sync", ftp, waits=w,
                               incs=(s_ptr[slot].inc(16),))
                    p_free[slot] = s_ptr[slot].now()
                    ptr_tok = s_ptr[slot].now()
                    mms = [(pt_bf[slot][:, 128 * u:128 * (u + 1)],
                            v_tok[:, 128 * (4 * j + u) + r0:
                                  128 * (4 * j + u) + r0 + 64],
                            pb[0][:, 64 * h2:64 * h2 + 64]) for u in range(4)]
                    w = [ptr_tok, vtr_tok]
                    if j == 0:
                        w.append(pexp_tok if not diag else pexp_tok)
                    n = len(mms)
                    for i, (lhsT, rhs, outp) in enumerate(mms):
                        def fpv(t, lhsT=lhsT, rhs=rhs, outp=outp,
                                st=(j == 0 and i == 0),
                                sp=(j == nblk - 1 and i == n - 1)):
                            return t.matmul(outp, lhsT, rhs, start=st, stop=sp)
                        sch.on("tensor", fpv, waits=w if i == 0 else ())
                pv_tok = t_fence()
                pt_free[0] = pv_tok
                pt_free[1] = pv_tok
                def flsum(s, h2=h2, nblk=nblk):
                    return s.activation(lsc[h2][:, 0:nblk],
                                        lc8[h2][:, 0:nblk], AF.Identity,
                                        accum_out=lsum[h2][:, :])
                sch.on("scalar", flsum, waits=(pv_tok,), incs=(s_sc.inc(),))
                lsum_tok = s_sc.now()
                def frl(v, h2=h2):
                    return v.reciprocal(rl_t[h2][:, :], lsum[h2][:, :])
                sch.on("vector", frl, waits=(lsum_tok,), incs=(s_vc.inc(),))
                rl_tok = s_vc.now()
                def fOev(s, h2=h2):
                    return s.activation(o_bf[:, 64 * h2:64 * h2 + 64],
                                        pb[0][:, 64 * h2:64 * h2 + 64],
                                        AF.Identity, scale=rl_t[h2][:, 0:1])
                sch.on("scalar", fOev, waits=(rl_tok, obf_free),
                       incs=(s_sc.inc(),))
                bank_free[0] = s_sc.now()
            fin_tok = s_sc.now()
            def ftro(s, p=p, qi=qi):
                return s.dma_start(out=o_fm[p][:, 128 * qi:128 * (qi + 1)],
                                   in_=o_bf[:, :], transpose=True)
            sch.on("sync", ftro, waits=(fin_tok,), incs=(s_otr.inc(16),))
            obf_free = s_otr.now()
        vt_free = s_tn.now()
        if DBG and p == 0 and PH >= 7:
            for nm, tt in (("d_vtok", v_tok), ("d_ofm0", o_fm[0])):
                a = dbg(nm, [128, tt.shape[1]], BF16)
                def fdv(sy, a=a, tt=tt):
                    return sy.dma_start(out=a, in_=tt[:, :])
                sch.on("sync", fdv, waits=(s_otr.now(),),
                       incs=(s_out.inc(16),))
        ofm_tok.append(s_otr.now())
        if PH >= 7:
            last_done = s_otr.now()

    segY.reset(markY1)
    o_full = [segY.alloc([128, TOK], BF16, f"ofl{g}") for g in range(CG)]
    wob_blk = [segY.alloc([128, D], BF16, f"wob{i}") for i in range(2)]
    xnm_st = [segY.alloc([128, 512], BF16, f"xnm{c}") for c in range(CG)]
    osum = segY.alloc([128, 512], F32, "osum")

    flash_done = s_tn.now()

    def foship(g):
        pid = parity_open(g)
        with g.If_eq(pid, 0):
            exA[0] += 4 * 16
            for p in range(4):
                g.dma_start(out=sh_O[0].ap()[128 * p:128 * (p + 1), :],
                            in_=o_fm[p][:, :]).then_inc(s_exA.h, 16)
            g.wait_ge(s_exA.h, exA[0])
            exA[0] += 16
            g.dma_start(out=sh_fl.ap()[0:1, 10:11],
                        in_=flagv[:, :]).then_inc(s_exA.h, 16)
            emit_poll(g, 11, 0)
            exA[0] += CG * 16
            for gg in range(CG):
                g.dma_start(
                    out=o_full[gg][:, :],
                    in_=sh_O[gg // 4].ap()[128 * (gg % 4):128 * (gg % 4 + 1),
                                           0:TOK]).then_inc(s_exA.h, 16)
            g.wait_ge(s_exA.h, exA[0])
        with g.Else():
            exB[0] += 4 * 16
            for p in range(4):
                g.dma_start(out=sh_O[1].ap()[128 * p:128 * (p + 1), :],
                            in_=o_fm[p][:, :]).then_inc(s_exB.h, 16)
            g.wait_ge(s_exB.h, exB[0])
            exB[0] += 16
            g.dma_start(out=sh_fl.ap()[0:1, 11:12],
                        in_=flagv[:, :]).then_inc(s_exB.h, 16)
            emit_poll(g, 10, 1)
            exB[0] += CG * 16
            for gg in range(CG):
                g.dma_start(
                    out=o_full[gg][:, :],
                    in_=sh_O[gg // 4].ap()[128 * (gg % 4):128 * (gg % 4 + 1),
                                           TOK:TFULL]).then_inc(s_exB.h, 16)
            g.wait_ge(s_exB.h, exB[0])
        return g.nop()
    if PH >= 8:
        sch.on("gpsimd", foship, waits=tuple(ofm_tok) + (flash_done,),
               incs=(s_gp.inc(),))
    o_ready = s_gp.now()

    wblk_free[("wob", 0)] = flash_done
    wblk_free[("wob", 1)] = flash_done
    xnm_free = None
    for bi, (t0, W) in enumerate(TB4 if PH >= 8 else []):
        def fxnm(g, t0=t0, W=W):
            pid = parity_open(g)
            with g.If_eq(pid, 0):
                exA[0] += CG * 16
                for c in range(CG):
                    g.dma_start(
                        out=xnm_st[c][:, 0:W],
                        in_=sh_xn[0].ap()[128 * c:128 * (c + 1),
                                          t0:t0 + W]).then_inc(s_exA.h, 16)
                g.wait_ge(s_exA.h, exA[0])
            with g.Else():
                exB[0] += CG * 16
                for c in range(CG):
                    g.dma_start(
                        out=xnm_st[c][:, 0:W],
                        in_=sh_xn[1].ap()[128 * c:128 * (c + 1),
                                          t0:t0 + W]).then_inc(s_exB.h, 16)
                g.wait_ge(s_exB.h, exB[0])
            return g.nop()
        sch.on("gpsimd", fxnm, waits=(o_ready, xnm_free), incs=(s_gp.inc(),))
        xnm_tok = s_gp.now()
        for cg in range(CG):
            buf = cg % 2
            wtok = load_wblk(wob_blk[buf], woTb, cg, ("wob", buf))
            bank = cg % 4
            mm_group(bank, [(wob_blk[buf][:, 128 * dd:128 * (dd + 1)],
                             o_full[dd][:, t0:t0 + W], pb[bank][:, 0:W])
                            for dd in range(CG)], waits=(wtok, o_ready))
            ptok = t_fence()
            wblk_free[("wob", buf)] = ptok
            def fbo(s, bank=bank, cg=cg, W=W):
                return s.activation(osum[:, 0:W], pb[bank][:, 0:W], AF.Identity,
                                    bias=bo_st[cg][:, 0:1])
            sch.on("scalar", fbo, waits=(ptok, s_vc.now()),
                       incs=(s_sc.inc(),))
            bo_tok = s_sc.now()
            bank_free[bank] = bo_tok
            def fr1(v, cg=cg, W=W):
                return v.tensor_add(osum[:, 0:W], osum[:, 0:W],
                                    xnm_st[cg][:, 0:W])
            def fr2(v, cg=cg, t0=t0, W=W):
                return v.tensor_add(x1[cg][:, t0:t0 + W],
                                    x1[cg][:, t0:t0 + W], osum[:, 0:W])
            sch.on("vector", fr1, waits=(bo_tok, xnm_tok),
                   incs=(s_vc.inc(),))
            sch.on("vector", fr2, waits=(s_vc.now(),), incs=(s_vc.inc(),))
        xnm_free = s_vc.now()
    x3_tok = s_vc.now()
    if PH >= 8:
        last_done = x3_tok

    if PH >= 9:
        x4_tok = emit_ffn(x3_tok, "f2")
        last_done = x4_tok
    else:
        x4_tok = last_done

    for cg in range(CG):
        def fo2(s, cg=cg):
            return s.dma_start(out=out_fm[128 * cg:128 * (cg + 1), :],
                               in_=x1[cg][:, :])
        sch.on("sync", fo2, waits=(last_done,) if cg == 0 else (),
               incs=(s_out.inc(16),))
    sch.on("sync", lambda s: s.nop(), waits=(s_out.now(),))

    with nc.Block() as block:
        block.tensor(sch.runner("tensor"))
        block.vector(sch.runner("vector"))
        block.scalar(sch.runner("scalar"))
        block.gpsimd(sch.runner("gpsimd"))
        block.sync(sch.runner("sync"))

    return nc


def _blk(wt, og):
    i, o = wt.shape
    return np.ascontiguousarray(wt.reshape(i, og, 128).transpose(1, 0, 2))


def _col(v):
    return np.ascontiguousarray(v.reshape(-1, 128).T)


def kernel(**inp):
    x = np.asarray(inp["L0_attn_a_x"], np.float32)
    g = lambda n: np.asarray(inp[n], np.float32)

    base = dict(
        wkT=_blk(g("L0_attn_a_Wk").T, CG),
        wvT=_blk(g("L0_attn_a_Wv").T, CG),
        wrT=_blk(g("L0_attn_a_Wr").T, CG),
        woTa=_blk(g("L0_attn_a_Wo").T, CG),
        veca=np.concatenate([_col(g("L0_attn_a_time_decay")),
                             _col(g("L0_attn_a_time_first")),
                             _col(g("L0_attn_a_time_mix_k")),
                             _col(g("L0_attn_a_time_mix_v")),
                             _col(g("L0_attn_a_time_mix_r"))], axis=1),
        w1T=_blk(g("L0_ffn_w1_w").T, HG),
        w2T=_blk(g("L0_ffn_w2_w").T, HG),
        h1T=_blk(g("L0_ffn_h1_w").T, HG),
        h2T=_blk(g("L0_ffn_h2_w").T, HG),
        w3T=_blk(g("L0_ffn_w3_w").T, CG),
        ffs=np.concatenate([_col(g("L0_ffn_w1_s")), _col(g("L0_ffn_w2_s")),
                            _col(g("L0_ffn_h1_s")), _col(g("L0_ffn_h2_s")),
                            _col(g("L0_ffn_w3_s"))], axis=1),
        woTb=_blk(g("L0_attn_b_Wo").T, CG),
    )
    dmask = np.zeros((128, 4 * 512), np.float32)
    for mv in range(4):
        keep = np.arange(512)[None, :] <= (128 * mv + np.arange(128)[:, None])
        dmask[:, 512 * mv:512 * (mv + 1)] = np.where(keep, 0.0, -1e30)
    base["dmask"] = dmask
    base["dmask01"] = (dmask == 0.0).astype(np.float32)
    base["czero"] = np.zeros((128, 2), np.float32)
    base["nonce"] = np.array([[np.int64(np.random.randint(1, 2**31 - 1))]],
                             dtype=np.int32)

    in_maps = []
    for b in range(4):
        for hh in range(2):
            m = dict(base)
            t0 = TOK * hh
            xin = np.empty((D, TOK + 1), np.float32)
            if hh == 0:
                xin[:, 0] = 0.0
            else:
                xin[:, 0] = x[b, TOK - 1, :]
            xin[:, 1:] = x[b, t0:t0 + TOK, :].T
            m["xin"] = np.ascontiguousarray(xin)
            hs = slice(512 * hh, 512 * (hh + 1))
            m["wqT"] = _blk(g("L0_attn_b_Wq")[hs, :].T, 4)
            m["wkTb"] = _blk(g("L0_attn_b_Wk")[hs, :].T, 4)
            m["wvTb"] = _blk(g("L0_attn_b_Wv")[hs, :].T, 4)
            m["vecb"] = np.concatenate(
                [_col(g("L0_attn_b_bq")[hs]), _col(g("L0_attn_b_bk")[hs]),
                 _col(g("L0_attn_b_bv")[hs]), _col(g("L0_attn_b_norm_w")),
                 _col(g("L0_attn_b_bo"))], axis=1)
            in_maps.append(m)

    nc = build_program()
    res = run_bass_kernel_spmd(nc, in_maps, core_ids=list(range(8)))
    global LAST_RES
    LAST_RES = res

    out = np.empty((4, TFULL, D), np.float32)
    for b in range(4):
        for hh in range(2):
            out[b, TOK * hh:TOK * (hh + 1), :] = \
                res.results[2 * b + hh]["out_fm"].T
    return out


# version 09
# speedup vs baseline: 1.1062x; 1.0801x over previous
import numpy as np
import concourse.bass as bass
import concourse.mybir as mybir
from concourse.bass_utils import run_bass_kernel_spmd

F32 = mybir.dt.float32
BF16 = mybir.dt.bfloat16
I32 = mybir.dt.int32
AF = mybir.ActivationFunctionType
ALU = mybir.AluOpType
AX = mybir.AxisListType.X

D = 1024
HID = 4096
TOK = 2048
TFULL = 4096
CG = 8
HG = 32
NQT = 32
FLAG = 77777
POLL_CAP = 50000000
LN_EPS = 1e-5
RMS_EPS = 1e-6
TB4 = [(0, 512), (512, 512), (1024, 512), (1536, 512)]
TB8 = [(512 * i, 512) for i in range(8)]

PHASES = 99
DBG = False
LAST_RES = None

ENGS = ("tensor", "vector", "scalar", "gpsimd", "sync")


class Sem:
    def __init__(self, nc, name):
        self.h = nc.alloc_semaphore(name)
        self.n = 0

    def inc(self, by=1):
        self.n += by
        return (self.h, by)

    def now(self):
        return (self.h, self.n)


class Sched:
    def __init__(self):
        self.ops = {e: [] for e in ENGS}

    def on(self, eng, fn, waits=(), incs=()):
        waits = tuple(w for w in waits if w is not None)
        self.ops[eng].append((waits, fn, tuple(incs)))

    def runner(self, name):
        ops = self.ops[name]

        def f(e):
            for waits, fn, incs in ops:
                for (h, v) in waits:
                    e.wait_ge(h, v)
                inst = fn(e)
                for (h, by) in incs:
                    inst = inst.then_inc(h, by)

        return f


class SegAlloc:
    def __init__(self, nc, base, size, tag):
        self.nc = nc
        self.base = base
        self.size = size
        self.ptr = base
        self.tag = tag
        self.uid = 0

    def alloc(self, shape, dt, name=None):
        per = int(np.prod(shape[1:])) * (2 if dt == BF16 else 4)
        per = (per + 31) & ~31
        assert self.ptr + per <= self.base + self.size, (
            f"seg {self.tag} overflow: {name} needs {per}B at "
            f"{self.ptr - self.base}, cap {self.size}")
        off = self.ptr
        self.ptr = off + per
        self.uid += 1
        return self.nc.alloc_sbuf_tensor_at(
            name or f"{self.tag}{self.uid}", shape, dt, offset=off)

    def mark(self):
        return self.ptr

    def reset(self, m=None):
        self.ptr = self.base if m is None else m


def build_program():
    PH = PHASES
    dbg_outs = {}

    def dbg(name, shape, dt=F32):
        dbg_outs[name] = nc.dram_tensor(name, shape, dt,
                                        kind="ExternalOutput").ap()
        return dbg_outs[name]
    nc = bass.Bass("TRN2", target_bir_lowering=False)

    def din(name, shape, dt=F32):
        return nc.dram_tensor(name, shape, dt, kind="ExternalInput").ap()

    xin = din("xin", [D, TOK + 1])
    wkT = din("wkT", [CG, 128, D])
    wvT = din("wvT", [CG, 128, D])
    wrT = din("wrT", [CG, 128, D])
    woTa = din("woTa", [CG, 128, D])
    veca = din("veca", [128, 40])
    w1T = din("w1T", [HG, 128, D])
    w2T = din("w2T", [HG, 128, D])
    h1T = din("h1T", [HG, 128, HID])
    h2T = din("h2T", [HG, 128, HID])
    w3T = din("w3T", [CG, 128, HID])
    ffs = din("ffs", [128, 4 * HG + CG])
    wqT = din("wqT", [4, 128, D])
    wkTb = din("wkTb", [4, 128, D])
    wvTb = din("wvTb", [4, 128, D])
    woTb = din("woTb", [CG, 128, D])
    vecb = din("vecb", [128, 28])
    dmask = din("dmask", [128, 4 * 512])
    dmask01 = din("dmask01", [128, 4 * 512])
    nonce = din("nonce", [1, 1], I32)
    czero = din("czero", [128, 2])
    out_fm = nc.dram_tensor("out_fm", [D, TOK], F32, kind="ExternalOutput").ap()

    def shared(name, shape, dt):
        return nc.dram_tensor(name, shape, dt, kind="Internal",
                              addr_space="Shared")

    sh_carry = [shared(f"sh_carry{d}", [128, 2], F32) for d in range(CG)]
    sh_xn = [shared(f"sh_xn{s}", [D, TOK], BF16) for s in range(2)]
    sh_O = [shared(f"sh_O{s}", [512, TFULL], BF16) for s in range(2)]
    sh_fl = shared("sh_fl", [1, 16], I32)

    s_wld = Sem(nc, "s_wld")
    s_xld = Sem(nc, "s_xld")
    s_tn = Sem(nc, "s_tn")
    s_vc = Sem(nc, "s_vc")
    s_sc = Sem(nc, "s_sc")
    s_gp = Sem(nc, "s_gp")
    s_tr = Sem(nc, "s_tr")
    s_out = Sem(nc, "s_out")
    s_junk = Sem(nc, "s_junk")
    s_exA = Sem(nc, "s_exA")
    s_exB = Sem(nc, "s_exB")
    exA = [0]
    exB = [0]

    vec_a = nc.alloc_sbuf_tensor("vec_a", [128, 40], F32)
    mix1m = nc.alloc_sbuf_tensor("mix1m", [128, 24], F32)
    ffs_s = nc.alloc_sbuf_tensor("ffs_s", [128, 4 * HG + CG], F32)
    vec_b = nc.alloc_sbuf_tensor("vec_b", [128, 28], F32)
    bq_sc = nc.alloc_sbuf_tensor("bq_sc", [128, 4], F32)
    bq_st = [nc.alloc_sbuf_tensor(f"bq_st{p}", [128, 1], F32) for p in range(4)]
    bk_st = [nc.alloc_sbuf_tensor(f"bk_st{p}", [128, 1], F32) for p in range(4)]
    bv_st = [nc.alloc_sbuf_tensor(f"bv_st{p}", [128, 1], F32) for p in range(4)]
    bo_st = [nc.alloc_sbuf_tensor(f"bo_st{c}", [128, 1], F32) for c in range(CG)]
    ew_d = nc.alloc_sbuf_tensor("ew_d", [128, 3 * CG], F32)
    ew_w = [nc.alloc_sbuf_tensor(f"ew_w{d}", [128, 1], F32) for d in range(CG)]
    ones_col = nc.alloc_sbuf_tensor("ones_col", [128, 1], F32)
    ones_row = nc.alloc_sbuf_tensor("ones_row", [1, 128], F32)
    flagv = nc.alloc_sbuf_tensor("flagv", [1, 1], I32)
    flagr = nc.alloc_sbuf_tensor("flagr", [1, 1], I32)
    eps1 = nc.alloc_sbuf_tensor("eps1", [1, 1], F32)
    eps2 = nc.alloc_sbuf_tensor("eps2", [1, 1], F32)
    srow = nc.alloc_sbuf_tensor("srow", [1, 512], F32)
    mrow = nc.alloc_sbuf_tensor("mrow", [1, 512], F32)
    vrow = nc.alloc_sbuf_tensor("vrow", [1, 512], F32)
    brow = nc.alloc_sbuf_tensor("brow", [1, 512], F32)
    sq_t = [nc.alloc_sbuf_tensor(f"sq{i}", [128, 512], F32) for i in range(2)]
    carry_sb = nc.alloc_sbuf_tensor("carry_sb", [128, 2 * CG], F32)
    carry_out = nc.alloc_sbuf_tensor("carry_out", [128, 2 * CG], F32)
    o_acc = nc.alloc_sbuf_tensor("o_acc", [128, 128], F32)
    o_bf = nc.alloc_sbuf_tensor("o_bf", [128, 128], BF16)
    mneg_t = [nc.alloc_sbuf_tensor(f"mneg{h}", [128, 1], F32) for h in range(2)]
    mx8 = [nc.alloc_sbuf_tensor(f"mx8_{h}", [128, 8], F32) for h in range(2)]
    lc8 = [nc.alloc_sbuf_tensor(f"lc8_{h}", [128, 8], F32) for h in range(2)]
    lsum = [nc.alloc_sbuf_tensor(f"lsum{h}", [128, 1], F32) for h in range(2)]
    lsc0 = nc.alloc_sbuf_tensor("lsc0", [128, 8], F32)
    lsc = [lsc0, lsc0]
    rl_t = [nc.alloc_sbuf_tensor(f"rl{h}", [128, 1], F32) for h in range(2)]

    SEGX_SZ = 64 * 1024
    base0 = (nc.sbuf_base + 31) & ~31
    avail = nc.sbuf_top - base0 - 64
    segX = SegAlloc(nc, base0, SEGX_SZ, "sx")
    segY = SegAlloc(nc, base0 + SEGX_SZ, avail - SEGX_SZ, "sy")

    pb = [nc.alloc_psum_tensor(f"pb{i}", [128, 512], F32) for i in range(8)]
    bank_free = {i: None for i in range(8)}

    sch = Sched()

    def mm_group(bank, mms, waits=()):
        w = [x for x in waits if x is not None]
        if bank_free[bank] is not None:
            w.append(bank_free[bank])
        n = len(mms)
        for i, (lhsT, rhs, out) in enumerate(mms):
            def f(t, lhsT=lhsT, rhs=rhs, out=out, st=(i == 0), sp=(i == n - 1)):
                return t.matmul(out, lhsT, rhs, start=st, stop=sp)
            sch.on("tensor", f, waits=w if i == 0 else ())

    def t_fence():
        sch.on("tensor", lambda t: (t.drain(), t.nop())[1], incs=(s_tn.inc(),))
        return s_tn.now()

    sch.on("gpsimd", lambda g: g.dma_start(out=vec_a[:, :], in_=veca),
           incs=(s_xld.inc(16),))
    sch.on("gpsimd", lambda g: g.dma_start(out=ffs_s[:, :], in_=ffs),
           incs=(s_xld.inc(16),))
    sch.on("gpsimd", lambda g: g.dma_start(out=vec_b[:, :], in_=vecb),
           incs=(s_xld.inc(16),))
    sch.on("gpsimd", lambda g: g.dma_start(out=flagv[:, :], in_=nonce),
           incs=(s_xld.inc(16),))
    setup_ld = s_xld.now()
    regs = {}

    def fsetup_regs(g):
        for r in ("nonce", "pid", "fv", "cnt", "t1", "t2"):
            regs[r] = g.alloc_register("R" + r)
        g.reg_load(regs["nonce"], flagv[0:1, 0:1])
        g.reg_mov(regs["pid"], g.partition_id())
        g.reg_mod(regs["pid"], regs["pid"], 2)
        return g.nop()
    sch.on("gpsimd", fsetup_regs, waits=(setup_ld,))

    sch.on("vector", lambda v: v.memset(ones_col[:, :], 1.0),
           incs=(s_vc.inc(),))
    sch.on("vector", lambda v: v.memset(eps1[:, :], LN_EPS),
           incs=(s_vc.inc(),))
    sch.on("vector", lambda v: v.memset(eps2[:, :], RMS_EPS),
           incs=(s_vc.inc(),))
    sch.on("vector", lambda v: v.memset(ones_row[:, :], 1.0),
           incs=(s_vc.inc(),))
    sch.on("vector", lambda v: v.memset(mx8[0][:, :], 0.0),
           incs=(s_vc.inc(),))
    sch.on("vector", lambda v: v.memset(mx8[1][:, :], 0.0),
           incs=(s_vc.inc(),))
    sch.on("vector",
           lambda v: v.tensor_scalar(mix1m[:, 0:24], vec_a[:, 16:40], -1.0, 1.0,
                                     ALU.mult, ALU.add),
           waits=(setup_ld,), incs=(s_vc.inc(),))
    for p in range(4):
        sch.on("vector", lambda v, p=p: v.tensor_scalar_mul(
            bq_st[p][:, :], vec_b[:, p:p + 1], 0.125), incs=(s_vc.inc(),))
        sch.on("vector", lambda v, p=p: v.tensor_copy(bk_st[p][:, :],
                                                      vec_b[:, 4 + p:5 + p]),
               incs=(s_vc.inc(),))
        sch.on("vector", lambda v, p=p: v.tensor_copy(bv_st[p][:, :],
                                                      vec_b[:, 8 + p:9 + p]),
               incs=(s_vc.inc(),))
    for c in range(CG):
        sch.on("vector", lambda v, c=c: v.tensor_copy(bo_st[c][:, :],
                                                      vec_b[:, 20 + c:21 + c]),
               incs=(s_vc.inc(),))
    setup_done = s_vc.now()

    def emit_ln_block(x_tiles, col0, W, out_tiles, ocol0, src_ready=()):
        mm_group(6, [(ones_col[:, 0:1], x_tiles[c][:, col0:col0 + W],
                      pb[6][0:1, 0:W]) for c in range(CG)], waits=src_ready)
        sq_guard = {}
        for c in range(CG):
            w = list(src_ready)
            if c >= 2:
                w.append((s_tn.h, sq_guard[c - 2]))
            def fsq(s, c=c, col0=col0, W=W):
                return s.activation(sq_t[c % 2][:, 0:W],
                                    x_tiles[c][:, col0:col0 + W], AF.Square)
            sch.on("scalar", fsq, waits=w, incs=(s_sc.inc(),))
            sqv = s_sc.now()
            bw = [sqv]
            if c == 0 and bank_free[7] is not None:
                bw.append(bank_free[7])
            def fm(t, c=c, W=W):
                return t.matmul(pb[7][0:1, 0:W], ones_col[:, 0:1],
                                sq_t[c % 2][:, 0:W], start=(c == 0),
                                stop=(c == CG - 1))
            sch.on("tensor", fm, waits=bw)
            sch.on("tensor", lambda t: (t.drain(), t.nop())[1],
                   incs=(s_tn.inc(),))
            sq_guard[c] = s_tn.n
        stats_tok = (s_tn.h, s_tn.n)
        sch.on("vector", lambda v, W=W: v.tensor_scalar_mul(
            mrow[0:1, 0:W], pb[6][0:1, 0:W], 1.0 / D), waits=(stats_tok,),
            incs=(s_vc.inc(),))
        sch.on("vector", lambda v, W=W: v.tensor_mul(
            vrow[0:1, 0:W], mrow[0:1, 0:W], mrow[0:1, 0:W]),
            waits=(s_vc.now(),), incs=(s_vc.inc(),))
        sch.on("vector", lambda v, W=W: v.scalar_tensor_tensor(
            vrow[0:1, 0:W], pb[7][0:1, 0:W], 1.0 / D, vrow[0:1, 0:W],
            ALU.mult, ALU.subtract), waits=(s_vc.now(),),
            incs=(s_vc.inc(),))
        var_tok = s_vc.now()
        sch.on("scalar", lambda s, W=W: s.activation(
            srow[0:1, 0:W], vrow[0:1, 0:W], AF.Sqrt, bias=eps1[0:1, 0:1]),
            waits=(var_tok,), incs=(s_sc.inc(),))
        sch.on("vector", lambda v, W=W: v.reciprocal(vrow[0:1, 0:W],
                                                     srow[0:1, 0:W]),
               waits=(s_sc.now(),), incs=(s_vc.inc(),))
        sch.on("vector", lambda v, W=W: v.scalar_tensor_tensor(
            brow[0:1, 0:W], mrow[0:1, 0:W], -1.0, vrow[0:1, 0:W],
            ALU.mult, ALU.mult), waits=(s_vc.now(),), incs=(s_vc.inc(),))
        rows_tok = s_vc.now()
        bank_free[6] = rows_tok
        bank_free[7] = rows_tok
        mm_group(6, [(ones_row[0:1, :], vrow[0:1, 0:W], pb[6][:, 0:W])],
                 waits=(rows_tok,))
        mm_group(7, [(ones_row[0:1, :], brow[0:1, 0:W], pb[7][:, 0:W])])
        bc_tok = t_fence()
        for c in range(CG):
            def fa(v, c=c, col0=col0, W=W):
                return v.tensor_mul(sq_t[c % 2][:, 0:W],
                                    x_tiles[c][:, col0:col0 + W], pb[6][:, 0:W])
            def fb(v, c=c, ocol0=ocol0, W=W):
                return v.tensor_add(out_tiles[c][:, ocol0:ocol0 + W],
                                    sq_t[c % 2][:, 0:W], pb[7][:, 0:W])
            sch.on("vector", fa,
                   waits=(bc_tok,) if c == 0 else (s_vc.now(),),
                   incs=(s_vc.inc(),))
            sch.on("vector", fb, waits=(s_vc.now(),), incs=(s_vc.inc(),))
        apply_tok = s_vc.now()
        bank_free[6] = apply_tok
        bank_free[7] = apply_tok
        return apply_tok

    xk_b = [segY.alloc([128, TOK], BF16, f"xk{c}") for c in range(CG)]
    xv_b = [segY.alloc([128, TOK], BF16, f"xv{c}") for c in range(CG)]
    xr_b = [segY.alloc([128, TOK], BF16, f"xr{c}") for c in range(CG)]
    rwkv_b = [segY.alloc([128, TOK], BF16, f"rw{c}") for c in range(CG)]
    ln_bf = [segX.alloc([128, TOK + 1], BF16, f"ln{c}") for c in range(CG)]
    x_blk = [segX.alloc([128, 512], F32, f"xblk{c}") for c in range(CG)]
    mixtmp = segX.alloc([128, 512], F32, "mixtmp")

    LN1_BLOCKS = [(0, 512), (512, 512), (1024, 512), (1536, 512), (1537, 512)]
    prev_apply = setup_done
    for (col0, W) in LN1_BLOCKS:
        for c in range(CG):
            def fl(g, c=c, col0=col0, W=W):
                return g.dma_start(out=x_blk[c][:, 0:W],
                                   in_=xin[128 * c:128 * (c + 1),
                                           col0:col0 + W])
            sch.on("gpsimd", fl, waits=(prev_apply,) if c == 0 else (),
                   incs=(s_xld.inc(16),))
        ld_tok = s_xld.now()
        prev_apply = emit_ln_block(x_blk, 0, W, ln_bf, col0,
                                   src_ready=(ld_tok,) if col0 else
                                   (ld_tok, setup_done))
    ln_done = prev_apply

    for (mi, out_t, tmcol, omcol) in ((0, xk_b, 16, 0), (1, xv_b, 24, 8),
                                      (2, xr_b, 32, 16)):
        for c in range(CG):
            for (t0, W) in TB4:
                def f1(v, c=c, t0=t0, W=W, tmcol=tmcol):
                    return v.tensor_scalar_mul(
                        mixtmp[:, 0:W], ln_bf[c][:, 1 + t0:1 + t0 + W],
                        vec_a[:, tmcol + c:tmcol + c + 1])
                def f2(v, c=c, t0=t0, W=W, omcol=omcol, out_t=out_t):
                    return v.scalar_tensor_tensor(
                        out_t[c][:, t0:t0 + W], ln_bf[c][:, t0:t0 + W],
                        mix1m[:, omcol + c:omcol + c + 1], mixtmp[:, 0:W],
                        ALU.mult, ALU.add)
                first = (mi == 0 and c == 0 and t0 == 0)
                sch.on("vector", f1,
                       waits=(ln_done,) if first else (s_vc.now(),),
                       incs=(s_vc.inc(),))
                sch.on("vector", f2, waits=(s_vc.now(),),
                       incs=(s_vc.inc(),))
    mix_done = s_vc.now()
    if DBG:
        d_ln0 = dbg("d_ln0", [128, TOK + 1], BF16)
        d_xk0 = dbg("d_xk0", [128, TOK], BF16)
        d_xr0 = dbg("d_xr0", [128, TOK], BF16)
        sch.on("sync", lambda sy: sy.dma_start(out=d_ln0, in_=ln_bf[0][:, :]),
               waits=(mix_done,), incs=(s_out.inc(16),))
        sch.on("sync", lambda sy: sy.dma_start(out=d_xk0, in_=xk_b[0][:, :]),
               incs=(s_out.inc(16),))
        sch.on("sync", lambda sy: sy.dma_start(out=d_xr0, in_=xr_b[0][:, :]),
               incs=(s_out.inc(16),))

    last_done = mix_done
    segX.reset()
    sr_d = segX.alloc([128, TOK], F32, "sr_d")
    ek_d = segX.alloc([128, TOK + 1], F32, "ek_d")
    ekv_d = segX.alloc([128, TOK + 1], F32, "ekv_d")
    aa_d = segX.alloc([128, TOK + 1], F32, "aa_d")
    bb_d = segX.alloc([128, TOK + 1], F32, "bb_d")
    ew_big = segX.alloc([128, TOK + 1], F32, "ew_big")
    wk_blk = [segX.alloc([128, D], BF16, f"wkb{i}") for i in range(2)]
    wv_blk = [segX.alloc([128, D], BF16, f"wvb{i}") for i in range(2)]
    wr_blk = [segX.alloc([128, D], BF16, f"wrb{i}") for i in range(2)]

    wblk_free = {}

    wsems = {}

    def wsem(key):
        if key not in wsems:
            wsems[key] = Sem(nc, "sw_" + "_".join(str(k) for k in key))
        return wsems[key]

    def load_wblk(dst, src_ap, idx, free_key, groups=CG):
        sem = wsem(free_key)
        def fl(g, dst=dst, src_ap=src_ap, idx=idx, groups=groups):
            return g.dma_start(out=dst[:, 0:groups * 128],
                               in_=src_ap[idx])
        sch.on("gpsimd", fl, waits=(wblk_free.get(free_key),),
               incs=(sem.inc(16),))
        return sem.now()

    def parity_open(g):
        return regs["pid"]

    def emit_poll(g, col, arm):
        sem, cntr = (s_exA, exA) if arm == 0 else (s_exB, exB)
        cntr[0] += 16
        g.dma_start(out=flagr[:, :],
                    in_=sh_fl.ap()[0:1, col:col + 1]).then_inc(sem.h, 16)
        g.wait_ge(sem.h, cntr[0])
        g.reg_mov(regs["cnt"], 0)

        def poll():
            g.dma_start(out=flagr[:, :],
                        in_=sh_fl.ap()[0:1, col:col + 1]).then_inc(s_junk.h, 16)
            g.drain()
            g.reg_add(regs["cnt"], regs["cnt"], 1)
            g.reg_load(regs["fv"], flagr[0:1, 0:1])
            g.reg_alu(regs["t1"], regs["fv"], regs["nonce"], ALU.not_equal)
            g.reg_alu(regs["t2"], regs["cnt"], POLL_CAP, ALU.is_lt)
            g.reg_alu(regs["t1"], regs["t1"], regs["t2"], ALU.logical_and)
            return regs["t1"]

        g.reg_load(regs["fv"], flagr[0:1, 0:1])
        g.reg_alu(regs["t1"], regs["fv"], regs["nonce"], ALU.not_equal)
        with g.If_ne(regs["t1"], 0):
            with g.While(poll):
                g.nop()

    for k in (("r", 0), ("r", 1), ("k", 0), ("k", 1), ("v", 0), ("v", 1)):
        wblk_free[k] = mix_done

    prev_y_tok = None
    for d in range(CG if PH >= 2 else 0):
        buf = d % 2
        wr_tok = load_wblk(wr_blk[buf], wrT, d, ("r", buf))
        wk_tok = load_wblk(wk_blk[buf], wkT, d, ("k", buf))
        wv_tok = load_wblk(wv_blk[buf], wvT, d, ("v", buf))
        for bi, (t0, W) in enumerate(TB4):
            bank = bi % 2
            mm_group(bank, [(wr_blk[buf][:, 128 * c:128 * (c + 1)],
                             xr_b[c][:, t0:t0 + W], pb[bank][:, 0:W])
                            for c in range(CG)], waits=(wr_tok, mix_done))
            ptok = t_fence()
            def fsg(s, bank=bank, t0=t0, W=W):
                return s.activation(sr_d[:, t0:t0 + W], pb[bank][:, 0:W],
                                    AF.Sigmoid)
            sch.on("scalar", fsg, waits=(ptok, prev_y_tok),
                   incs=(s_sc.inc(),))
            bank_free[bank] = s_sc.now()
        sr_tok = s_sc.now()
        for bi, (t0, W) in enumerate(TB4):
            bank = 2 + bi % 2
            mm_group(bank, [(wk_blk[buf][:, 128 * c:128 * (c + 1)],
                             xk_b[c][:, t0:t0 + W], pb[bank][:, 0:W])
                            for c in range(CG)], waits=(wk_tok,))
            ptok = t_fence()
            def fek(s, bank=bank, t0=t0, W=W):
                return s.activation(ek_d[:, 1 + t0:1 + t0 + W],
                                    pb[bank][:, 0:W], AF.Exp)
            sch.on("scalar", fek, waits=(ptok, prev_y_tok),
                   incs=(s_sc.inc(),))
            bank_free[bank] = s_sc.now()
        ek_tok = s_sc.now()
        def fcarry_recv(g, d=d):
            pid = parity_open(g)
            with g.If_eq(pid, 0):
                exA[0] += 16
                g.dma_start(out=carry_sb[:, 2 * d:2 * d + 2],
                            in_=czero).then_inc(s_exA.h, 16)
                g.wait_ge(s_exA.h, exA[0])
            with g.Else():
                emit_poll(g, d, 1)
                exB[0] += 16
                g.dma_start(out=carry_sb[:, 2 * d:2 * d + 2],
                            in_=sh_carry[d].ap()).then_inc(s_exB.h, 16)
                g.wait_ge(s_exB.h, exB[0])
            return g.nop()
        sch.on("gpsimd", fcarry_recv, incs=(s_gp.inc(),))
        carry_tok = s_gp.now()
        for bi, (t0, W) in enumerate(TB4):
            bank = 4 + bi % 2
            mm_group(bank, [(wv_blk[buf][:, 128 * c:128 * (c + 1)],
                             xv_b[c][:, t0:t0 + W], pb[bank][:, 0:W])
                            for c in range(CG)], waits=(wv_tok,))
            ptok = t_fence()
            def fekv(v, bank=bank, t0=t0, W=W):
                return v.tensor_mul(ekv_d[:, 1 + t0:1 + t0 + W],
                                    pb[bank][:, 0:W],
                                    ek_d[:, 1 + t0:1 + t0 + W])
            sch.on("vector", fekv, waits=(ptok, ek_tok), incs=(s_vc.inc(),))
            bank_free[bank] = s_vc.now()
        sch.on("vector", lambda v, d=d: v.tensor_copy(
            ekv_d[:, 0:1], carry_sb[:, 2 * d:2 * d + 1]), waits=(carry_tok,),
            incs=(s_vc.inc(),))
        sch.on("vector", lambda v, d=d: v.tensor_copy(
            ek_d[:, 0:1], carry_sb[:, 2 * d + 1:2 * d + 2]),
            incs=(s_vc.inc(),))
        cpy_tok = s_vc.now()
        sch.on("scalar", lambda s, d=d: s.activation(
            ew_d[:, 3 * d:3 * d + 1], vec_a[:, d:d + 1], AF.Exp),
            waits=(setup_done,), incs=(s_sc.inc(),))
        sch.on("scalar", lambda s, d=d: s.activation(
            ew_d[:, 3 * d + 1:3 * d + 2], ew_d[:, 3 * d:3 * d + 1],
            AF.Exp, scale=-1.0), waits=(s_sc.now(),), incs=(s_sc.inc(),))
        sch.on("scalar", lambda s, d=d: s.activation(
            ew_d[:, 3 * d + 2:3 * d + 3], vec_a[:, 8 + d:9 + d], AF.Exp),
            incs=(s_sc.inc(),))
        sch.on("scalar", lambda s, d=d: s.activation(
            ew_big[:, 0:TOK], sr_d[:, 0:TOK], AF.Identity, scale=0.0,
            bias=ew_d[:, 3 * d + 1:3 * d + 2]), waits=(s_sc.now(), sr_tok),
            incs=(s_sc.inc(),))
        sch.on("scalar", lambda s, d=d: s.activation(
            ew_big[:, TOK:TOK + 1], sr_d[:, 0:1], AF.Identity, scale=0.0,
            bias=ew_d[:, 3 * d + 1:3 * d + 2]), incs=(s_sc.inc(),))
        ewb_tok = s_sc.now()
        sch.on("vector", lambda v: v.tensor_tensor_scan(
            aa_d[:, :], ew_big[:, :], ekv_d[:, :], 0.0, ALU.mult, ALU.add),
            waits=(ewb_tok, cpy_tok), incs=(s_vc.inc(),))
        sch.on("vector", lambda v: v.tensor_tensor_scan(
            bb_d[:, :], ew_big[:, :], ek_d[:, :], 0.0, ALU.mult, ALU.add),
            incs=(s_vc.inc(),))
        scans_tok = s_vc.now()
        sch.on("vector", lambda v, d=d: v.tensor_copy(
            carry_out[:, 2 * d:2 * d + 1], aa_d[:, TOK:TOK + 1]),
            waits=(scans_tok,), incs=(s_vc.inc(),))
        sch.on("vector", lambda v, d=d: v.tensor_copy(
            carry_out[:, 2 * d + 1:2 * d + 2], bb_d[:, TOK:TOK + 1]),
            incs=(s_vc.inc(),))
        scan_tok = s_vc.now()
        def fcarry_send(g, d=d):
            pid = parity_open(g)
            with g.If_eq(pid, 0):
                exA[0] += 16
                g.dma_start(out=sh_carry[d].ap(),
                            in_=carry_out[:, 2 * d:2 * d + 2]
                            ).then_inc(s_exA.h, 16)
                g.wait_ge(s_exA.h, exA[0])
                exA[0] += 16
                g.dma_start(out=sh_fl.ap()[0:1, d:d + 1],
                            in_=flagv[:, :]).then_inc(s_exA.h, 16)
            with g.Else():
                g.nop()
            return g.nop()
        sch.on("gpsimd", fcarry_send, waits=(scan_tok,))
        if DBG and d in (0, 1, CG - 1):
            sfx = str(d) if d != CG - 1 else "7"
            for nm, tt in ((f"d_aa{sfx}", aa_d), (f"d_bb{sfx}", bb_d),
                           (f"d_ewb{sfx}", ew_big), (f"d_srg{sfx}", sr_d),
                           (f"d_ekg{sfx}", ek_d), (f"d_ekvg{sfx}", ekv_d)):
                a = dbg(nm, [128, tt.shape[1]])
                def fdmp(sy, a=a, tt=tt):
                    return sy.dma_start(out=a, in_=tt[:, :])
                sch.on("sync", fdmp, waits=(scan_tok,), incs=(s_out.inc(16),))
            sch.on("vector", lambda v: v.nop(), waits=(s_out.now(),))
        sch.on("vector", lambda v, d=d: v.scalar_tensor_tensor(
            aa_d[:, 0:TOK], ekv_d[:, 1:TOK + 1], ew_d[:, 3 * d + 2:3 * d + 3],
            aa_d[:, 0:TOK], ALU.mult, ALU.add), waits=(scan_tok,),
            incs=(s_vc.inc(),))
        sch.on("vector", lambda v, d=d: v.scalar_tensor_tensor(
            bb_d[:, 0:TOK], ek_d[:, 1:TOK + 1], ew_d[:, 3 * d + 2:3 * d + 3],
            bb_d[:, 0:TOK], ALU.mult, ALU.add), incs=(s_vc.inc(),))
        sch.on("vector", lambda v: v.reciprocal(ew_big[:, 0:TOK],
                                                bb_d[:, 0:TOK]),
               waits=(s_vc.now(),), incs=(s_vc.inc(),))
        sch.on("vector", lambda v: v.tensor_mul(aa_d[:, 0:TOK],
                                                aa_d[:, 0:TOK],
                                                ew_big[:, 0:TOK]),
               waits=(s_vc.now(),), incs=(s_vc.inc(),))
        sch.on("vector", lambda v, d=d: v.tensor_mul(
            rwkv_b[d][:, :], aa_d[:, 0:TOK], sr_d[:, :]),
            waits=(s_vc.now(),), incs=(s_vc.inc(),))
        prev_y_tok = s_vc.now()
    rwkv_done = s_vc.now()
    if DBG and PH >= 2:
        for dd in range(CG):
            ap = dbg(f"d_rw{dd}", [128, TOK], BF16)
            def fdr(sy, ap=ap, dd=dd):
                return sy.dma_start(out=ap, in_=rwkv_b[dd][:, :])
            sch.on("sync", fdr, incs=(s_out.inc(16),))
    if PH >= 2:
        last_done = rwkv_done

    segX.reset()
    x1 = [segX.alloc([128, TOK], F32, f"x1_{c}") for c in range(CG)]
    segY.reset()
    woa_blk = [segY.alloc([128, D], BF16, f"woab{i}") for i in range(2)]
    xi_blk = segY.alloc([128, 512], F32, "xi_blk")
    wblk_free[("woa", 0)] = rwkv_done
    wblk_free[("woa", 1)] = rwkv_done
    xi_free = rwkv_done
    for cg in range(CG if PH >= 3 else 0):
        buf = cg % 2
        wtok = load_wblk(woa_blk[buf], woTa, cg, ("woa", buf))
        for bi, (t0, W) in enumerate(TB4):
            bank = bi % 2
            mm_group(bank, [(woa_blk[buf][:, 128 * dd:128 * (dd + 1)],
                             rwkv_b[dd][:, t0:t0 + W], pb[bank][:, 0:W])
                            for dd in range(CG)], waits=(wtok, rwkv_done))
            ptok = t_fence()
            def fxl(g, cg=cg, t0=t0, W=W):
                return g.dma_start(out=xi_blk[:, 0:W],
                                   in_=xin[128 * cg:128 * (cg + 1),
                                           1 + t0:1 + t0 + W])
            sch.on("gpsimd", fxl, waits=(xi_free,), incs=(s_xld.inc(16),))
            xl_tok = s_xld.now()
            def fadd(v, cg=cg, t0=t0, W=W, bank=bank):
                return v.tensor_add(x1[cg][:, t0:t0 + W], pb[bank][:, 0:W],
                                    xi_blk[:, 0:W])
            sch.on("vector", fadd, waits=(ptok, xl_tok), incs=(s_vc.inc(),))
            bank_free[bank] = s_vc.now()
            xi_free = s_vc.now()
        wblk_free[("woa", buf)] = s_vc.now()
    x1_tok = s_vc.now()
    if PH >= 3:
        last_done = x1_tok

    def emit_ffn(ready_tok, tag):
        segY.reset()
        mark = segY.mark()
        lnb = [segY.alloc([128, 512], BF16, f"lnb{c}{tag}") for c in range(CG)]
        hbuf = [segY.alloc([128, 512], BF16, f"hb{h}{tag}") for h in range(HG)]
        gbuf = [segY.alloc([128, 512], BF16, f"gb{h}{tag}") for h in range(HG)]
        sA = segY.alloc([128, 512], F32, f"sA{tag}")
        w12 = [segY.alloc([128, D], BF16, f"w12_{i}{tag}") for i in range(4)]
        wbig = [segY.alloc([128, HID], BF16, f"wbig{i}{tag}") for i in range(2)]
        wbig_free = {0: ready_tok, 1: ready_tok}
        w12_free = {0: ready_tok, 1: ready_tok}
        last_tok = ready_tok
        for (t0, W) in TB4:
            ln_tok = emit_ln_block(x1, t0, W, lnb, 0, src_ready=(last_tok,))
            for h in range(HG):
                b0, b1 = (0, 1) if h % 2 == 0 else (2, 3)
                wb1, wb2 = w12[2 * (h % 2)], w12[2 * (h % 2) + 1]
                def fl1(g, wb1=wb1, h=h):
                    return g.dma_start(out=wb1[:, :], in_=w1T[h])
                def fl2(g, wb2=wb2, h=h):
                    return g.dma_start(out=wb2[:, :], in_=w2T[h])
                sem12 = wsem(("w12", h % 2))
                sch.on("gpsimd", fl1, waits=(w12_free.get(h % 2),),
                       incs=(sem12.inc(16),))
                sch.on("gpsimd", fl2, incs=(sem12.inc(16),))
                wl_tok = sem12.now()
                mm_group(b0, [(wb1[:, 128 * c:128 * (c + 1)], lnb[c][:, 0:W],
                               pb[b0][:, 0:W]) for c in range(CG)],
                         waits=(wl_tok, ln_tok))
                mm_group(b1, [(wb2[:, 128 * c:128 * (c + 1)], lnb[c][:, 0:W],
                               pb[b1][:, 0:W]) for c in range(CG)])
                ptok = t_fence()
                w12_free[h % 2] = ptok
                def fsilu(s, b0=b0, h=h, W=W):
                    return s.activation(sA[:, 0:W], pb[b0][:, 0:W], AF.Silu,
                                        scale=ffs_s[:, h:h + 1])
                sch.on("scalar", fsilu, waits=(ptok, s_vc.now()),
                       incs=(s_sc.inc(),))
                si_tok = s_sc.now()
                bank_free[b0] = si_tok
                def fh(v, b1=b1, h=h, W=W):
                    return v.scalar_tensor_tensor(
                        hbuf[h][:, 0:W], pb[b1][:, 0:W],
                        ffs_s[:, HG + h:HG + h + 1], sA[:, 0:W],
                        ALU.mult, ALU.mult)
                sch.on("vector", fh, waits=(si_tok,), incs=(s_vc.inc(),))
                bank_free[b1] = s_vc.now()
            stage_in_tok = s_vc.now()
            src, dst = hbuf, gbuf
            for (wT, scol) in ((h1T, 2 * HG), (h2T, 3 * HG)):
                for og in range(HG):
                    bank = og % 4
                    wb = wbig[og % 2]
                    def flw(g, wb=wb, wT=wT, og=og):
                        return g.dma_start(out=wb[:, :], in_=wT[og])
                    semb = wsem(("wbig", og % 2))
                    sch.on("gpsimd", flw, waits=(wbig_free.get(og % 2),),
                           incs=(semb.inc(16),))
                    wl_tok = semb.now()
                    mm_group(bank, [(wb[:, 128 * k:128 * (k + 1)],
                                     src[k][:, 0:W], pb[bank][:, 0:W])
                                    for k in range(HG)],
                             waits=(wl_tok, stage_in_tok))
                    ptok = t_fence()
                    wbig_free[og % 2] = ptok
                    def fg(s, bank=bank, og=og, W=W, scol=scol, dst=dst):
                        return s.activation(
                            dst[og][:, 0:W], pb[bank][:, 0:W], AF.Gelu,
                            scale=ffs_s[:, scol + og:scol + og + 1])
                    sch.on("scalar", fg, waits=(ptok,), incs=(s_sc.inc(),))
                    bank_free[bank] = s_sc.now()
                src, dst = dst, src
                stage_in_tok = s_sc.now()
            for cg in range(CG):
                bank = cg % 4
                wb = wbig[cg % 2]
                def flw3(g, wb=wb, cg=cg):
                    return g.dma_start(out=wb[:, :], in_=w3T[cg])
                semb = wsem(("wbig", cg % 2))
                sch.on("gpsimd", flw3, waits=(wbig_free.get(cg % 2),),
                       incs=(semb.inc(16),))
                wl_tok = semb.now()
                mm_group(bank, [(wb[:, 128 * k:128 * (k + 1)],
                                 src[k][:, 0:W], pb[bank][:, 0:W])
                                for k in range(HG)],
                         waits=(wl_tok, stage_in_tok))
                ptok = t_fence()
                wbig_free[cg % 2] = ptok
                def fx(v, bank=bank, cg=cg, t0=t0, W=W):
                    return v.scalar_tensor_tensor(
                        x1[cg][:, t0:t0 + W], pb[bank][:, 0:W],
                        ffs_s[:, 4 * HG + cg:4 * HG + cg + 1],
                        x1[cg][:, t0:t0 + W], ALU.mult, ALU.add)
                sch.on("vector", fx, waits=(ptok,), incs=(s_vc.inc(),))
                bank_free[bank] = s_vc.now()
            last_tok = s_vc.now()
        segY.reset(mark)
        return last_tok

    if PH >= 4:
        x2_tok = emit_ffn(x1_tok, "f1")
        last_done = x2_tok
    else:
        x2_tok = x1_tok

    segY.reset()
    markY0 = segY.mark()
    xn_loc = [segY.alloc([128, 512], BF16, f"xnl{c}") for c in range(CG)]
    t_f32 = [segY.alloc([128, 512], F32, f"t3_{c}") for c in range(CG)]
    last = x2_tok
    for bi, (t0, W) in enumerate(TB4 if PH >= 5 else []):
        ln_tok = emit_ln_block(x1, t0, W, t_f32, 0, src_ready=(last,))
        sq_guard = {}
        for c in range(CG):
            w = [ln_tok] if c == 0 else []
            if c >= 2:
                w.append((s_tn.h, sq_guard[c - 2]))
            def fsq(s, c=c, W=W):
                return s.activation(sq_t[c % 2][:, 0:W], t_f32[c][:, 0:W],
                                    AF.Square)
            sch.on("scalar", fsq, waits=w, incs=(s_sc.inc(),))
            sqv = s_sc.now()
            bw = [sqv]
            if c == 0 and bank_free[7] is not None:
                bw.append(bank_free[7])
            def fm(t, c=c, W=W):
                return t.matmul(pb[7][0:1, 0:W], ones_col[:, 0:1],
                                sq_t[c % 2][:, 0:W], start=(c == 0),
                                stop=(c == CG - 1))
            sch.on("tensor", fm, waits=bw)
            sch.on("tensor", lambda t: (t.drain(), t.nop())[1],
                   incs=(s_tn.inc(),))
            sq_guard[c] = s_tn.n
        rms_tok = (s_tn.h, s_tn.n)
        sch.on("scalar", lambda s, W=W: s.activation(
            srow[0:1, 0:W], pb[7][0:1, 0:W], AF.Sqrt, bias=eps2[0:1, 0:1],
            scale=1.0 / D), waits=(rms_tok,), incs=(s_sc.inc(),))
        sch.on("vector", lambda v, W=W: v.reciprocal(vrow[0:1, 0:W],
                                                     srow[0:1, 0:W]),
               waits=(s_sc.now(),), incs=(s_vc.inc(),))
        rr_tok = s_vc.now()
        bank_free[7] = rr_tok
        mm_group(7, [(ones_row[0:1, :], vrow[0:1, 0:W], pb[7][:, 0:W])],
                 waits=(rr_tok,))
        bc_tok = t_fence()
        for c in range(CG):
            def fa(v, c=c, W=W):
                return v.tensor_mul(sq_t[c % 2][:, 0:W], t_f32[c][:, 0:W],
                                    pb[7][:, 0:W])
            def fb(v, c=c, W=W):
                return v.tensor_scalar_mul(xn_loc[c][:, 0:W],
                                           sq_t[c % 2][:, 0:W],
                                           vec_b[:, 12 + c:13 + c])
            sch.on("vector", fa,
                   waits=(bc_tok,) if c == 0 else (s_vc.now(),),
                   incs=(s_vc.inc(),))
            sch.on("vector", fb, waits=(s_vc.now(),), incs=(s_vc.inc(),))
        xn_tok = s_vc.now()
        bank_free[7] = xn_tok
        def fship(g, t0=t0, W=W):
            pid = parity_open(g)
            with g.If_eq(pid, 0):
                exA[0] += CG * 16
                for c in range(CG):
                    g.dma_start(
                        out=sh_xn[0].ap()[128 * c:128 * (c + 1), t0:t0 + W],
                        in_=xn_loc[c][:, 0:W]).then_inc(s_exA.h, 16)
                g.wait_ge(s_exA.h, exA[0])
            with g.Else():
                exB[0] += CG * 16
                for c in range(CG):
                    g.dma_start(
                        out=sh_xn[1].ap()[128 * c:128 * (c + 1), t0:t0 + W],
                        in_=xn_loc[c][:, 0:W]).then_inc(s_exB.h, 16)
                g.wait_ge(s_exB.h, exB[0])
            return g.nop()
        sch.on("gpsimd", fship, waits=(xn_tok,), incs=(s_gp.inc(),))
        last = s_gp.now()

    def fxflag(g):
        pid = parity_open(g)
        with g.If_eq(pid, 0):
            exA[0] += 16
            g.dma_start(out=sh_fl.ap()[0:1, 8:9],
                        in_=flagv[:, :]).then_inc(s_exA.h, 16)
            emit_poll(g, 9, 0)
        with g.Else():
            exB[0] += 16
            g.dma_start(out=sh_fl.ap()[0:1, 9:10],
                        in_=flagv[:, :]).then_inc(s_exB.h, 16)
            emit_poll(g, 8, 1)
        return g.nop()
    if PH >= 5:
        sch.on("gpsimd", fxflag, incs=(s_gp.inc(),))
        last_done = s_gp.now()
    xn_ready = s_gp.now()

    segY.reset(markY0)
    o_fm = [segY.alloc([128, TFULL], BF16, f"ofm{p}") for p in range(4)]
    markY1 = segY.mark()
    q_fm = segY.alloc([128, TFULL], BF16, "q_fm")
    k_fm = segY.alloc([128, TFULL], BF16, "k_fm")
    v_fm = segY.alloc([128, TFULL], BF16, "v_fm")
    v_tok = segY.alloc([128, TFULL], BF16, "v_tok")
    xn_st = [segY.alloc([128, 512], BF16, f"xst{c}") for c in range(CG)]
    wq_blk = segY.alloc([128, D], BF16, "wq_blk")
    wkb_blk = segY.alloc([128, D], BF16, "wkb_blk")
    wvb_blk = segY.alloc([128, D], BF16, "wvb_blk")
    p_bf = [segY.alloc([128, 512], BF16, f"pbf{h}") for h in range(2)]
    pt_bf = [segY.alloc([128, 512], BF16, f"ptb{h}") for h in range(2)]
    mask01_s = segY.alloc([128, 4 * 512], BF16, "mask01_s")
    s_msk = wsem(("mask",))
    sch.on("gpsimd", lambda g: g.dma_start(out=mask01_s[:, :], in_=dmask01),
           waits=(x2_tok,), incs=(s_msk.inc(16),))
    mask_ld = s_msk.now()

    s_vtr = wsem(("vtr",))
    s_otr = wsem(("otr",))
    s_ptr = [wsem(("ptr", i)) for i in range(2)]
    xnst_free = None
    vt_free = None
    p_free = [None, None]
    pt_free = [None, None]
    obf_free = None
    ofm_tok = []
    wblk_free[("wq", 0)] = x2_tok
    wblk_free[("wkb", 0)] = x2_tok
    wblk_free[("wvb", 0)] = x2_tok
    for p in range(4 if PH >= 6 else 0):
        tq_tok = load_wblk(wq_blk, wqT, p, ("wq", 0))
        tk_tok = load_wblk(wkb_blk, wkTb, p, ("wkb", 0))
        tv_tok = load_wblk(wvb_blk, wvTb, p, ("wvb", 0))
        pw_toks = {"q": tq_tok, "k": tk_tok, "v": tv_tok}
        for bi, (t0, W) in enumerate(TB8):
            slot = 0 if t0 < TOK else 1
            c0 = t0 % TOK
            for c in range(CG):
                def fxn(g, c=c, slot=slot, c0=c0, W=W):
                    return g.dma_start(
                        out=xn_st[c][:, 0:W],
                        in_=sh_xn[slot].ap()[128 * c:128 * (c + 1),
                                             c0:c0 + W])
                w = (xn_ready, xnst_free) if c == 0 else ()
                sch.on("gpsimd", fxn, waits=w, incs=(s_xld.inc(16),))
            xl_tok = s_xld.now()
            for (bank, wb, dstt, kind) in ((0, wq_blk, q_fm, "q"),
                                           (1, wkb_blk, k_fm, "k"),
                                           (2, wvb_blk, v_fm, "v")):
                mm_group(bank, [(wb[:, 128 * c:128 * (c + 1)],
                                 xn_st[c][:, 0:W], pb[bank][:, 0:W])
                                for c in range(CG)],
                         waits=(pw_toks[kind], xl_tok))
                ptok = t_fence()
                if kind == "q":
                    def fe(s, bank=bank, dstt=dstt, t0=t0, W=W, p=p):
                        return s.activation(dstt[:, t0:t0 + W],
                                            pb[bank][:, 0:W], AF.Identity,
                                            bias=bq_st[p][:, 0:1],
                                            scale=0.125)
                elif kind == "k":
                    def fe(s, bank=bank, dstt=dstt, t0=t0, W=W, p=p):
                        return s.activation(dstt[:, t0:t0 + W],
                                            pb[bank][:, 0:W], AF.Identity,
                                            bias=bk_st[p][:, 0:1])
                else:
                    def fe(s, bank=bank, dstt=dstt, t0=t0, W=W, p=p):
                        return s.activation(dstt[:, t0:t0 + W],
                                            pb[bank][:, 0:W], AF.Identity,
                                            bias=bv_st[p][:, 0:1])
                sch.on("scalar", fe, waits=(ptok,), incs=(s_sc.inc(),))
                bank_free[bank] = s_sc.now()
            xnst_free = s_sc.now()
        proj_tok = s_sc.now()
        wblk_free[("wq", 0)] = proj_tok
        wblk_free[("wkb", 0)] = proj_tok
        wblk_free[("wvb", 0)] = proj_tok
        if DBG and p == 0:
            for nm, tt in (("d_qfm", q_fm), ("d_kfm", k_fm), ("d_vfm", v_fm),
                           ("d_bqst", bq_st[0]), ("d_bqsc", bq_sc)):
                a = dbg(nm, [128, tt.shape[1]],
                        BF16 if tt.dtype == BF16 else F32)
                def fdq(sy, a=a, tt=tt):
                    return sy.dma_start(out=a, in_=tt[:, :])
                sch.on("sync", fdq, waits=(proj_tok,), incs=(s_out.inc(16),))
        for tj in range(TFULL // 128):
            def ftr(s, tj=tj):
                return s.dma_start(out=v_tok[:, 128 * tj:128 * (tj + 1)],
                                   in_=v_fm[:, 128 * tj:128 * (tj + 1)],
                                   transpose=True)
            w = (proj_tok, vt_free) if tj == 0 else ()
            sch.on("sync", ftr, waits=w, incs=(s_vtr.inc(16),))
        vtr_tok = s_vtr.now()

        if PH >= 6 and PH < 7:
            last_done = vtr_tok
        for qi in range(NQT if PH >= 7 else 0):
            nblk = qi // 4 + 1
            for h2 in range(2):
                r0 = 64 * h2
                for j in range(nblk):
                    mm_group(j, [(q_fm[r0:r0 + 64, 128 * qi:128 * (qi + 1)],
                                  k_fm[r0:r0 + 64, 512 * j:512 * (j + 1)],
                                  pb[j][:, 0:512])],
                             waits=(proj_tok,) if j == 0 else ())
                sc_tok = t_fence()
                for j in range(nblk):
                    def fmx(v, h2=h2, j=j):
                        return v.reduce_max(mx8[h2][:, j:j + 1],
                                            pb[j][:, 0:512], axis=AX)
                    sch.on("vector", fmx, waits=(sc_tok,) if j == 0 else (),
                           incs=(s_vc.inc(),))
                def fcmb(v, h2=h2, nblk=nblk):
                    return v.reduce_max(mneg_t[h2][:, :],
                                        mx8[h2][:, 0:nblk], axis=AX,
                                        negate=True)
                sch.on("vector", fcmb, waits=(s_vc.now(),),
                       incs=(s_vc.inc(),))
                mstat_tok = s_vc.now()
                pv_started = False
                for j in range(nblk):
                    diag = (j == nblk - 1)
                    slot = j % 2
                    def fpexp(s, h2=h2, j=j, slot=slot):
                        return s.activation(p_bf[slot][:, 0:512],
                                            pb[j][:, 0:512], AF.Exp,
                                            bias=mneg_t[h2][:, 0:1],
                                            accum_out=lc8[h2][:, j:j + 1])
                    sch.on("scalar", fpexp,
                           waits=(mstat_tok, p_free[slot]),
                           incs=(s_sc.inc(),))
                    pexp_tok = s_sc.now()
                    bank_free[j] = pexp_tok
                    if diag:
                        mv = qi % 4
                        def fpm(v, slot=slot, mv=mv):
                            return v.tensor_mul(
                                p_bf[slot][:, 0:512], p_bf[slot][:, 0:512],
                                mask01_s[:, 512 * mv:512 * (mv + 1)])
                        sch.on("vector", fpm, waits=(pexp_tok, mask_ld),
                               incs=(s_vc.inc(),))
                        def fls(v, h2=h2, j=j, slot=slot):
                            return v.reduce_sum(lc8[h2][:, j:j + 1],
                                                p_bf[slot][:, 0:512], axis=AX)
                        sch.on("vector", fls, waits=(s_vc.now(),),
                               incs=(s_vc.inc(),))
                        pexp_tok = s_vc.now()
                    for u in range(4):
                        def ftp(sy, slot=slot, u=u):
                            return sy.dma_start(
                                out=pt_bf[slot][:, 128 * u:128 * (u + 1)],
                                in_=p_bf[slot][:, 128 * u:128 * (u + 1)],
                                transpose=True)
                        w = (pexp_tok, pt_free[slot]) if u == 0 else ()
                        sch.on("sync", ftp, waits=w,
                               incs=(s_ptr[slot].inc(16),))
                    p_free[slot] = s_ptr[slot].now()
                    ptr_tok = s_ptr[slot].now()
                    mms = [(pt_bf[slot][:, 128 * u:128 * (u + 1)],
                            v_tok[:, 128 * (4 * j + u) + r0:
                                  128 * (4 * j + u) + r0 + 64],
                            pb[0][:, 64 * h2:64 * h2 + 64]) for u in range(4)]
                    w = [ptr_tok, vtr_tok]
                    if j == 0:
                        w.append(pexp_tok if not diag else pexp_tok)
                    n = len(mms)
                    for i, (lhsT, rhs, outp) in enumerate(mms):
                        def fpv(t, lhsT=lhsT, rhs=rhs, outp=outp,
                                st=(j == 0 and i == 0),
                                sp=(j == nblk - 1 and i == n - 1)):
                            return t.matmul(outp, lhsT, rhs, start=st, stop=sp)
                        sch.on("tensor", fpv, waits=w if i == 0 else ())
                pv_tok = t_fence()
                pt_free[0] = pv_tok
                pt_free[1] = pv_tok
                def flsum(s, h2=h2, nblk=nblk):
                    return s.activation(lsc[h2][:, 0:nblk],
                                        lc8[h2][:, 0:nblk], AF.Identity,
                                        accum_out=lsum[h2][:, :])
                sch.on("scalar", flsum, waits=(pv_tok,), incs=(s_sc.inc(),))
                lsum_tok = s_sc.now()
                def frl(v, h2=h2):
                    return v.reciprocal(rl_t[h2][:, :], lsum[h2][:, :])
                sch.on("vector", frl, waits=(lsum_tok,), incs=(s_vc.inc(),))
                rl_tok = s_vc.now()
                def fOev(s, h2=h2):
                    return s.activation(o_bf[:, 64 * h2:64 * h2 + 64],
                                        pb[0][:, 64 * h2:64 * h2 + 64],
                                        AF.Identity, scale=rl_t[h2][:, 0:1])
                sch.on("scalar", fOev, waits=(rl_tok, obf_free),
                       incs=(s_sc.inc(),))
                bank_free[0] = s_sc.now()
            fin_tok = s_sc.now()
            def ftro(s, p=p, qi=qi):
                return s.dma_start(out=o_fm[p][:, 128 * qi:128 * (qi + 1)],
                                   in_=o_bf[:, :], transpose=True)
            sch.on("sync", ftro, waits=(fin_tok,), incs=(s_otr.inc(16),))
            obf_free = s_otr.now()
        vt_free = s_tn.now()
        if DBG and p == 0 and PH >= 7:
            for nm, tt in (("d_vtok", v_tok), ("d_ofm0", o_fm[0])):
                a = dbg(nm, [128, tt.shape[1]], BF16)
                def fdv(sy, a=a, tt=tt):
                    return sy.dma_start(out=a, in_=tt[:, :])
                sch.on("sync", fdv, waits=(s_otr.now(),),
                       incs=(s_out.inc(16),))
        ofm_tok.append(s_otr.now())
        if PH >= 7:
            last_done = s_otr.now()

    segY.reset(markY1)
    o_full = [segY.alloc([128, TOK], BF16, f"ofl{g}") for g in range(CG)]
    wob_blk = [segY.alloc([128, D], BF16, f"wob{i}") for i in range(2)]
    xnm_st = [segY.alloc([128, 512], BF16, f"xnm{c}") for c in range(CG)]
    osum = segY.alloc([128, 512], F32, "osum")

    flash_done = s_tn.now()

    def foship(g):
        pid = parity_open(g)
        with g.If_eq(pid, 0):
            exA[0] += 4 * 16
            for p in range(4):
                g.dma_start(out=sh_O[0].ap()[128 * p:128 * (p + 1), :],
                            in_=o_fm[p][:, :]).then_inc(s_exA.h, 16)
            g.wait_ge(s_exA.h, exA[0])
            exA[0] += 16
            g.dma_start(out=sh_fl.ap()[0:1, 10:11],
                        in_=flagv[:, :]).then_inc(s_exA.h, 16)
            emit_poll(g, 11, 0)
            exA[0] += CG * 16
            for gg in range(CG):
                g.dma_start(
                    out=o_full[gg][:, :],
                    in_=sh_O[gg // 4].ap()[128 * (gg % 4):128 * (gg % 4 + 1),
                                           0:TOK]).then_inc(s_exA.h, 16)
            g.wait_ge(s_exA.h, exA[0])
        with g.Else():
            exB[0] += 4 * 16
            for p in range(4):
                g.dma_start(out=sh_O[1].ap()[128 * p:128 * (p + 1), :],
                            in_=o_fm[p][:, :]).then_inc(s_exB.h, 16)
            g.wait_ge(s_exB.h, exB[0])
            exB[0] += 16
            g.dma_start(out=sh_fl.ap()[0:1, 11:12],
                        in_=flagv[:, :]).then_inc(s_exB.h, 16)
            emit_poll(g, 10, 1)
            exB[0] += CG * 16
            for gg in range(CG):
                g.dma_start(
                    out=o_full[gg][:, :],
                    in_=sh_O[gg // 4].ap()[128 * (gg % 4):128 * (gg % 4 + 1),
                                           TOK:TFULL]).then_inc(s_exB.h, 16)
            g.wait_ge(s_exB.h, exB[0])
        return g.nop()
    if PH >= 8:
        sch.on("gpsimd", foship, waits=tuple(ofm_tok) + (flash_done,),
               incs=(s_gp.inc(),))
    o_ready = s_gp.now()

    wblk_free[("wob", 0)] = flash_done
    wblk_free[("wob", 1)] = flash_done
    xnm_free = None
    for bi, (t0, W) in enumerate(TB4 if PH >= 8 else []):
        def fxnm(g, t0=t0, W=W):
            pid = parity_open(g)
            with g.If_eq(pid, 0):
                exA[0] += CG * 16
                for c in range(CG):
                    g.dma_start(
                        out=xnm_st[c][:, 0:W],
                        in_=sh_xn[0].ap()[128 * c:128 * (c + 1),
                                          t0:t0 + W]).then_inc(s_exA.h, 16)
                g.wait_ge(s_exA.h, exA[0])
            with g.Else():
                exB[0] += CG * 16
                for c in range(CG):
                    g.dma_start(
                        out=xnm_st[c][:, 0:W],
                        in_=sh_xn[1].ap()[128 * c:128 * (c + 1),
                                          t0:t0 + W]).then_inc(s_exB.h, 16)
                g.wait_ge(s_exB.h, exB[0])
            return g.nop()
        sch.on("gpsimd", fxnm, waits=(o_ready, xnm_free), incs=(s_gp.inc(),))
        xnm_tok = s_gp.now()
        for cg in range(CG):
            buf = cg % 2
            wtok = load_wblk(wob_blk[buf], woTb, cg, ("wob", buf))
            bank = cg % 4
            mm_group(bank, [(wob_blk[buf][:, 128 * dd:128 * (dd + 1)],
                             o_full[dd][:, t0:t0 + W], pb[bank][:, 0:W])
                            for dd in range(CG)], waits=(wtok, o_ready))
            ptok = t_fence()
            wblk_free[("wob", buf)] = ptok
            def fbo(s, bank=bank, cg=cg, W=W):
                return s.activation(osum[:, 0:W], pb[bank][:, 0:W], AF.Identity,
                                    bias=bo_st[cg][:, 0:1])
            sch.on("scalar", fbo, waits=(ptok, s_vc.now()),
                       incs=(s_sc.inc(),))
            bo_tok = s_sc.now()
            bank_free[bank] = bo_tok
            def fr1(v, cg=cg, W=W):
                return v.tensor_add(osum[:, 0:W], osum[:, 0:W],
                                    xnm_st[cg][:, 0:W])
            def fr2(v, cg=cg, t0=t0, W=W):
                return v.tensor_add(x1[cg][:, t0:t0 + W],
                                    x1[cg][:, t0:t0 + W], osum[:, 0:W])
            sch.on("vector", fr1, waits=(bo_tok, xnm_tok),
                   incs=(s_vc.inc(),))
            sch.on("vector", fr2, waits=(s_vc.now(),), incs=(s_vc.inc(),))
        xnm_free = s_vc.now()
    x3_tok = s_vc.now()
    if PH >= 8:
        last_done = x3_tok

    if PH >= 9:
        x4_tok = emit_ffn(x3_tok, "f2")
        last_done = x4_tok
    else:
        x4_tok = last_done

    for cg in range(CG):
        def fo2(s, cg=cg):
            return s.dma_start(out=out_fm[128 * cg:128 * (cg + 1), :],
                               in_=x1[cg][:, :])
        sch.on("sync", fo2, waits=(last_done,) if cg == 0 else (),
               incs=(s_out.inc(16),))
    sch.on("sync", lambda s: s.nop(), waits=(s_out.now(),))

    with nc.Block() as block:
        block.tensor(sch.runner("tensor"))
        block.vector(sch.runner("vector"))
        block.scalar(sch.runner("scalar"))
        block.gpsimd(sch.runner("gpsimd"))
        block.sync(sch.runner("sync"))

    return nc


def _blk(wt, og):
    i, o = wt.shape
    c = i // 128
    a = wt.reshape(i, og, 128).transpose(1, 0, 2)
    a = a.reshape(og, c, 128, 128).transpose(0, 2, 1, 3)
    return np.ascontiguousarray(a.reshape(og, 128, i))


def _col(v):
    return np.ascontiguousarray(v.reshape(-1, 128).T)


def kernel(**inp):
    x = np.asarray(inp["L0_attn_a_x"], np.float32)
    g = lambda n: np.asarray(inp[n], np.float32)

    base = dict(
        wkT=_blk(g("L0_attn_a_Wk").T, CG),
        wvT=_blk(g("L0_attn_a_Wv").T, CG),
        wrT=_blk(g("L0_attn_a_Wr").T, CG),
        woTa=_blk(g("L0_attn_a_Wo").T, CG),
        veca=np.concatenate([_col(g("L0_attn_a_time_decay")),
                             _col(g("L0_attn_a_time_first")),
                             _col(g("L0_attn_a_time_mix_k")),
                             _col(g("L0_attn_a_time_mix_v")),
                             _col(g("L0_attn_a_time_mix_r"))], axis=1),
        w1T=_blk(g("L0_ffn_w1_w").T, HG),
        w2T=_blk(g("L0_ffn_w2_w").T, HG),
        h1T=_blk(g("L0_ffn_h1_w").T, HG),
        h2T=_blk(g("L0_ffn_h2_w").T, HG),
        w3T=_blk(g("L0_ffn_w3_w").T, CG),
        ffs=np.concatenate([_col(g("L0_ffn_w1_s")), _col(g("L0_ffn_w2_s")),
                            _col(g("L0_ffn_h1_s")), _col(g("L0_ffn_h2_s")),
                            _col(g("L0_ffn_w3_s"))], axis=1),
        woTb=_blk(g("L0_attn_b_Wo").T, CG),
    )
    dmask = np.zeros((128, 4 * 512), np.float32)
    for mv in range(4):
        keep = np.arange(512)[None, :] <= (128 * mv + np.arange(128)[:, None])
        dmask[:, 512 * mv:512 * (mv + 1)] = np.where(keep, 0.0, -1e30)
    base["dmask"] = dmask
    base["dmask01"] = (dmask == 0.0).astype(np.float32)
    base["czero"] = np.zeros((128, 2), np.float32)
    base["nonce"] = np.array([[np.int64(np.random.randint(1, 2**31 - 1))]],
                             dtype=np.int32)

    in_maps = []
    for b in range(4):
        for hh in range(2):
            m = dict(base)
            t0 = TOK * hh
            xin = np.empty((D, TOK + 1), np.float32)
            if hh == 0:
                xin[:, 0] = 0.0
            else:
                xin[:, 0] = x[b, TOK - 1, :]
            xin[:, 1:] = x[b, t0:t0 + TOK, :].T
            m["xin"] = np.ascontiguousarray(xin)
            hs = slice(512 * hh, 512 * (hh + 1))
            m["wqT"] = _blk(g("L0_attn_b_Wq")[hs, :].T, 4)
            m["wkTb"] = _blk(g("L0_attn_b_Wk")[hs, :].T, 4)
            m["wvTb"] = _blk(g("L0_attn_b_Wv")[hs, :].T, 4)
            m["vecb"] = np.concatenate(
                [_col(g("L0_attn_b_bq")[hs]), _col(g("L0_attn_b_bk")[hs]),
                 _col(g("L0_attn_b_bv")[hs]), _col(g("L0_attn_b_norm_w")),
                 _col(g("L0_attn_b_bo"))], axis=1)
            in_maps.append(m)

    nc = build_program()
    res = run_bass_kernel_spmd(nc, in_maps, core_ids=list(range(8)))
    global LAST_RES
    LAST_RES = res

    out = np.empty((4, TFULL, D), np.float32)
    for b in range(4):
        for hh in range(2):
            out[b, TOK * hh:TOK * (hh + 1), :] = \
                res.results[2 * b + hh]["out_fm"].T
    return out


# version 10 (final)
# speedup vs baseline: 1.1382x; 1.0289x over previous
import numpy as np
import concourse.bass as bass
import concourse.mybir as mybir
from concourse.bass_utils import run_bass_kernel_spmd

F32 = mybir.dt.float32
BF16 = mybir.dt.bfloat16
I32 = mybir.dt.int32
AF = mybir.ActivationFunctionType
ALU = mybir.AluOpType
AX = mybir.AxisListType.X

D = 1024
HID = 4096
TOK = 2048
TFULL = 4096
CG = 8
HG = 32
NQT = 32
FLAG = 77777
POLL_CAP = 50000000
LN_EPS = 1e-5
RMS_EPS = 1e-6
TB4 = [(0, 512), (512, 512), (1024, 512), (1536, 512)]
TB8 = [(512 * i, 512) for i in range(8)]

PHASES = 99
DBG = False
LAST_RES = None

ENGS = ("tensor", "vector", "scalar", "gpsimd", "sync")


class Sem:
    def __init__(self, nc, name):
        self.h = nc.alloc_semaphore(name)
        self.n = 0

    def inc(self, by=1):
        self.n += by
        return (self.h, by)

    def now(self):
        return (self.h, self.n)


class Sched:
    def __init__(self):
        self.ops = {e: [] for e in ENGS}

    def on(self, eng, fn, waits=(), incs=()):
        waits = tuple(w for w in waits if w is not None)
        self.ops[eng].append((waits, fn, tuple(incs)))

    def runner(self, name):
        ops = self.ops[name]

        def f(e):
            for waits, fn, incs in ops:
                for (h, v) in waits:
                    e.wait_ge(h, v)
                inst = fn(e)
                for (h, by) in incs:
                    inst = inst.then_inc(h, by)

        return f


class SegAlloc:
    def __init__(self, nc, base, size, tag):
        self.nc = nc
        self.base = base
        self.size = size
        self.ptr = base
        self.tag = tag
        self.uid = 0

    def alloc(self, shape, dt, name=None):
        per = int(np.prod(shape[1:])) * (2 if dt == BF16 else 4)
        per = (per + 31) & ~31
        assert self.ptr + per <= self.base + self.size, (
            f"seg {self.tag} overflow: {name} needs {per}B at "
            f"{self.ptr - self.base}, cap {self.size}")
        off = self.ptr
        self.ptr = off + per
        self.uid += 1
        return self.nc.alloc_sbuf_tensor_at(
            name or f"{self.tag}{self.uid}", shape, dt, offset=off)

    def mark(self):
        return self.ptr

    def reset(self, m=None):
        self.ptr = self.base if m is None else m


def build_program():
    PH = PHASES
    dbg_outs = {}

    def dbg(name, shape, dt=F32):
        dbg_outs[name] = nc.dram_tensor(name, shape, dt,
                                        kind="ExternalOutput").ap()
        return dbg_outs[name]
    nc = bass.Bass("TRN2", target_bir_lowering=False)

    def din(name, shape, dt=F32):
        return nc.dram_tensor(name, shape, dt, kind="ExternalInput").ap()

    xin = din("xin", [D, TOK + 1])
    wkT = din("wkT", [CG, 128, D])
    wvT = din("wvT", [CG, 128, D])
    wrT = din("wrT", [CG, 128, D])
    woTa = din("woTa", [CG, 128, D])
    veca = din("veca", [128, 40])
    w1T = din("w1T", [HG, 128, D])
    w2T = din("w2T", [HG, 128, D])
    h1T = din("h1T", [HG, 128, HID])
    h2T = din("h2T", [HG, 128, HID])
    w3T = din("w3T", [CG, 128, HID])
    ffs = din("ffs", [128, 4 * HG + CG])
    wqT = din("wqT", [4, 128, D])
    wkTb = din("wkTb", [4, 128, D])
    wvTb = din("wvTb", [4, 128, D])
    woTb = din("woTb", [CG, 128, D])
    vecb = din("vecb", [128, 28])
    dmask = din("dmask", [128, 4 * 512])
    dmask01 = din("dmask01", [128, 4 * 512])
    nonce = din("nonce", [1, 1], I32)
    czero = din("czero", [128, 2])
    out_fm = nc.dram_tensor("out_fm", [D, TOK], F32, kind="ExternalOutput").ap()

    def shared(name, shape, dt):
        return nc.dram_tensor(name, shape, dt, kind="Internal",
                              addr_space="Shared")

    sh_carry = [shared(f"sh_carry{d}", [128, 2], F32) for d in range(CG)]
    sh_xn = [shared(f"sh_xn{s}", [D, TOK], BF16) for s in range(2)]
    sh_O = [shared(f"sh_O{s}", [512, TFULL], BF16) for s in range(2)]
    sh_fl = shared("sh_fl", [1, 16], I32)

    s_wld = Sem(nc, "s_wld")
    s_xld = Sem(nc, "s_xld")
    s_tn = Sem(nc, "s_tn")
    s_vc = Sem(nc, "s_vc")
    s_sc = Sem(nc, "s_sc")
    s_gp = Sem(nc, "s_gp")
    s_tr = Sem(nc, "s_tr")
    s_out = Sem(nc, "s_out")
    s_junk = Sem(nc, "s_junk")
    s_exA = Sem(nc, "s_exA")
    s_exB = Sem(nc, "s_exB")
    exA = [0]
    exB = [0]

    vec_a = nc.alloc_sbuf_tensor("vec_a", [128, 40], F32)
    mix1m = nc.alloc_sbuf_tensor("mix1m", [128, 24], F32)
    ffs_s = nc.alloc_sbuf_tensor("ffs_s", [128, 4 * HG + CG], F32)
    vec_b = nc.alloc_sbuf_tensor("vec_b", [128, 28], F32)
    bq_sc = nc.alloc_sbuf_tensor("bq_sc", [128, 4], F32)
    bq_st = [nc.alloc_sbuf_tensor(f"bq_st{p}", [128, 1], F32) for p in range(4)]
    bk_st = [nc.alloc_sbuf_tensor(f"bk_st{p}", [128, 1], F32) for p in range(4)]
    bv_st = [nc.alloc_sbuf_tensor(f"bv_st{p}", [128, 1], F32) for p in range(4)]
    bo_st = [nc.alloc_sbuf_tensor(f"bo_st{c}", [128, 1], F32) for c in range(CG)]
    ew_d = nc.alloc_sbuf_tensor("ew_d", [128, 3 * CG], F32)
    ew_w = [nc.alloc_sbuf_tensor(f"ew_w{d}", [128, 1], F32) for d in range(CG)]
    ones_col = nc.alloc_sbuf_tensor("ones_col", [128, 1], F32)
    ones_row = nc.alloc_sbuf_tensor("ones_row", [1, 128], F32)
    flagv = nc.alloc_sbuf_tensor("flagv", [1, 1], I32)
    flagr = nc.alloc_sbuf_tensor("flagr", [1, 1], I32)
    eps1 = nc.alloc_sbuf_tensor("eps1", [1, 1], F32)
    eps2 = nc.alloc_sbuf_tensor("eps2", [1, 1], F32)
    srow = nc.alloc_sbuf_tensor("srow", [1, 512], F32)
    mrow = nc.alloc_sbuf_tensor("mrow", [1, 512], F32)
    vrow = nc.alloc_sbuf_tensor("vrow", [1, 512], F32)
    brow = nc.alloc_sbuf_tensor("brow", [1, 512], F32)
    sq_t = [nc.alloc_sbuf_tensor(f"sq{i}", [128, 512], F32) for i in range(2)]
    carry_sb = nc.alloc_sbuf_tensor("carry_sb", [128, 2 * CG], F32)
    carry_out = nc.alloc_sbuf_tensor("carry_out", [128, 2 * CG], F32)
    o_acc = nc.alloc_sbuf_tensor("o_acc", [128, 128], F32)
    o_bf = nc.alloc_sbuf_tensor("o_bf", [128, 128], BF16)
    mneg_t = [nc.alloc_sbuf_tensor(f"mneg{h}", [128, 1], F32) for h in range(2)]
    mx8 = [nc.alloc_sbuf_tensor(f"mx8_{h}", [128, 8], F32) for h in range(2)]
    lc8 = [nc.alloc_sbuf_tensor(f"lc8_{h}", [128, 8], F32) for h in range(2)]
    lsum = [nc.alloc_sbuf_tensor(f"lsum{h}", [128, 1], F32) for h in range(2)]
    lsc0 = nc.alloc_sbuf_tensor("lsc0", [128, 8], F32)
    lsc = [lsc0, lsc0]
    rl_t = [nc.alloc_sbuf_tensor(f"rl{h}", [128, 1], F32) for h in range(2)]

    SEGX_SZ = 64 * 1024
    base0 = (nc.sbuf_base + 31) & ~31
    avail = nc.sbuf_top - base0 - 64
    segX = SegAlloc(nc, base0, SEGX_SZ, "sx")
    segY = SegAlloc(nc, base0 + SEGX_SZ, avail - SEGX_SZ, "sy")

    pb = [nc.alloc_psum_tensor(f"pb{i}", [128, 512], F32) for i in range(8)]
    bank_free = {i: None for i in range(8)}

    sch = Sched()

    def mm_group(bank, mms, waits=()):
        w = [x for x in waits if x is not None]
        if bank_free[bank] is not None:
            w.append(bank_free[bank])
        n = len(mms)
        for i, (lhsT, rhs, out) in enumerate(mms):
            def f(t, lhsT=lhsT, rhs=rhs, out=out, st=(i == 0), sp=(i == n - 1)):
                return t.matmul(out, lhsT, rhs, start=st, stop=sp)
            sch.on("tensor", f, waits=w if i == 0 else ())

    def t_fence():
        sch.on("tensor", lambda t: (t.drain(), t.nop())[1], incs=(s_tn.inc(),))
        return s_tn.now()

    sch.on("gpsimd", lambda g: g.dma_start(out=vec_a[:, :], in_=veca),
           incs=(s_xld.inc(16),))
    sch.on("gpsimd", lambda g: g.dma_start(out=ffs_s[:, :], in_=ffs),
           incs=(s_xld.inc(16),))
    sch.on("gpsimd", lambda g: g.dma_start(out=vec_b[:, :], in_=vecb),
           incs=(s_xld.inc(16),))
    sch.on("gpsimd", lambda g: g.dma_start(out=flagv[:, :], in_=nonce),
           incs=(s_xld.inc(16),))
    setup_ld = s_xld.now()
    regs = {}

    def fsetup_regs(g):
        for r in ("nonce", "pid", "fv", "cnt", "t1", "t2"):
            regs[r] = g.alloc_register("R" + r)
        g.reg_load(regs["nonce"], flagv[0:1, 0:1])
        g.reg_mov(regs["pid"], g.partition_id())
        g.reg_mod(regs["pid"], regs["pid"], 2)
        return g.nop()
    sch.on("gpsimd", fsetup_regs, waits=(setup_ld,))

    sch.on("vector", lambda v: v.memset(ones_col[:, :], 1.0),
           incs=(s_vc.inc(),))
    sch.on("vector", lambda v: v.memset(eps1[:, :], LN_EPS),
           incs=(s_vc.inc(),))
    sch.on("vector", lambda v: v.memset(eps2[:, :], RMS_EPS),
           incs=(s_vc.inc(),))
    sch.on("vector", lambda v: v.memset(ones_row[:, :], 1.0),
           incs=(s_vc.inc(),))
    sch.on("vector", lambda v: v.memset(mx8[0][:, :], 0.0),
           incs=(s_vc.inc(),))
    sch.on("vector", lambda v: v.memset(mx8[1][:, :], 0.0),
           incs=(s_vc.inc(),))
    sch.on("vector",
           lambda v: v.tensor_scalar(mix1m[:, 0:24], vec_a[:, 16:40], -1.0, 1.0,
                                     ALU.mult, ALU.add),
           waits=(setup_ld,), incs=(s_vc.inc(),))
    for p in range(4):
        sch.on("vector", lambda v, p=p: v.tensor_scalar_mul(
            bq_st[p][:, :], vec_b[:, p:p + 1], 0.125), incs=(s_vc.inc(),))
        sch.on("vector", lambda v, p=p: v.tensor_copy(bk_st[p][:, :],
                                                      vec_b[:, 4 + p:5 + p]),
               incs=(s_vc.inc(),))
        sch.on("vector", lambda v, p=p: v.tensor_copy(bv_st[p][:, :],
                                                      vec_b[:, 8 + p:9 + p]),
               incs=(s_vc.inc(),))
    for c in range(CG):
        sch.on("vector", lambda v, c=c: v.tensor_copy(bo_st[c][:, :],
                                                      vec_b[:, 20 + c:21 + c]),
               incs=(s_vc.inc(),))
    setup_done = s_vc.now()

    def emit_ln_block(x_tiles, col0, W, out_tiles, ocol0, src_ready=()):
        mm_group(6, [(ones_col[:, 0:1], x_tiles[c][:, col0:col0 + W],
                      pb[6][0:1, 0:W]) for c in range(CG)], waits=src_ready)
        sq_guard = {}
        for c in range(CG):
            w = list(src_ready)
            if c >= 2:
                w.append((s_tn.h, sq_guard[c - 2]))
            def fsq(s, c=c, col0=col0, W=W):
                return s.activation(sq_t[c % 2][:, 0:W],
                                    x_tiles[c][:, col0:col0 + W], AF.Square)
            sch.on("scalar", fsq, waits=w, incs=(s_sc.inc(),))
            sqv = s_sc.now()
            bw = [sqv]
            if c == 0 and bank_free[7] is not None:
                bw.append(bank_free[7])
            def fm(t, c=c, W=W):
                return t.matmul(pb[7][0:1, 0:W], ones_col[:, 0:1],
                                sq_t[c % 2][:, 0:W], start=(c == 0),
                                stop=(c == CG - 1))
            sch.on("tensor", fm, waits=bw)
            sch.on("tensor", lambda t: (t.drain(), t.nop())[1],
                   incs=(s_tn.inc(),))
            sq_guard[c] = s_tn.n
        stats_tok = (s_tn.h, s_tn.n)
        sch.on("vector", lambda v, W=W: v.tensor_scalar_mul(
            mrow[0:1, 0:W], pb[6][0:1, 0:W], 1.0 / D), waits=(stats_tok,),
            incs=(s_vc.inc(),))
        sch.on("vector", lambda v, W=W: v.tensor_mul(
            vrow[0:1, 0:W], mrow[0:1, 0:W], mrow[0:1, 0:W]),
            waits=(s_vc.now(),), incs=(s_vc.inc(),))
        sch.on("vector", lambda v, W=W: v.scalar_tensor_tensor(
            vrow[0:1, 0:W], pb[7][0:1, 0:W], 1.0 / D, vrow[0:1, 0:W],
            ALU.mult, ALU.subtract), waits=(s_vc.now(),),
            incs=(s_vc.inc(),))
        var_tok = s_vc.now()
        sch.on("scalar", lambda s, W=W: s.activation(
            srow[0:1, 0:W], vrow[0:1, 0:W], AF.Sqrt, bias=eps1[0:1, 0:1]),
            waits=(var_tok,), incs=(s_sc.inc(),))
        sch.on("vector", lambda v, W=W: v.reciprocal(vrow[0:1, 0:W],
                                                     srow[0:1, 0:W]),
               waits=(s_sc.now(),), incs=(s_vc.inc(),))
        sch.on("vector", lambda v, W=W: v.scalar_tensor_tensor(
            brow[0:1, 0:W], mrow[0:1, 0:W], -1.0, vrow[0:1, 0:W],
            ALU.mult, ALU.mult), waits=(s_vc.now(),), incs=(s_vc.inc(),))
        rows_tok = s_vc.now()
        bank_free[6] = rows_tok
        bank_free[7] = rows_tok
        mm_group(6, [(ones_row[0:1, :], vrow[0:1, 0:W], pb[6][:, 0:W])],
                 waits=(rows_tok,))
        mm_group(7, [(ones_row[0:1, :], brow[0:1, 0:W], pb[7][:, 0:W])])
        bc_tok = t_fence()
        for c in range(CG):
            def fa(v, c=c, col0=col0, W=W):
                return v.tensor_mul(sq_t[c % 2][:, 0:W],
                                    x_tiles[c][:, col0:col0 + W], pb[6][:, 0:W])
            def fb(v, c=c, ocol0=ocol0, W=W):
                return v.tensor_add(out_tiles[c][:, ocol0:ocol0 + W],
                                    sq_t[c % 2][:, 0:W], pb[7][:, 0:W])
            sch.on("vector", fa,
                   waits=(bc_tok,) if c == 0 else (s_vc.now(),),
                   incs=(s_vc.inc(),))
            sch.on("vector", fb, waits=(s_vc.now(),), incs=(s_vc.inc(),))
        apply_tok = s_vc.now()
        bank_free[6] = apply_tok
        bank_free[7] = apply_tok
        return apply_tok

    xk_b = [segY.alloc([128, TOK], BF16, f"xk{c}") for c in range(CG)]
    xv_b = [segY.alloc([128, TOK], BF16, f"xv{c}") for c in range(CG)]
    xr_b = [segY.alloc([128, TOK], BF16, f"xr{c}") for c in range(CG)]
    rwkv_b = [segY.alloc([128, TOK], BF16, f"rw{c}") for c in range(CG)]
    ln_bf = [segX.alloc([128, TOK + 1], BF16, f"ln{c}") for c in range(CG)]
    x_blk = [segX.alloc([128, 512], F32, f"xblk{c}") for c in range(CG)]
    mixtmp = segX.alloc([128, 512], F32, "mixtmp")

    LN1_BLOCKS = [(0, 512), (512, 512), (1024, 512), (1536, 512), (1537, 512)]
    prev_apply = setup_done
    for (col0, W) in LN1_BLOCKS:
        for c in range(CG):
            def fl(g, c=c, col0=col0, W=W):
                return g.dma_start(out=x_blk[c][:, 0:W],
                                   in_=xin[128 * c:128 * (c + 1),
                                           col0:col0 + W])
            sch.on("gpsimd", fl, waits=(prev_apply,) if c == 0 else (),
                   incs=(s_xld.inc(16),))
        ld_tok = s_xld.now()
        prev_apply = emit_ln_block(x_blk, 0, W, ln_bf, col0,
                                   src_ready=(ld_tok,) if col0 else
                                   (ld_tok, setup_done))
    ln_done = prev_apply

    for (mi, out_t, tmcol, omcol) in ((0, xk_b, 16, 0), (1, xv_b, 24, 8),
                                      (2, xr_b, 32, 16)):
        for c in range(CG):
            for (t0, W) in TB4:
                def f1(v, c=c, t0=t0, W=W, tmcol=tmcol):
                    return v.tensor_scalar_mul(
                        mixtmp[:, 0:W], ln_bf[c][:, 1 + t0:1 + t0 + W],
                        vec_a[:, tmcol + c:tmcol + c + 1])
                def f2(v, c=c, t0=t0, W=W, omcol=omcol, out_t=out_t):
                    return v.scalar_tensor_tensor(
                        out_t[c][:, t0:t0 + W], ln_bf[c][:, t0:t0 + W],
                        mix1m[:, omcol + c:omcol + c + 1], mixtmp[:, 0:W],
                        ALU.mult, ALU.add)
                first = (mi == 0 and c == 0 and t0 == 0)
                sch.on("vector", f1,
                       waits=(ln_done,) if first else (s_vc.now(),),
                       incs=(s_vc.inc(),))
                sch.on("vector", f2, waits=(s_vc.now(),),
                       incs=(s_vc.inc(),))
    mix_done = s_vc.now()
    if DBG:
        d_ln0 = dbg("d_ln0", [128, TOK + 1], BF16)
        d_xk0 = dbg("d_xk0", [128, TOK], BF16)
        d_xr0 = dbg("d_xr0", [128, TOK], BF16)
        sch.on("sync", lambda sy: sy.dma_start(out=d_ln0, in_=ln_bf[0][:, :]),
               waits=(mix_done,), incs=(s_out.inc(16),))
        sch.on("sync", lambda sy: sy.dma_start(out=d_xk0, in_=xk_b[0][:, :]),
               incs=(s_out.inc(16),))
        sch.on("sync", lambda sy: sy.dma_start(out=d_xr0, in_=xr_b[0][:, :]),
               incs=(s_out.inc(16),))

    last_done = mix_done
    segX.reset()
    sr_d = segX.alloc([128, TOK], F32, "sr_d")
    ek_d = segX.alloc([128, TOK + 1], F32, "ek_d")
    ekv_d = segX.alloc([128, TOK + 1], F32, "ekv_d")
    aa_d = segX.alloc([128, TOK + 1], F32, "aa_d")
    bb_d = segX.alloc([128, TOK + 1], F32, "bb_d")
    ew_big = segX.alloc([128, TOK + 1], F32, "ew_big")
    wk_blk = [segX.alloc([128, D], BF16, f"wkb{i}") for i in range(2)]
    wv_blk = [segX.alloc([128, D], BF16, f"wvb{i}") for i in range(2)]
    wr_blk = [segX.alloc([128, D], BF16, f"wrb{i}") for i in range(2)]

    wblk_free = {}

    wsems = {}

    def wsem(key):
        if key not in wsems:
            wsems[key] = Sem(nc, "sw_" + "_".join(str(k) for k in key))
        return wsems[key]

    def load_wblk(dst, src_ap, idx, free_key, groups=CG):
        sem = wsem(free_key)
        def fl(g, dst=dst, src_ap=src_ap, idx=idx, groups=groups):
            return g.dma_start(out=dst[:, 0:groups * 128],
                               in_=src_ap[idx])
        sch.on("gpsimd", fl, waits=(wblk_free.get(free_key),),
               incs=(sem.inc(16),))
        return sem.now()

    def parity_open(g):
        return regs["pid"]

    def emit_poll(g, col, arm):
        sem, cntr = (s_exA, exA) if arm == 0 else (s_exB, exB)
        cntr[0] += 16
        g.dma_start(out=flagr[:, :],
                    in_=sh_fl.ap()[0:1, col:col + 1]).then_inc(sem.h, 16)
        g.wait_ge(sem.h, cntr[0])
        g.reg_mov(regs["cnt"], 0)

        def poll():
            g.dma_start(out=flagr[:, :],
                        in_=sh_fl.ap()[0:1, col:col + 1]).then_inc(s_junk.h, 16)
            g.drain()
            g.reg_add(regs["cnt"], regs["cnt"], 1)
            g.reg_load(regs["fv"], flagr[0:1, 0:1])
            g.reg_alu(regs["t1"], regs["fv"], regs["nonce"], ALU.not_equal)
            g.reg_alu(regs["t2"], regs["cnt"], POLL_CAP, ALU.is_lt)
            g.reg_alu(regs["t1"], regs["t1"], regs["t2"], ALU.logical_and)
            return regs["t1"]

        g.reg_load(regs["fv"], flagr[0:1, 0:1])
        g.reg_alu(regs["t1"], regs["fv"], regs["nonce"], ALU.not_equal)
        with g.If_ne(regs["t1"], 0):
            with g.While(poll):
                g.nop()

    for k in (("r", 0), ("r", 1), ("k", 0), ("k", 1), ("v", 0), ("v", 1)):
        wblk_free[k] = mix_done

    prev_y_tok = None
    for d in range(CG if PH >= 2 else 0):
        buf = d % 2
        wr_tok = load_wblk(wr_blk[buf], wrT, d, ("r", buf))
        wk_tok = load_wblk(wk_blk[buf], wkT, d, ("k", buf))
        wv_tok = load_wblk(wv_blk[buf], wvT, d, ("v", buf))
        for bi, (t0, W) in enumerate(TB4):
            bank = bi % 2
            mm_group(bank, [(wr_blk[buf][:, 128 * c:128 * (c + 1)],
                             xr_b[c][:, t0:t0 + W], pb[bank][:, 0:W])
                            for c in range(CG)], waits=(wr_tok, mix_done))
            ptok = t_fence()
            def fsg(s, bank=bank, t0=t0, W=W):
                return s.activation(sr_d[:, t0:t0 + W], pb[bank][:, 0:W],
                                    AF.Sigmoid)
            sch.on("scalar", fsg, waits=(ptok, prev_y_tok),
                   incs=(s_sc.inc(),))
            bank_free[bank] = s_sc.now()
        sr_tok = s_sc.now()
        for bi, (t0, W) in enumerate(TB4):
            bank = 2 + bi % 2
            mm_group(bank, [(wk_blk[buf][:, 128 * c:128 * (c + 1)],
                             xk_b[c][:, t0:t0 + W], pb[bank][:, 0:W])
                            for c in range(CG)], waits=(wk_tok,))
            ptok = t_fence()
            def fek(s, bank=bank, t0=t0, W=W):
                return s.activation(ek_d[:, 1 + t0:1 + t0 + W],
                                    pb[bank][:, 0:W], AF.Exp)
            sch.on("scalar", fek, waits=(ptok, prev_y_tok),
                   incs=(s_sc.inc(),))
            bank_free[bank] = s_sc.now()
        ek_tok = s_sc.now()
        def fcarry_recv(g, d=d):
            pid = parity_open(g)
            with g.If_eq(pid, 0):
                exA[0] += 16
                g.dma_start(out=carry_sb[:, 2 * d:2 * d + 2],
                            in_=czero).then_inc(s_exA.h, 16)
                g.wait_ge(s_exA.h, exA[0])
            with g.Else():
                emit_poll(g, d, 1)
                exB[0] += 16
                g.dma_start(out=carry_sb[:, 2 * d:2 * d + 2],
                            in_=sh_carry[d].ap()).then_inc(s_exB.h, 16)
                g.wait_ge(s_exB.h, exB[0])
            return g.nop()
        sch.on("gpsimd", fcarry_recv, incs=(s_gp.inc(),))
        carry_tok = s_gp.now()
        for bi, (t0, W) in enumerate(TB4):
            bank = 4 + bi % 2
            mm_group(bank, [(wv_blk[buf][:, 128 * c:128 * (c + 1)],
                             xv_b[c][:, t0:t0 + W], pb[bank][:, 0:W])
                            for c in range(CG)], waits=(wv_tok,))
            ptok = t_fence()
            def fekv(v, bank=bank, t0=t0, W=W):
                return v.tensor_mul(ekv_d[:, 1 + t0:1 + t0 + W],
                                    pb[bank][:, 0:W],
                                    ek_d[:, 1 + t0:1 + t0 + W])
            sch.on("vector", fekv, waits=(ptok, ek_tok), incs=(s_vc.inc(),))
            bank_free[bank] = s_vc.now()
        sch.on("vector", lambda v, d=d: v.tensor_copy(
            ekv_d[:, 0:1], carry_sb[:, 2 * d:2 * d + 1]), waits=(carry_tok,),
            incs=(s_vc.inc(),))
        sch.on("vector", lambda v, d=d: v.tensor_copy(
            ek_d[:, 0:1], carry_sb[:, 2 * d + 1:2 * d + 2]),
            incs=(s_vc.inc(),))
        cpy_tok = s_vc.now()
        sch.on("scalar", lambda s, d=d: s.activation(
            ew_d[:, 3 * d:3 * d + 1], vec_a[:, d:d + 1], AF.Exp),
            waits=(setup_done,), incs=(s_sc.inc(),))
        sch.on("scalar", lambda s, d=d: s.activation(
            ew_d[:, 3 * d + 1:3 * d + 2], ew_d[:, 3 * d:3 * d + 1],
            AF.Exp, scale=-1.0), waits=(s_sc.now(),), incs=(s_sc.inc(),))
        sch.on("scalar", lambda s, d=d: s.activation(
            ew_d[:, 3 * d + 2:3 * d + 3], vec_a[:, 8 + d:9 + d], AF.Exp),
            incs=(s_sc.inc(),))
        sch.on("scalar", lambda s, d=d: s.activation(
            ew_big[:, 0:TOK], sr_d[:, 0:TOK], AF.Identity, scale=0.0,
            bias=ew_d[:, 3 * d + 1:3 * d + 2]), waits=(s_sc.now(), sr_tok),
            incs=(s_sc.inc(),))
        sch.on("scalar", lambda s, d=d: s.activation(
            ew_big[:, TOK:TOK + 1], sr_d[:, 0:1], AF.Identity, scale=0.0,
            bias=ew_d[:, 3 * d + 1:3 * d + 2]), incs=(s_sc.inc(),))
        ewb_tok = s_sc.now()
        sch.on("vector", lambda v: v.tensor_tensor_scan(
            aa_d[:, :], ew_big[:, :], ekv_d[:, :], 0.0, ALU.mult, ALU.add),
            waits=(ewb_tok, cpy_tok), incs=(s_vc.inc(),))
        sch.on("vector", lambda v: v.tensor_tensor_scan(
            bb_d[:, :], ew_big[:, :], ek_d[:, :], 0.0, ALU.mult, ALU.add),
            incs=(s_vc.inc(),))
        scans_tok = s_vc.now()
        sch.on("vector", lambda v, d=d: v.tensor_copy(
            carry_out[:, 2 * d:2 * d + 1], aa_d[:, TOK:TOK + 1]),
            waits=(scans_tok,), incs=(s_vc.inc(),))
        sch.on("vector", lambda v, d=d: v.tensor_copy(
            carry_out[:, 2 * d + 1:2 * d + 2], bb_d[:, TOK:TOK + 1]),
            incs=(s_vc.inc(),))
        scan_tok = s_vc.now()
        def fcarry_send(g, d=d):
            pid = parity_open(g)
            with g.If_eq(pid, 0):
                exA[0] += 16
                g.dma_start(out=sh_carry[d].ap(),
                            in_=carry_out[:, 2 * d:2 * d + 2]
                            ).then_inc(s_exA.h, 16)
                g.wait_ge(s_exA.h, exA[0])
                exA[0] += 16
                g.dma_start(out=sh_fl.ap()[0:1, d:d + 1],
                            in_=flagv[:, :]).then_inc(s_exA.h, 16)
            with g.Else():
                g.nop()
            return g.nop()
        sch.on("gpsimd", fcarry_send, waits=(scan_tok,))
        if DBG and d in (0, 1, CG - 1):
            sfx = str(d) if d != CG - 1 else "7"
            for nm, tt in ((f"d_aa{sfx}", aa_d), (f"d_bb{sfx}", bb_d),
                           (f"d_ewb{sfx}", ew_big), (f"d_srg{sfx}", sr_d),
                           (f"d_ekg{sfx}", ek_d), (f"d_ekvg{sfx}", ekv_d)):
                a = dbg(nm, [128, tt.shape[1]])
                def fdmp(sy, a=a, tt=tt):
                    return sy.dma_start(out=a, in_=tt[:, :])
                sch.on("sync", fdmp, waits=(scan_tok,), incs=(s_out.inc(16),))
            sch.on("vector", lambda v: v.nop(), waits=(s_out.now(),))
        sch.on("vector", lambda v, d=d: v.scalar_tensor_tensor(
            aa_d[:, 0:TOK], ekv_d[:, 1:TOK + 1], ew_d[:, 3 * d + 2:3 * d + 3],
            aa_d[:, 0:TOK], ALU.mult, ALU.add), waits=(scan_tok,),
            incs=(s_vc.inc(),))
        sch.on("vector", lambda v, d=d: v.scalar_tensor_tensor(
            bb_d[:, 0:TOK], ek_d[:, 1:TOK + 1], ew_d[:, 3 * d + 2:3 * d + 3],
            bb_d[:, 0:TOK], ALU.mult, ALU.add), incs=(s_vc.inc(),))
        sch.on("vector", lambda v: v.reciprocal(ew_big[:, 0:TOK],
                                                bb_d[:, 0:TOK]),
               waits=(s_vc.now(),), incs=(s_vc.inc(),))
        sch.on("vector", lambda v: v.tensor_mul(aa_d[:, 0:TOK],
                                                aa_d[:, 0:TOK],
                                                ew_big[:, 0:TOK]),
               waits=(s_vc.now(),), incs=(s_vc.inc(),))
        sch.on("vector", lambda v, d=d: v.tensor_mul(
            rwkv_b[d][:, :], aa_d[:, 0:TOK], sr_d[:, :]),
            waits=(s_vc.now(),), incs=(s_vc.inc(),))
        prev_y_tok = s_vc.now()
    rwkv_done = s_vc.now()
    if DBG and PH >= 2:
        for dd in range(CG):
            ap = dbg(f"d_rw{dd}", [128, TOK], BF16)
            def fdr(sy, ap=ap, dd=dd):
                return sy.dma_start(out=ap, in_=rwkv_b[dd][:, :])
            sch.on("sync", fdr, incs=(s_out.inc(16),))
    if PH >= 2:
        last_done = rwkv_done

    segX.reset()
    x1 = [segX.alloc([128, TOK], F32, f"x1_{c}") for c in range(CG)]
    segY.reset()
    woa_blk = [segY.alloc([128, D], BF16, f"woab{i}") for i in range(2)]
    xi_blk = segY.alloc([128, 512], F32, "xi_blk")
    wblk_free[("woa", 0)] = rwkv_done
    wblk_free[("woa", 1)] = rwkv_done
    xi_free = rwkv_done
    for cg in range(CG if PH >= 3 else 0):
        buf = cg % 2
        wtok = load_wblk(woa_blk[buf], woTa, cg, ("woa", buf))
        for bi, (t0, W) in enumerate(TB4):
            bank = bi % 2
            mm_group(bank, [(woa_blk[buf][:, 128 * dd:128 * (dd + 1)],
                             rwkv_b[dd][:, t0:t0 + W], pb[bank][:, 0:W])
                            for dd in range(CG)], waits=(wtok, rwkv_done))
            ptok = t_fence()
            def fxl(g, cg=cg, t0=t0, W=W):
                return g.dma_start(out=xi_blk[:, 0:W],
                                   in_=xin[128 * cg:128 * (cg + 1),
                                           1 + t0:1 + t0 + W])
            sch.on("gpsimd", fxl, waits=(xi_free,), incs=(s_xld.inc(16),))
            xl_tok = s_xld.now()
            def fadd(v, cg=cg, t0=t0, W=W, bank=bank):
                return v.tensor_add(x1[cg][:, t0:t0 + W], pb[bank][:, 0:W],
                                    xi_blk[:, 0:W])
            sch.on("vector", fadd, waits=(ptok, xl_tok), incs=(s_vc.inc(),))
            bank_free[bank] = s_vc.now()
            xi_free = s_vc.now()
        wblk_free[("woa", buf)] = s_vc.now()
    x1_tok = s_vc.now()
    if PH >= 3:
        last_done = x1_tok

    def emit_ffn(ready_tok, tag):
        segY.reset()
        mark = segY.mark()
        lnb = [segY.alloc([128, 512], BF16, f"lnb{c}{tag}") for c in range(CG)]
        hbuf = [segY.alloc([128, 512], BF16, f"hb{h}{tag}") for h in range(HG)]
        gbuf = [segY.alloc([128, 512], BF16, f"gb{h}{tag}") for h in range(HG)]
        sA = segY.alloc([128, 512], F32, f"sA{tag}")
        w12 = [segY.alloc([128, D], BF16, f"w12_{i}{tag}") for i in range(4)]
        wbig = [segY.alloc([128, HID], BF16, f"wbig{i}{tag}") for i in range(4)]
        wbig_free = {i: ready_tok for i in range(4)}
        w12_free = {0: ready_tok, 1: ready_tok}
        last_tok = ready_tok
        for (t0, W) in TB4:
            ln_tok = emit_ln_block(x1, t0, W, lnb, 0, src_ready=(last_tok,))
            for h in range(HG):
                b0, b1 = (0, 1) if h % 2 == 0 else (2, 3)
                wb1, wb2 = w12[2 * (h % 2)], w12[2 * (h % 2) + 1]
                def fl1(g, wb1=wb1, h=h):
                    return g.dma_start(out=wb1[:, :], in_=w1T[h])
                def fl2(g, wb2=wb2, h=h):
                    return g.dma_start(out=wb2[:, :], in_=w2T[h])
                sem12 = wsem(("w12", h % 2))
                sch.on("gpsimd", fl1, waits=(w12_free.get(h % 2),),
                       incs=(sem12.inc(16),))
                sch.on("gpsimd", fl2, incs=(sem12.inc(16),))
                wl_tok = sem12.now()
                mm_group(b0, [(wb1[:, 128 * c:128 * (c + 1)], lnb[c][:, 0:W],
                               pb[b0][:, 0:W]) for c in range(CG)],
                         waits=(wl_tok, ln_tok))
                mm_group(b1, [(wb2[:, 128 * c:128 * (c + 1)], lnb[c][:, 0:W],
                               pb[b1][:, 0:W]) for c in range(CG)])
                ptok = t_fence()
                w12_free[h % 2] = ptok
                def fsilu(s, b0=b0, h=h, W=W):
                    return s.activation(sA[:, 0:W], pb[b0][:, 0:W], AF.Silu,
                                        scale=ffs_s[:, h:h + 1])
                sch.on("scalar", fsilu, waits=(ptok, s_vc.now()),
                       incs=(s_sc.inc(),))
                si_tok = s_sc.now()
                bank_free[b0] = si_tok
                def fh(v, b1=b1, h=h, W=W):
                    return v.scalar_tensor_tensor(
                        hbuf[h][:, 0:W], pb[b1][:, 0:W],
                        ffs_s[:, HG + h:HG + h + 1], sA[:, 0:W],
                        ALU.mult, ALU.mult)
                sch.on("vector", fh, waits=(si_tok,), incs=(s_vc.inc(),))
                bank_free[b1] = s_vc.now()
            stage_in_tok = s_vc.now()
            src, dst = hbuf, gbuf
            for (wT, scol) in ((h1T, 2 * HG), (h2T, 3 * HG)):
                for og in range(HG):
                    bank = og % 4
                    wb = wbig[og % 4]
                    def flw(g, wb=wb, wT=wT, og=og):
                        return g.dma_start(out=wb[:, :], in_=wT[og])
                    semb = wsem(("wbig", og % 4))
                    sch.on("gpsimd", flw, waits=(wbig_free.get(og % 4),),
                           incs=(semb.inc(16),))
                    wl_tok = semb.now()
                    mm_group(bank, [(wb[:, 128 * k:128 * (k + 1)],
                                     src[k][:, 0:W], pb[bank][:, 0:W])
                                    for k in range(HG)],
                             waits=(wl_tok, stage_in_tok))
                    ptok = t_fence()
                    wbig_free[og % 4] = ptok
                    def fg(s, bank=bank, og=og, W=W, scol=scol, dst=dst):
                        return s.activation(
                            dst[og][:, 0:W], pb[bank][:, 0:W], AF.Gelu,
                            scale=ffs_s[:, scol + og:scol + og + 1])
                    sch.on("scalar", fg, waits=(ptok,), incs=(s_sc.inc(),))
                    bank_free[bank] = s_sc.now()
                src, dst = dst, src
                stage_in_tok = s_sc.now()
            for cg in range(CG):
                bank = cg % 4
                wb = wbig[cg % 4]
                def flw3(g, wb=wb, cg=cg):
                    return g.dma_start(out=wb[:, :], in_=w3T[cg])
                semb = wsem(("wbig", cg % 4))
                sch.on("gpsimd", flw3, waits=(wbig_free.get(cg % 4),),
                       incs=(semb.inc(16),))
                wl_tok = semb.now()
                mm_group(bank, [(wb[:, 128 * k:128 * (k + 1)],
                                 src[k][:, 0:W], pb[bank][:, 0:W])
                                for k in range(HG)],
                         waits=(wl_tok, stage_in_tok))
                ptok = t_fence()
                wbig_free[cg % 4] = ptok
                def fx(v, bank=bank, cg=cg, t0=t0, W=W):
                    return v.scalar_tensor_tensor(
                        x1[cg][:, t0:t0 + W], pb[bank][:, 0:W],
                        ffs_s[:, 4 * HG + cg:4 * HG + cg + 1],
                        x1[cg][:, t0:t0 + W], ALU.mult, ALU.add)
                sch.on("vector", fx, waits=(ptok,), incs=(s_vc.inc(),))
                bank_free[bank] = s_vc.now()
            last_tok = s_vc.now()
        segY.reset(mark)
        return last_tok

    if PH >= 4:
        x2_tok = emit_ffn(x1_tok, "f1")
        last_done = x2_tok
    else:
        x2_tok = x1_tok

    segY.reset()
    markY0 = segY.mark()
    xn_loc = [segY.alloc([128, 512], BF16, f"xnl{c}") for c in range(CG)]
    t_f32 = [segY.alloc([128, 512], F32, f"t3_{c}") for c in range(CG)]
    last = x2_tok
    for bi, (t0, W) in enumerate(TB4 if PH >= 5 else []):
        ln_tok = emit_ln_block(x1, t0, W, t_f32, 0, src_ready=(last,))
        sq_guard = {}
        for c in range(CG):
            w = [ln_tok] if c == 0 else []
            if c >= 2:
                w.append((s_tn.h, sq_guard[c - 2]))
            def fsq(s, c=c, W=W):
                return s.activation(sq_t[c % 2][:, 0:W], t_f32[c][:, 0:W],
                                    AF.Square)
            sch.on("scalar", fsq, waits=w, incs=(s_sc.inc(),))
            sqv = s_sc.now()
            bw = [sqv]
            if c == 0 and bank_free[7] is not None:
                bw.append(bank_free[7])
            def fm(t, c=c, W=W):
                return t.matmul(pb[7][0:1, 0:W], ones_col[:, 0:1],
                                sq_t[c % 2][:, 0:W], start=(c == 0),
                                stop=(c == CG - 1))
            sch.on("tensor", fm, waits=bw)
            sch.on("tensor", lambda t: (t.drain(), t.nop())[1],
                   incs=(s_tn.inc(),))
            sq_guard[c] = s_tn.n
        rms_tok = (s_tn.h, s_tn.n)
        sch.on("scalar", lambda s, W=W: s.activation(
            srow[0:1, 0:W], pb[7][0:1, 0:W], AF.Sqrt, bias=eps2[0:1, 0:1],
            scale=1.0 / D), waits=(rms_tok,), incs=(s_sc.inc(),))
        sch.on("vector", lambda v, W=W: v.reciprocal(vrow[0:1, 0:W],
                                                     srow[0:1, 0:W]),
               waits=(s_sc.now(),), incs=(s_vc.inc(),))
        rr_tok = s_vc.now()
        bank_free[7] = rr_tok
        mm_group(7, [(ones_row[0:1, :], vrow[0:1, 0:W], pb[7][:, 0:W])],
                 waits=(rr_tok,))
        bc_tok = t_fence()
        for c in range(CG):
            def fa(v, c=c, W=W):
                return v.tensor_mul(sq_t[c % 2][:, 0:W], t_f32[c][:, 0:W],
                                    pb[7][:, 0:W])
            def fb(v, c=c, W=W):
                return v.tensor_scalar_mul(xn_loc[c][:, 0:W],
                                           sq_t[c % 2][:, 0:W],
                                           vec_b[:, 12 + c:13 + c])
            sch.on("vector", fa,
                   waits=(bc_tok,) if c == 0 else (s_vc.now(),),
                   incs=(s_vc.inc(),))
            sch.on("vector", fb, waits=(s_vc.now(),), incs=(s_vc.inc(),))
        xn_tok = s_vc.now()
        bank_free[7] = xn_tok
        def fship(g, t0=t0, W=W):
            pid = parity_open(g)
            with g.If_eq(pid, 0):
                exA[0] += CG * 16
                for c in range(CG):
                    g.dma_start(
                        out=sh_xn[0].ap()[128 * c:128 * (c + 1), t0:t0 + W],
                        in_=xn_loc[c][:, 0:W]).then_inc(s_exA.h, 16)
                g.wait_ge(s_exA.h, exA[0])
            with g.Else():
                exB[0] += CG * 16
                for c in range(CG):
                    g.dma_start(
                        out=sh_xn[1].ap()[128 * c:128 * (c + 1), t0:t0 + W],
                        in_=xn_loc[c][:, 0:W]).then_inc(s_exB.h, 16)
                g.wait_ge(s_exB.h, exB[0])
            return g.nop()
        sch.on("gpsimd", fship, waits=(xn_tok,), incs=(s_gp.inc(),))
        last = s_gp.now()

    def fxflag(g):
        pid = parity_open(g)
        with g.If_eq(pid, 0):
            exA[0] += 16
            g.dma_start(out=sh_fl.ap()[0:1, 8:9],
                        in_=flagv[:, :]).then_inc(s_exA.h, 16)
            emit_poll(g, 9, 0)
        with g.Else():
            exB[0] += 16
            g.dma_start(out=sh_fl.ap()[0:1, 9:10],
                        in_=flagv[:, :]).then_inc(s_exB.h, 16)
            emit_poll(g, 8, 1)
        return g.nop()
    if PH >= 5:
        sch.on("gpsimd", fxflag, incs=(s_gp.inc(),))
        last_done = s_gp.now()
    xn_ready = s_gp.now()

    segY.reset(markY0)
    o_fm = [segY.alloc([128, TFULL], BF16, f"ofm{p}") for p in range(4)]
    markY1 = segY.mark()
    q_fm = segY.alloc([128, TFULL], BF16, "q_fm")
    k_fm = segY.alloc([128, TFULL], BF16, "k_fm")
    v_fm = segY.alloc([128, TFULL], BF16, "v_fm")
    v_tok = segY.alloc([128, TFULL], BF16, "v_tok")
    xn_st = [segY.alloc([128, 512], BF16, f"xst{c}") for c in range(CG)]
    wq_blk = segY.alloc([128, D], BF16, "wq_blk")
    wkb_blk = segY.alloc([128, D], BF16, "wkb_blk")
    wvb_blk = segY.alloc([128, D], BF16, "wvb_blk")
    p_bf = [segY.alloc([128, 512], BF16, f"pbf{h}") for h in range(2)]
    pt_bf = [segY.alloc([128, 512], BF16, f"ptb{h}") for h in range(2)]
    mask01_s = segY.alloc([128, 4 * 512], BF16, "mask01_s")
    s_msk = wsem(("mask",))
    sch.on("gpsimd", lambda g: g.dma_start(out=mask01_s[:, :], in_=dmask01),
           waits=(x2_tok,), incs=(s_msk.inc(16),))
    mask_ld = s_msk.now()

    s_vtr = wsem(("vtr",))
    s_otr = wsem(("otr",))
    s_ptr = [wsem(("ptr", i)) for i in range(2)]
    xnst_free = None
    vt_free = None
    p_free = [None, None]
    pt_free = [None, None]
    obf_free = None
    ofm_tok = []
    wblk_free[("wq", 0)] = x2_tok
    wblk_free[("wkb", 0)] = x2_tok
    wblk_free[("wvb", 0)] = x2_tok
    for p in range(4 if PH >= 6 else 0):
        tq_tok = load_wblk(wq_blk, wqT, p, ("wq", 0))
        tk_tok = load_wblk(wkb_blk, wkTb, p, ("wkb", 0))
        tv_tok = load_wblk(wvb_blk, wvTb, p, ("wvb", 0))
        pw_toks = {"q": tq_tok, "k": tk_tok, "v": tv_tok}
        for bi, (t0, W) in enumerate(TB8):
            slot = 0 if t0 < TOK else 1
            c0 = t0 % TOK
            for c in range(CG):
                def fxn(g, c=c, slot=slot, c0=c0, W=W):
                    return g.dma_start(
                        out=xn_st[c][:, 0:W],
                        in_=sh_xn[slot].ap()[128 * c:128 * (c + 1),
                                             c0:c0 + W])
                w = (xn_ready, xnst_free) if c == 0 else ()
                sch.on("gpsimd", fxn, waits=w, incs=(s_xld.inc(16),))
            xl_tok = s_xld.now()
            for (bank, wb, dstt, kind) in ((0, wq_blk, q_fm, "q"),
                                           (1, wkb_blk, k_fm, "k"),
                                           (2, wvb_blk, v_fm, "v")):
                mm_group(bank, [(wb[:, 128 * c:128 * (c + 1)],
                                 xn_st[c][:, 0:W], pb[bank][:, 0:W])
                                for c in range(CG)],
                         waits=(pw_toks[kind], xl_tok))
                ptok = t_fence()
                if kind == "q":
                    def fe(s, bank=bank, dstt=dstt, t0=t0, W=W, p=p):
                        return s.activation(dstt[:, t0:t0 + W],
                                            pb[bank][:, 0:W], AF.Identity,
                                            bias=bq_st[p][:, 0:1],
                                            scale=0.125)
                elif kind == "k":
                    def fe(s, bank=bank, dstt=dstt, t0=t0, W=W, p=p):
                        return s.activation(dstt[:, t0:t0 + W],
                                            pb[bank][:, 0:W], AF.Identity,
                                            bias=bk_st[p][:, 0:1])
                else:
                    def fe(s, bank=bank, dstt=dstt, t0=t0, W=W, p=p):
                        return s.activation(dstt[:, t0:t0 + W],
                                            pb[bank][:, 0:W], AF.Identity,
                                            bias=bv_st[p][:, 0:1])
                sch.on("scalar", fe, waits=(ptok,), incs=(s_sc.inc(),))
                bank_free[bank] = s_sc.now()
            xnst_free = s_sc.now()
        proj_tok = s_sc.now()
        wblk_free[("wq", 0)] = proj_tok
        wblk_free[("wkb", 0)] = proj_tok
        wblk_free[("wvb", 0)] = proj_tok
        if DBG and p == 0:
            for nm, tt in (("d_qfm", q_fm), ("d_kfm", k_fm), ("d_vfm", v_fm),
                           ("d_bqst", bq_st[0]), ("d_bqsc", bq_sc)):
                a = dbg(nm, [128, tt.shape[1]],
                        BF16 if tt.dtype == BF16 else F32)
                def fdq(sy, a=a, tt=tt):
                    return sy.dma_start(out=a, in_=tt[:, :])
                sch.on("sync", fdq, waits=(proj_tok,), incs=(s_out.inc(16),))
        for tj in range(TFULL // 128):
            def ftr(s, tj=tj):
                return s.dma_start(out=v_tok[:, 128 * tj:128 * (tj + 1)],
                                   in_=v_fm[:, 128 * tj:128 * (tj + 1)],
                                   transpose=True)
            w = (proj_tok, vt_free) if tj == 0 else ()
            sch.on("sync", ftr, waits=w, incs=(s_vtr.inc(16),))
        vtr_tok = s_vtr.now()

        if PH >= 6 and PH < 7:
            last_done = vtr_tok
        for qi in range(NQT if PH >= 7 else 0):
            nblk = qi // 4 + 1
            for h2 in range(2):
                r0 = 64 * h2
                for j in range(nblk):
                    mm_group(j, [(q_fm[r0:r0 + 64, 128 * qi:128 * (qi + 1)],
                                  k_fm[r0:r0 + 64, 512 * j:512 * (j + 1)],
                                  pb[j][:, 0:512])],
                             waits=(proj_tok,) if j == 0 else ())
                sc_tok = t_fence()
                for j in range(nblk):
                    def fmx(v, h2=h2, j=j):
                        return v.reduce_max(mx8[h2][:, j:j + 1],
                                            pb[j][:, 0:512], axis=AX)
                    sch.on("vector", fmx, waits=(sc_tok,) if j == 0 else (),
                           incs=(s_vc.inc(),))
                def fcmb(v, h2=h2, nblk=nblk):
                    return v.reduce_max(mneg_t[h2][:, :],
                                        mx8[h2][:, 0:nblk], axis=AX,
                                        negate=True)
                sch.on("vector", fcmb, waits=(s_vc.now(),),
                       incs=(s_vc.inc(),))
                mstat_tok = s_vc.now()
                pv_started = False
                for j in range(nblk):
                    diag = (j == nblk - 1)
                    slot = j % 2
                    def fpexp(s, h2=h2, j=j, slot=slot):
                        return s.activation(p_bf[slot][:, 0:512],
                                            pb[j][:, 0:512], AF.Exp,
                                            bias=mneg_t[h2][:, 0:1],
                                            accum_out=lc8[h2][:, j:j + 1])
                    sch.on("scalar", fpexp,
                           waits=(mstat_tok, p_free[slot]),
                           incs=(s_sc.inc(),))
                    pexp_tok = s_sc.now()
                    bank_free[j] = pexp_tok
                    if diag:
                        mv = qi % 4
                        def fpm(v, slot=slot, mv=mv):
                            return v.tensor_mul(
                                p_bf[slot][:, 0:512], p_bf[slot][:, 0:512],
                                mask01_s[:, 512 * mv:512 * (mv + 1)])
                        sch.on("vector", fpm, waits=(pexp_tok, mask_ld),
                               incs=(s_vc.inc(),))
                        def fls(v, h2=h2, j=j, slot=slot):
                            return v.reduce_sum(lc8[h2][:, j:j + 1],
                                                p_bf[slot][:, 0:512], axis=AX)
                        sch.on("vector", fls, waits=(s_vc.now(),),
                               incs=(s_vc.inc(),))
                        pexp_tok = s_vc.now()
                    for u in range(4):
                        def ftp(sy, slot=slot, u=u):
                            return sy.dma_start(
                                out=pt_bf[slot][:, 128 * u:128 * (u + 1)],
                                in_=p_bf[slot][:, 128 * u:128 * (u + 1)],
                                transpose=True)
                        w = (pexp_tok, pt_free[slot]) if u == 0 else ()
                        sch.on("sync", ftp, waits=w,
                               incs=(s_ptr[slot].inc(16),))
                    p_free[slot] = s_ptr[slot].now()
                    ptr_tok = s_ptr[slot].now()
                    mms = [(pt_bf[slot][:, 128 * u:128 * (u + 1)],
                            v_tok[:, 128 * (4 * j + u) + r0:
                                  128 * (4 * j + u) + r0 + 64],
                            pb[0][:, 64 * h2:64 * h2 + 64]) for u in range(4)]
                    w = [ptr_tok, vtr_tok]
                    if j == 0:
                        w.append(pexp_tok if not diag else pexp_tok)
                    n = len(mms)
                    for i, (lhsT, rhs, outp) in enumerate(mms):
                        def fpv(t, lhsT=lhsT, rhs=rhs, outp=outp,
                                st=(j == 0 and i == 0),
                                sp=(j == nblk - 1 and i == n - 1)):
                            return t.matmul(outp, lhsT, rhs, start=st, stop=sp)
                        sch.on("tensor", fpv, waits=w if i == 0 else ())
                pv_tok = t_fence()
                pt_free[0] = pv_tok
                pt_free[1] = pv_tok
                def flsum(s, h2=h2, nblk=nblk):
                    return s.activation(lsc[h2][:, 0:nblk],
                                        lc8[h2][:, 0:nblk], AF.Identity,
                                        accum_out=lsum[h2][:, :])
                sch.on("scalar", flsum, waits=(pv_tok,), incs=(s_sc.inc(),))
                lsum_tok = s_sc.now()
                def frl(v, h2=h2):
                    return v.reciprocal(rl_t[h2][:, :], lsum[h2][:, :])
                sch.on("vector", frl, waits=(lsum_tok,), incs=(s_vc.inc(),))
                rl_tok = s_vc.now()
                def fOev(s, h2=h2):
                    return s.activation(o_bf[:, 64 * h2:64 * h2 + 64],
                                        pb[0][:, 64 * h2:64 * h2 + 64],
                                        AF.Identity, scale=rl_t[h2][:, 0:1])
                sch.on("scalar", fOev, waits=(rl_tok, obf_free),
                       incs=(s_sc.inc(),))
                bank_free[0] = s_sc.now()
            fin_tok = s_sc.now()
            def ftro(s, p=p, qi=qi):
                return s.dma_start(out=o_fm[p][:, 128 * qi:128 * (qi + 1)],
                                   in_=o_bf[:, :], transpose=True)
            sch.on("sync", ftro, waits=(fin_tok,), incs=(s_otr.inc(16),))
            obf_free = s_otr.now()
        vt_free = s_tn.now()
        if DBG and p == 0 and PH >= 7:
            for nm, tt in (("d_vtok", v_tok), ("d_ofm0", o_fm[0])):
                a = dbg(nm, [128, tt.shape[1]], BF16)
                def fdv(sy, a=a, tt=tt):
                    return sy.dma_start(out=a, in_=tt[:, :])
                sch.on("sync", fdv, waits=(s_otr.now(),),
                       incs=(s_out.inc(16),))
        ofm_tok.append(s_otr.now())
        if PH >= 7:
            last_done = s_otr.now()

    segY.reset(markY1)
    o_full = [segY.alloc([128, TOK], BF16, f"ofl{g}") for g in range(CG)]
    wob_blk = [segY.alloc([128, D], BF16, f"wob{i}") for i in range(2)]
    xnm_st = [segY.alloc([128, 512], BF16, f"xnm{c}") for c in range(CG)]
    osum = segY.alloc([128, 512], F32, "osum")

    flash_done = s_tn.now()

    def foship(g):
        pid = parity_open(g)
        with g.If_eq(pid, 0):
            exA[0] += 4 * 16
            for p in range(4):
                g.dma_start(out=sh_O[0].ap()[128 * p:128 * (p + 1), :],
                            in_=o_fm[p][:, :]).then_inc(s_exA.h, 16)
            g.wait_ge(s_exA.h, exA[0])
            exA[0] += 16
            g.dma_start(out=sh_fl.ap()[0:1, 10:11],
                        in_=flagv[:, :]).then_inc(s_exA.h, 16)
            emit_poll(g, 11, 0)
            exA[0] += CG * 16
            for gg in range(CG):
                g.dma_start(
                    out=o_full[gg][:, :],
                    in_=sh_O[gg // 4].ap()[128 * (gg % 4):128 * (gg % 4 + 1),
                                           0:TOK]).then_inc(s_exA.h, 16)
            g.wait_ge(s_exA.h, exA[0])
        with g.Else():
            exB[0] += 4 * 16
            for p in range(4):
                g.dma_start(out=sh_O[1].ap()[128 * p:128 * (p + 1), :],
                            in_=o_fm[p][:, :]).then_inc(s_exB.h, 16)
            g.wait_ge(s_exB.h, exB[0])
            exB[0] += 16
            g.dma_start(out=sh_fl.ap()[0:1, 11:12],
                        in_=flagv[:, :]).then_inc(s_exB.h, 16)
            emit_poll(g, 10, 1)
            exB[0] += CG * 16
            for gg in range(CG):
                g.dma_start(
                    out=o_full[gg][:, :],
                    in_=sh_O[gg // 4].ap()[128 * (gg % 4):128 * (gg % 4 + 1),
                                           TOK:TFULL]).then_inc(s_exB.h, 16)
            g.wait_ge(s_exB.h, exB[0])
        return g.nop()
    if PH >= 8:
        sch.on("gpsimd", foship, waits=tuple(ofm_tok) + (flash_done,),
               incs=(s_gp.inc(),))
    o_ready = s_gp.now()

    wblk_free[("wob", 0)] = flash_done
    wblk_free[("wob", 1)] = flash_done
    xnm_free = None
    for bi, (t0, W) in enumerate(TB4 if PH >= 8 else []):
        def fxnm(g, t0=t0, W=W):
            pid = parity_open(g)
            with g.If_eq(pid, 0):
                exA[0] += CG * 16
                for c in range(CG):
                    g.dma_start(
                        out=xnm_st[c][:, 0:W],
                        in_=sh_xn[0].ap()[128 * c:128 * (c + 1),
                                          t0:t0 + W]).then_inc(s_exA.h, 16)
                g.wait_ge(s_exA.h, exA[0])
            with g.Else():
                exB[0] += CG * 16
                for c in range(CG):
                    g.dma_start(
                        out=xnm_st[c][:, 0:W],
                        in_=sh_xn[1].ap()[128 * c:128 * (c + 1),
                                          t0:t0 + W]).then_inc(s_exB.h, 16)
                g.wait_ge(s_exB.h, exB[0])
            return g.nop()
        sch.on("gpsimd", fxnm, waits=(o_ready, xnm_free), incs=(s_gp.inc(),))
        xnm_tok = s_gp.now()
        for cg in range(CG):
            buf = cg % 2
            wtok = load_wblk(wob_blk[buf], woTb, cg, ("wob", buf))
            bank = cg % 4
            mm_group(bank, [(wob_blk[buf][:, 128 * dd:128 * (dd + 1)],
                             o_full[dd][:, t0:t0 + W], pb[bank][:, 0:W])
                            for dd in range(CG)], waits=(wtok, o_ready))
            ptok = t_fence()
            wblk_free[("wob", buf)] = ptok
            def fbo(s, bank=bank, cg=cg, W=W):
                return s.activation(osum[:, 0:W], pb[bank][:, 0:W], AF.Identity,
                                    bias=bo_st[cg][:, 0:1])
            sch.on("scalar", fbo, waits=(ptok, s_vc.now()),
                       incs=(s_sc.inc(),))
            bo_tok = s_sc.now()
            bank_free[bank] = bo_tok
            def fr1(v, cg=cg, W=W):
                return v.tensor_add(osum[:, 0:W], osum[:, 0:W],
                                    xnm_st[cg][:, 0:W])
            def fr2(v, cg=cg, t0=t0, W=W):
                return v.tensor_add(x1[cg][:, t0:t0 + W],
                                    x1[cg][:, t0:t0 + W], osum[:, 0:W])
            sch.on("vector", fr1, waits=(bo_tok, xnm_tok),
                   incs=(s_vc.inc(),))
            sch.on("vector", fr2, waits=(s_vc.now(),), incs=(s_vc.inc(),))
        xnm_free = s_vc.now()
    x3_tok = s_vc.now()
    if PH >= 8:
        last_done = x3_tok

    if PH >= 9:
        x4_tok = emit_ffn(x3_tok, "f2")
        last_done = x4_tok
    else:
        x4_tok = last_done

    for cg in range(CG):
        def fo2(s, cg=cg):
            return s.dma_start(out=out_fm[128 * cg:128 * (cg + 1), :],
                               in_=x1[cg][:, :])
        sch.on("sync", fo2, waits=(last_done,) if cg == 0 else (),
               incs=(s_out.inc(16),))
    sch.on("sync", lambda s: s.nop(), waits=(s_out.now(),))

    with nc.Block() as block:
        block.tensor(sch.runner("tensor"))
        block.vector(sch.runner("vector"))
        block.scalar(sch.runner("scalar"))
        block.gpsimd(sch.runner("gpsimd"))
        block.sync(sch.runner("sync"))

    return nc


def _blk(wt, og):
    i, o = wt.shape
    c = i // 128
    a = wt.reshape(i, og, 128).transpose(1, 0, 2)
    a = a.reshape(og, c, 128, 128).transpose(0, 2, 1, 3)
    return np.ascontiguousarray(a.reshape(og, 128, i))


def _col(v):
    return np.ascontiguousarray(v.reshape(-1, 128).T)


def kernel(**inp):
    x = np.asarray(inp["L0_attn_a_x"], np.float32)
    g = lambda n: np.asarray(inp[n], np.float32)

    base = dict(
        wkT=_blk(g("L0_attn_a_Wk").T, CG),
        wvT=_blk(g("L0_attn_a_Wv").T, CG),
        wrT=_blk(g("L0_attn_a_Wr").T, CG),
        woTa=_blk(g("L0_attn_a_Wo").T, CG),
        veca=np.concatenate([_col(g("L0_attn_a_time_decay")),
                             _col(g("L0_attn_a_time_first")),
                             _col(g("L0_attn_a_time_mix_k")),
                             _col(g("L0_attn_a_time_mix_v")),
                             _col(g("L0_attn_a_time_mix_r"))], axis=1),
        w1T=_blk(g("L0_ffn_w1_w").T, HG),
        w2T=_blk(g("L0_ffn_w2_w").T, HG),
        h1T=_blk(g("L0_ffn_h1_w").T, HG),
        h2T=_blk(g("L0_ffn_h2_w").T, HG),
        w3T=_blk(g("L0_ffn_w3_w").T, CG),
        ffs=np.concatenate([_col(g("L0_ffn_w1_s")), _col(g("L0_ffn_w2_s")),
                            _col(g("L0_ffn_h1_s")), _col(g("L0_ffn_h2_s")),
                            _col(g("L0_ffn_w3_s"))], axis=1),
        woTb=_blk(g("L0_attn_b_Wo").T, CG),
    )
    dmask = np.zeros((128, 4 * 512), np.float32)
    for mv in range(4):
        keep = np.arange(512)[None, :] <= (128 * mv + np.arange(128)[:, None])
        dmask[:, 512 * mv:512 * (mv + 1)] = np.where(keep, 0.0, -1e30)
    base["dmask"] = dmask
    base["dmask01"] = (dmask == 0.0).astype(np.float32)
    base["czero"] = np.zeros((128, 2), np.float32)
    base["nonce"] = np.array([[np.int64(np.random.randint(1, 2**31 - 1))]],
                             dtype=np.int32)

    in_maps = []
    for b in range(4):
        for hh in range(2):
            m = dict(base)
            t0 = TOK * hh
            xin = np.empty((D, TOK + 1), np.float32)
            if hh == 0:
                xin[:, 0] = 0.0
            else:
                xin[:, 0] = x[b, TOK - 1, :]
            xin[:, 1:] = x[b, t0:t0 + TOK, :].T
            m["xin"] = np.ascontiguousarray(xin)
            hs = slice(512 * hh, 512 * (hh + 1))
            m["wqT"] = _blk(g("L0_attn_b_Wq")[hs, :].T, 4)
            m["wkTb"] = _blk(g("L0_attn_b_Wk")[hs, :].T, 4)
            m["wvTb"] = _blk(g("L0_attn_b_Wv")[hs, :].T, 4)
            m["vecb"] = np.concatenate(
                [_col(g("L0_attn_b_bq")[hs]), _col(g("L0_attn_b_bk")[hs]),
                 _col(g("L0_attn_b_bv")[hs]), _col(g("L0_attn_b_norm_w")),
                 _col(g("L0_attn_b_bo"))], axis=1)
            in_maps.append(m)

    nc = build_program()
    res = run_bass_kernel_spmd(nc, in_maps, core_ids=list(range(8)))
    global LAST_RES
    LAST_RES = res

    out = np.empty((4, TFULL, D), np.float32)
    for b in range(4):
        for hh in range(2):
            out[b, TOK * hh:TOK * (hh + 1), :] = \
                res.results[2 * b + hh]["out_fm"].T
    return out


# version 11
# speedup vs baseline: 1.1652x; 1.0237x over previous
import numpy as np
import concourse.bass as bass
import concourse.mybir as mybir
from concourse.bass_utils import run_bass_kernel_spmd

F32 = mybir.dt.float32
BF16 = mybir.dt.bfloat16
I32 = mybir.dt.int32
AF = mybir.ActivationFunctionType
ALU = mybir.AluOpType
AX = mybir.AxisListType.X

D = 1024
HID = 4096
TOK = 2048
TFULL = 4096
CG = 8
HG = 32
NQT = 32
FLAG = 77777
POLL_CAP = 50000000
LN_EPS = 1e-5
RMS_EPS = 1e-6
TB4 = [(0, 512), (512, 512), (1024, 512), (1536, 512)]
TB8 = [(512 * i, 512) for i in range(8)]

PHASES = 99
DBG = False
LAST_RES = None

ENGS = ("tensor", "vector", "scalar", "gpsimd", "sync")


class Sem:
    def __init__(self, nc, name):
        self.h = nc.alloc_semaphore(name)
        self.n = 0

    def inc(self, by=1):
        self.n += by
        return (self.h, by)

    def now(self):
        return (self.h, self.n)


class Sched:
    def __init__(self):
        self.ops = {e: [] for e in ENGS}

    def on(self, eng, fn, waits=(), incs=()):
        waits = tuple(w for w in waits if w is not None)
        self.ops[eng].append((waits, fn, tuple(incs)))

    def runner(self, name):
        ops = self.ops[name]

        def f(e):
            for waits, fn, incs in ops:
                for (h, v) in waits:
                    e.wait_ge(h, v)
                inst = fn(e)
                for (h, by) in incs:
                    inst = inst.then_inc(h, by)

        return f


class SegAlloc:
    def __init__(self, nc, base, size, tag):
        self.nc = nc
        self.base = base
        self.size = size
        self.ptr = base
        self.tag = tag
        self.uid = 0

    def alloc(self, shape, dt, name=None):
        per = int(np.prod(shape[1:])) * (2 if dt == BF16 else 4)
        per = (per + 31) & ~31
        assert self.ptr + per <= self.base + self.size, (
            f"seg {self.tag} overflow: {name} needs {per}B at "
            f"{self.ptr - self.base}, cap {self.size}")
        off = self.ptr
        self.ptr = off + per
        self.uid += 1
        return self.nc.alloc_sbuf_tensor_at(
            name or f"{self.tag}{self.uid}", shape, dt, offset=off)

    def mark(self):
        return self.ptr

    def reset(self, m=None):
        self.ptr = self.base if m is None else m


def build_program():
    PH = PHASES
    dbg_outs = {}

    def dbg(name, shape, dt=F32):
        dbg_outs[name] = nc.dram_tensor(name, shape, dt,
                                        kind="ExternalOutput").ap()
        return dbg_outs[name]
    nc = bass.Bass("TRN2", target_bir_lowering=False)

    def din(name, shape, dt=F32):
        return nc.dram_tensor(name, shape, dt, kind="ExternalInput").ap()

    xin = din("xin", [D, TOK + 1])
    wkT = din("wkT", [CG, 128, D])
    wvT = din("wvT", [CG, 128, D])
    wrT = din("wrT", [CG, 128, D])
    woTa = din("woTa", [CG, 128, D])
    veca = din("veca", [128, 40])
    w1T = din("w1T", [HG, 128, D])
    w2T = din("w2T", [HG, 128, D])
    h1T = din("h1T", [HG, 128, HID])
    h2T = din("h2T", [HG, 128, HID])
    w3T = din("w3T", [CG, 128, HID])
    ffs = din("ffs", [128, 4 * HG + CG])
    wqT = din("wqT", [4, 128, D])
    wkTb = din("wkTb", [4, 128, D])
    wvTb = din("wvTb", [4, 128, D])
    woTb = din("woTb", [CG, 128, D])
    vecb = din("vecb", [128, 28])
    dmask = din("dmask", [128, 4 * 512])
    dmask01 = din("dmask01", [128, 4 * 512])
    nonce = din("nonce", [1, 1], I32)
    czero = din("czero", [128, 2])
    out_fm = nc.dram_tensor("out_fm", [D, TOK], F32, kind="ExternalOutput").ap()

    def shared(name, shape, dt):
        return nc.dram_tensor(name, shape, dt, kind="Internal",
                              addr_space="Shared")

    sh_carry = [shared(f"sh_carry{d}", [128, 2], F32) for d in range(CG)]
    sh_xn = [shared(f"sh_xn{s}", [D, TOK], BF16) for s in range(2)]
    sh_O = [shared(f"sh_O{s}", [512, TFULL], BF16) for s in range(2)]
    sh_fl = shared("sh_fl", [1, 16], I32)

    s_wld = Sem(nc, "s_wld")
    s_xld = Sem(nc, "s_xld")
    s_tn = Sem(nc, "s_tn")
    s_vc = Sem(nc, "s_vc")
    s_sc = Sem(nc, "s_sc")
    s_gp = Sem(nc, "s_gp")
    s_tr = Sem(nc, "s_tr")
    s_out = Sem(nc, "s_out")
    s_junk = Sem(nc, "s_junk")
    s_exA = Sem(nc, "s_exA")
    s_exB = Sem(nc, "s_exB")
    exA = [0]
    exB = [0]

    vec_a = nc.alloc_sbuf_tensor("vec_a", [128, 40], F32)
    mix1m = nc.alloc_sbuf_tensor("mix1m", [128, 24], F32)
    ffs_s = nc.alloc_sbuf_tensor("ffs_s", [128, 4 * HG + CG], F32)
    vec_b = nc.alloc_sbuf_tensor("vec_b", [128, 28], F32)
    bq_sc = nc.alloc_sbuf_tensor("bq_sc", [128, 4], F32)
    bq_st = [nc.alloc_sbuf_tensor(f"bq_st{p}", [128, 1], F32) for p in range(4)]
    bk_st = [nc.alloc_sbuf_tensor(f"bk_st{p}", [128, 1], F32) for p in range(4)]
    bv_st = [nc.alloc_sbuf_tensor(f"bv_st{p}", [128, 1], F32) for p in range(4)]
    bo_st = [nc.alloc_sbuf_tensor(f"bo_st{c}", [128, 1], F32) for c in range(CG)]
    ew_d = nc.alloc_sbuf_tensor("ew_d", [128, 3 * CG], F32)
    ew_w = [nc.alloc_sbuf_tensor(f"ew_w{d}", [128, 1], F32) for d in range(CG)]
    ones_col = nc.alloc_sbuf_tensor("ones_col", [128, 1], F32)
    ones_row = nc.alloc_sbuf_tensor("ones_row", [1, 128], F32)
    flagv = nc.alloc_sbuf_tensor("flagv", [1, 1], I32)
    flagr = nc.alloc_sbuf_tensor("flagr", [1, 1], I32)
    eps1 = nc.alloc_sbuf_tensor("eps1", [1, 1], F32)
    eps2 = nc.alloc_sbuf_tensor("eps2", [1, 1], F32)
    srow = nc.alloc_sbuf_tensor("srow", [1, 512], F32)
    mrow = nc.alloc_sbuf_tensor("mrow", [1, 512], F32)
    vrow = nc.alloc_sbuf_tensor("vrow", [1, 512], F32)
    brow = nc.alloc_sbuf_tensor("brow", [1, 512], F32)
    sq_t = [nc.alloc_sbuf_tensor(f"sq{i}", [128, 512], F32) for i in range(2)]
    carry_sb = nc.alloc_sbuf_tensor("carry_sb", [128, 2 * CG], F32)
    carry_out = nc.alloc_sbuf_tensor("carry_out", [128, 2 * CG], F32)
    o_acc = nc.alloc_sbuf_tensor("o_acc", [128, 128], F32)
    o_bf = nc.alloc_sbuf_tensor("o_bf", [128, 128], BF16)
    mneg_t = [nc.alloc_sbuf_tensor(f"mneg{h}", [128, 1], F32) for h in range(2)]
    mx8 = [nc.alloc_sbuf_tensor(f"mx8_{h}", [128, 8], F32) for h in range(2)]
    lc8 = [nc.alloc_sbuf_tensor(f"lc8_{h}", [128, 8], F32) for h in range(2)]
    lsum = [nc.alloc_sbuf_tensor(f"lsum{h}", [128, 1], F32) for h in range(2)]
    lsc0 = nc.alloc_sbuf_tensor("lsc0", [128, 8], F32)
    lsc = [lsc0, lsc0]
    rl_t = [nc.alloc_sbuf_tensor(f"rl{h}", [128, 1], F32) for h in range(2)]

    SEGX_SZ = 64 * 1024
    base0 = (nc.sbuf_base + 31) & ~31
    avail = nc.sbuf_top - base0 - 64
    segX = SegAlloc(nc, base0, SEGX_SZ, "sx")
    segY = SegAlloc(nc, base0 + SEGX_SZ, avail - SEGX_SZ, "sy")

    pb = [nc.alloc_psum_tensor(f"pb{i}", [128, 512], F32) for i in range(8)]
    bank_free = {i: None for i in range(8)}

    sch = Sched()

    def mm_group(bank, mms, waits=()):
        w = [x for x in waits if x is not None]
        if bank_free[bank] is not None:
            w.append(bank_free[bank])
        n = len(mms)
        for i, (lhsT, rhs, out) in enumerate(mms):
            def f(t, lhsT=lhsT, rhs=rhs, out=out, st=(i == 0), sp=(i == n - 1)):
                return t.matmul(out, lhsT, rhs, start=st, stop=sp)
            sch.on("tensor", f, waits=w if i == 0 else ())

    def t_fence():
        sch.on("tensor", lambda t: (t.drain(), t.nop())[1], incs=(s_tn.inc(),))
        return s_tn.now()

    sch.on("gpsimd", lambda g: g.dma_start(out=vec_a[:, :], in_=veca),
           incs=(s_xld.inc(16),))
    sch.on("gpsimd", lambda g: g.dma_start(out=ffs_s[:, :], in_=ffs),
           incs=(s_xld.inc(16),))
    sch.on("gpsimd", lambda g: g.dma_start(out=vec_b[:, :], in_=vecb),
           incs=(s_xld.inc(16),))
    sch.on("gpsimd", lambda g: g.dma_start(out=flagv[:, :], in_=nonce),
           incs=(s_xld.inc(16),))
    setup_ld = s_xld.now()
    regs = {}

    def fsetup_regs(g):
        for r in ("nonce", "pid", "fv", "cnt", "t1", "t2"):
            regs[r] = g.alloc_register("R" + r)
        g.reg_load(regs["nonce"], flagv[0:1, 0:1])
        g.reg_mov(regs["pid"], g.partition_id())
        g.reg_mod(regs["pid"], regs["pid"], 2)
        return g.nop()
    sch.on("gpsimd", fsetup_regs, waits=(setup_ld,))

    sch.on("vector", lambda v: v.memset(ones_col[:, :], 1.0),
           incs=(s_vc.inc(),))
    sch.on("vector", lambda v: v.memset(eps1[:, :], LN_EPS),
           incs=(s_vc.inc(),))
    sch.on("vector", lambda v: v.memset(eps2[:, :], RMS_EPS),
           incs=(s_vc.inc(),))
    sch.on("vector", lambda v: v.memset(ones_row[:, :], 1.0),
           incs=(s_vc.inc(),))
    sch.on("vector", lambda v: v.memset(mx8[0][:, :], 0.0),
           incs=(s_vc.inc(),))
    sch.on("vector", lambda v: v.memset(mx8[1][:, :], 0.0),
           incs=(s_vc.inc(),))
    sch.on("vector",
           lambda v: v.tensor_scalar(mix1m[:, 0:24], vec_a[:, 16:40], -1.0, 1.0,
                                     ALU.mult, ALU.add),
           waits=(setup_ld,), incs=(s_vc.inc(),))
    for p in range(4):
        sch.on("vector", lambda v, p=p: v.tensor_scalar_mul(
            bq_st[p][:, :], vec_b[:, p:p + 1], 0.125), incs=(s_vc.inc(),))
        sch.on("vector", lambda v, p=p: v.tensor_copy(bk_st[p][:, :],
                                                      vec_b[:, 4 + p:5 + p]),
               incs=(s_vc.inc(),))
        sch.on("vector", lambda v, p=p: v.tensor_copy(bv_st[p][:, :],
                                                      vec_b[:, 8 + p:9 + p]),
               incs=(s_vc.inc(),))
    for c in range(CG):
        sch.on("vector", lambda v, c=c: v.tensor_copy(bo_st[c][:, :],
                                                      vec_b[:, 20 + c:21 + c]),
               incs=(s_vc.inc(),))
    setup_done = s_vc.now()

    def emit_ln_block(x_tiles, col0, W, out_tiles, ocol0, src_ready=()):
        mm_group(6, [(ones_col[:, 0:1], x_tiles[c][:, col0:col0 + W],
                      pb[6][0:1, 0:W]) for c in range(CG)], waits=src_ready)
        sq_guard = {}
        for c in range(CG):
            w = list(src_ready)
            if c >= 2:
                w.append((s_tn.h, sq_guard[c - 2]))
            def fsq(s, c=c, col0=col0, W=W):
                return s.activation(sq_t[c % 2][:, 0:W],
                                    x_tiles[c][:, col0:col0 + W], AF.Square)
            sch.on("scalar", fsq, waits=w, incs=(s_sc.inc(),))
            sqv = s_sc.now()
            bw = [sqv]
            if c == 0 and bank_free[7] is not None:
                bw.append(bank_free[7])
            def fm(t, c=c, W=W):
                return t.matmul(pb[7][0:1, 0:W], ones_col[:, 0:1],
                                sq_t[c % 2][:, 0:W], start=(c == 0),
                                stop=(c == CG - 1))
            sch.on("tensor", fm, waits=bw)
            sch.on("tensor", lambda t: (t.drain(), t.nop())[1],
                   incs=(s_tn.inc(),))
            sq_guard[c] = s_tn.n
        stats_tok = (s_tn.h, s_tn.n)
        sch.on("vector", lambda v, W=W: v.tensor_scalar_mul(
            mrow[0:1, 0:W], pb[6][0:1, 0:W], 1.0 / D), waits=(stats_tok,),
            incs=(s_vc.inc(),))
        sch.on("vector", lambda v, W=W: v.tensor_mul(
            vrow[0:1, 0:W], mrow[0:1, 0:W], mrow[0:1, 0:W]),
            waits=(s_vc.now(),), incs=(s_vc.inc(),))
        sch.on("vector", lambda v, W=W: v.scalar_tensor_tensor(
            vrow[0:1, 0:W], pb[7][0:1, 0:W], 1.0 / D, vrow[0:1, 0:W],
            ALU.mult, ALU.subtract), waits=(s_vc.now(),),
            incs=(s_vc.inc(),))
        var_tok = s_vc.now()
        sch.on("scalar", lambda s, W=W: s.activation(
            srow[0:1, 0:W], vrow[0:1, 0:W], AF.Sqrt, bias=eps1[0:1, 0:1]),
            waits=(var_tok,), incs=(s_sc.inc(),))
        sch.on("vector", lambda v, W=W: v.reciprocal(vrow[0:1, 0:W],
                                                     srow[0:1, 0:W]),
               waits=(s_sc.now(),), incs=(s_vc.inc(),))
        sch.on("vector", lambda v, W=W: v.scalar_tensor_tensor(
            brow[0:1, 0:W], mrow[0:1, 0:W], -1.0, vrow[0:1, 0:W],
            ALU.mult, ALU.mult), waits=(s_vc.now(),), incs=(s_vc.inc(),))
        rows_tok = s_vc.now()
        bank_free[6] = rows_tok
        bank_free[7] = rows_tok
        mm_group(6, [(ones_row[0:1, :], vrow[0:1, 0:W], pb[6][:, 0:W])],
                 waits=(rows_tok,))
        mm_group(7, [(ones_row[0:1, :], brow[0:1, 0:W], pb[7][:, 0:W])])
        bc_tok = t_fence()
        for c in range(CG):
            def fa(v, c=c, col0=col0, W=W):
                return v.tensor_mul(sq_t[c % 2][:, 0:W],
                                    x_tiles[c][:, col0:col0 + W], pb[6][:, 0:W])
            def fb(v, c=c, ocol0=ocol0, W=W):
                return v.tensor_add(out_tiles[c][:, ocol0:ocol0 + W],
                                    sq_t[c % 2][:, 0:W], pb[7][:, 0:W])
            sch.on("vector", fa,
                   waits=(bc_tok,) if c == 0 else (s_vc.now(),),
                   incs=(s_vc.inc(),))
            sch.on("vector", fb, waits=(s_vc.now(),), incs=(s_vc.inc(),))
        apply_tok = s_vc.now()
        bank_free[6] = apply_tok
        bank_free[7] = apply_tok
        return apply_tok

    xk_b = [segY.alloc([128, TOK], BF16, f"xk{c}") for c in range(CG)]
    xv_b = [segY.alloc([128, TOK], BF16, f"xv{c}") for c in range(CG)]
    xr_b = [segY.alloc([128, TOK], BF16, f"xr{c}") for c in range(CG)]
    rwkv_b = [segY.alloc([128, TOK], BF16, f"rw{c}") for c in range(CG)]
    ln_bf = [segX.alloc([128, TOK + 1], BF16, f"ln{c}") for c in range(CG)]
    x_blk = [segX.alloc([128, 512], F32, f"xblk{c}") for c in range(CG)]
    mixtmp = segX.alloc([128, 512], F32, "mixtmp")

    LN1_BLOCKS = [(0, 512), (512, 512), (1024, 512), (1536, 512), (1537, 512)]
    prev_apply = setup_done
    for (col0, W) in LN1_BLOCKS:
        for c in range(CG):
            def fl(g, c=c, col0=col0, W=W):
                return g.dma_start(out=x_blk[c][:, 0:W],
                                   in_=xin[128 * c:128 * (c + 1),
                                           col0:col0 + W])
            sch.on("gpsimd", fl, waits=(prev_apply,) if c == 0 else (),
                   incs=(s_xld.inc(16),))
        ld_tok = s_xld.now()
        prev_apply = emit_ln_block(x_blk, 0, W, ln_bf, col0,
                                   src_ready=(ld_tok,) if col0 else
                                   (ld_tok, setup_done))
    ln_done = prev_apply

    for (mi, out_t, tmcol, omcol) in ((0, xk_b, 16, 0), (1, xv_b, 24, 8),
                                      (2, xr_b, 32, 16)):
        for c in range(CG):
            for (t0, W) in TB4:
                def f1(v, c=c, t0=t0, W=W, tmcol=tmcol):
                    return v.tensor_scalar_mul(
                        mixtmp[:, 0:W], ln_bf[c][:, 1 + t0:1 + t0 + W],
                        vec_a[:, tmcol + c:tmcol + c + 1])
                def f2(v, c=c, t0=t0, W=W, omcol=omcol, out_t=out_t):
                    return v.scalar_tensor_tensor(
                        out_t[c][:, t0:t0 + W], ln_bf[c][:, t0:t0 + W],
                        mix1m[:, omcol + c:omcol + c + 1], mixtmp[:, 0:W],
                        ALU.mult, ALU.add)
                first = (mi == 0 and c == 0 and t0 == 0)
                sch.on("vector", f1,
                       waits=(ln_done,) if first else (s_vc.now(),),
                       incs=(s_vc.inc(),))
                sch.on("vector", f2, waits=(s_vc.now(),),
                       incs=(s_vc.inc(),))
    mix_done = s_vc.now()
    if DBG:
        d_ln0 = dbg("d_ln0", [128, TOK + 1], BF16)
        d_xk0 = dbg("d_xk0", [128, TOK], BF16)
        d_xr0 = dbg("d_xr0", [128, TOK], BF16)
        sch.on("sync", lambda sy: sy.dma_start(out=d_ln0, in_=ln_bf[0][:, :]),
               waits=(mix_done,), incs=(s_out.inc(16),))
        sch.on("sync", lambda sy: sy.dma_start(out=d_xk0, in_=xk_b[0][:, :]),
               incs=(s_out.inc(16),))
        sch.on("sync", lambda sy: sy.dma_start(out=d_xr0, in_=xr_b[0][:, :]),
               incs=(s_out.inc(16),))

    last_done = mix_done
    segX.reset()
    sr_d = segX.alloc([128, TOK], F32, "sr_d")
    ek_d = segX.alloc([128, TOK + 1], F32, "ek_d")
    ekv_d = segX.alloc([128, TOK + 1], F32, "ekv_d")
    aa_d = segX.alloc([128, TOK + 1], F32, "aa_d")
    bb_d = segX.alloc([128, TOK + 1], F32, "bb_d")
    ew_big = segX.alloc([128, TOK + 1], F32, "ew_big")
    wk_blk = [segX.alloc([128, D], BF16, f"wkb{i}") for i in range(2)]
    wv_blk = [segX.alloc([128, D], BF16, f"wvb{i}") for i in range(2)]
    wr_blk = [segX.alloc([128, D], BF16, f"wrb{i}") for i in range(2)]

    wblk_free = {}

    wsems = {}

    def wsem(key):
        if key not in wsems:
            wsems[key] = Sem(nc, "sw_" + "_".join(str(k) for k in key))
        return wsems[key]

    def load_wblk(dst, src_ap, idx, free_key, groups=CG):
        sem = wsem(free_key)
        def fl(g, dst=dst, src_ap=src_ap, idx=idx, groups=groups):
            return g.dma_start(out=dst[:, 0:groups * 128],
                               in_=src_ap[idx])
        sch.on("gpsimd", fl, waits=(wblk_free.get(free_key),),
               incs=(sem.inc(16),))
        return sem.now()

    def parity_open(g):
        return regs["pid"]

    def emit_poll(g, col, arm):
        sem, cntr = (s_exA, exA) if arm == 0 else (s_exB, exB)
        cntr[0] += 16
        g.dma_start(out=flagr[:, :],
                    in_=sh_fl.ap()[0:1, col:col + 1]).then_inc(sem.h, 16)
        g.wait_ge(sem.h, cntr[0])
        g.reg_mov(regs["cnt"], 0)

        def poll():
            g.dma_start(out=flagr[:, :],
                        in_=sh_fl.ap()[0:1, col:col + 1]).then_inc(s_junk.h, 16)
            g.drain()
            g.reg_add(regs["cnt"], regs["cnt"], 1)
            g.reg_load(regs["fv"], flagr[0:1, 0:1])
            g.reg_alu(regs["t1"], regs["fv"], regs["nonce"], ALU.not_equal)
            g.reg_alu(regs["t2"], regs["cnt"], POLL_CAP, ALU.is_lt)
            g.reg_alu(regs["t1"], regs["t1"], regs["t2"], ALU.logical_and)
            return regs["t1"]

        g.reg_load(regs["fv"], flagr[0:1, 0:1])
        g.reg_alu(regs["t1"], regs["fv"], regs["nonce"], ALU.not_equal)
        with g.If_ne(regs["t1"], 0):
            with g.While(poll):
                g.nop()

    for k in (("r", 0), ("r", 1), ("k", 0), ("k", 1), ("v", 0), ("v", 1)):
        wblk_free[k] = mix_done

    prev_y_tok = None
    for d in range(CG if PH >= 2 else 0):
        buf = d % 2
        wr_tok = load_wblk(wr_blk[buf], wrT, d, ("r", buf))
        wk_tok = load_wblk(wk_blk[buf], wkT, d, ("k", buf))
        wv_tok = load_wblk(wv_blk[buf], wvT, d, ("v", buf))
        for bi, (t0, W) in enumerate(TB4):
            bank = bi % 2
            mm_group(bank, [(wr_blk[buf][:, 128 * c:128 * (c + 1)],
                             xr_b[c][:, t0:t0 + W], pb[bank][:, 0:W])
                            for c in range(CG)], waits=(wr_tok, mix_done))
            ptok = t_fence()
            def fsg(s, bank=bank, t0=t0, W=W):
                return s.activation(sr_d[:, t0:t0 + W], pb[bank][:, 0:W],
                                    AF.Sigmoid)
            sch.on("scalar", fsg, waits=(ptok, prev_y_tok),
                   incs=(s_sc.inc(),))
            bank_free[bank] = s_sc.now()
        sr_tok = s_sc.now()
        for bi, (t0, W) in enumerate(TB4):
            bank = 2 + bi % 2
            mm_group(bank, [(wk_blk[buf][:, 128 * c:128 * (c + 1)],
                             xk_b[c][:, t0:t0 + W], pb[bank][:, 0:W])
                            for c in range(CG)], waits=(wk_tok,))
            ptok = t_fence()
            def fek(s, bank=bank, t0=t0, W=W):
                return s.activation(ek_d[:, 1 + t0:1 + t0 + W],
                                    pb[bank][:, 0:W], AF.Exp)
            sch.on("scalar", fek, waits=(ptok, prev_y_tok),
                   incs=(s_sc.inc(),))
            bank_free[bank] = s_sc.now()
        ek_tok = s_sc.now()
        def fcarry_recv(g, d=d):
            pid = parity_open(g)
            with g.If_eq(pid, 0):
                exA[0] += 16
                g.dma_start(out=carry_sb[:, 2 * d:2 * d + 2],
                            in_=czero).then_inc(s_exA.h, 16)
                g.wait_ge(s_exA.h, exA[0])
            with g.Else():
                emit_poll(g, d, 1)
                exB[0] += 16
                g.dma_start(out=carry_sb[:, 2 * d:2 * d + 2],
                            in_=sh_carry[d].ap()).then_inc(s_exB.h, 16)
                g.wait_ge(s_exB.h, exB[0])
            return g.nop()
        sch.on("gpsimd", fcarry_recv, incs=(s_gp.inc(),))
        carry_tok = s_gp.now()
        for bi, (t0, W) in enumerate(TB4):
            bank = 4 + bi % 2
            mm_group(bank, [(wv_blk[buf][:, 128 * c:128 * (c + 1)],
                             xv_b[c][:, t0:t0 + W], pb[bank][:, 0:W])
                            for c in range(CG)], waits=(wv_tok,))
            ptok = t_fence()
            def fekv(v, bank=bank, t0=t0, W=W):
                return v.tensor_mul(ekv_d[:, 1 + t0:1 + t0 + W],
                                    pb[bank][:, 0:W],
                                    ek_d[:, 1 + t0:1 + t0 + W])
            sch.on("vector", fekv, waits=(ptok, ek_tok), incs=(s_vc.inc(),))
            bank_free[bank] = s_vc.now()
        sch.on("vector", lambda v, d=d: v.tensor_copy(
            ekv_d[:, 0:1], carry_sb[:, 2 * d:2 * d + 1]), waits=(carry_tok,),
            incs=(s_vc.inc(),))
        sch.on("vector", lambda v, d=d: v.tensor_copy(
            ek_d[:, 0:1], carry_sb[:, 2 * d + 1:2 * d + 2]),
            incs=(s_vc.inc(),))
        cpy_tok = s_vc.now()
        sch.on("scalar", lambda s, d=d: s.activation(
            ew_d[:, 3 * d:3 * d + 1], vec_a[:, d:d + 1], AF.Exp),
            waits=(setup_done,), incs=(s_sc.inc(),))
        sch.on("scalar", lambda s, d=d: s.activation(
            ew_d[:, 3 * d + 1:3 * d + 2], ew_d[:, 3 * d:3 * d + 1],
            AF.Exp, scale=-1.0), waits=(s_sc.now(),), incs=(s_sc.inc(),))
        sch.on("scalar", lambda s, d=d: s.activation(
            ew_d[:, 3 * d + 2:3 * d + 3], vec_a[:, 8 + d:9 + d], AF.Exp),
            incs=(s_sc.inc(),))
        sch.on("scalar", lambda s, d=d: s.activation(
            ew_big[:, 0:TOK], sr_d[:, 0:TOK], AF.Identity, scale=0.0,
            bias=ew_d[:, 3 * d + 1:3 * d + 2]), waits=(s_sc.now(), sr_tok),
            incs=(s_sc.inc(),))
        sch.on("scalar", lambda s, d=d: s.activation(
            ew_big[:, TOK:TOK + 1], sr_d[:, 0:1], AF.Identity, scale=0.0,
            bias=ew_d[:, 3 * d + 1:3 * d + 2]), incs=(s_sc.inc(),))
        ewb_tok = s_sc.now()
        sch.on("vector", lambda v: v.tensor_tensor_scan(
            aa_d[:, :], ew_big[:, :], ekv_d[:, :], 0.0, ALU.mult, ALU.add),
            waits=(ewb_tok, cpy_tok), incs=(s_vc.inc(),))
        sch.on("vector", lambda v: v.tensor_tensor_scan(
            bb_d[:, :], ew_big[:, :], ek_d[:, :], 0.0, ALU.mult, ALU.add),
            incs=(s_vc.inc(),))
        scans_tok = s_vc.now()
        sch.on("vector", lambda v, d=d: v.tensor_copy(
            carry_out[:, 2 * d:2 * d + 1], aa_d[:, TOK:TOK + 1]),
            waits=(scans_tok,), incs=(s_vc.inc(),))
        sch.on("vector", lambda v, d=d: v.tensor_copy(
            carry_out[:, 2 * d + 1:2 * d + 2], bb_d[:, TOK:TOK + 1]),
            incs=(s_vc.inc(),))
        scan_tok = s_vc.now()
        def fcarry_send(g, d=d):
            pid = parity_open(g)
            with g.If_eq(pid, 0):
                exA[0] += 16
                g.dma_start(out=sh_carry[d].ap(),
                            in_=carry_out[:, 2 * d:2 * d + 2]
                            ).then_inc(s_exA.h, 16)
                g.wait_ge(s_exA.h, exA[0])
                exA[0] += 16
                g.dma_start(out=sh_fl.ap()[0:1, d:d + 1],
                            in_=flagv[:, :]).then_inc(s_exA.h, 16)
            with g.Else():
                g.nop()
            return g.nop()
        sch.on("gpsimd", fcarry_send, waits=(scan_tok,))
        if DBG and d in (0, 1, CG - 1):
            sfx = str(d) if d != CG - 1 else "7"
            for nm, tt in ((f"d_aa{sfx}", aa_d), (f"d_bb{sfx}", bb_d),
                           (f"d_ewb{sfx}", ew_big), (f"d_srg{sfx}", sr_d),
                           (f"d_ekg{sfx}", ek_d), (f"d_ekvg{sfx}", ekv_d)):
                a = dbg(nm, [128, tt.shape[1]])
                def fdmp(sy, a=a, tt=tt):
                    return sy.dma_start(out=a, in_=tt[:, :])
                sch.on("sync", fdmp, waits=(scan_tok,), incs=(s_out.inc(16),))
            sch.on("vector", lambda v: v.nop(), waits=(s_out.now(),))
        sch.on("vector", lambda v, d=d: v.scalar_tensor_tensor(
            aa_d[:, 0:TOK], ekv_d[:, 1:TOK + 1], ew_d[:, 3 * d + 2:3 * d + 3],
            aa_d[:, 0:TOK], ALU.mult, ALU.add), waits=(scan_tok,),
            incs=(s_vc.inc(),))
        sch.on("vector", lambda v, d=d: v.scalar_tensor_tensor(
            bb_d[:, 0:TOK], ek_d[:, 1:TOK + 1], ew_d[:, 3 * d + 2:3 * d + 3],
            bb_d[:, 0:TOK], ALU.mult, ALU.add), incs=(s_vc.inc(),))
        sch.on("vector", lambda v: v.reciprocal(ew_big[:, 0:TOK],
                                                bb_d[:, 0:TOK]),
               waits=(s_vc.now(),), incs=(s_vc.inc(),))
        sch.on("vector", lambda v: v.tensor_mul(aa_d[:, 0:TOK],
                                                aa_d[:, 0:TOK],
                                                ew_big[:, 0:TOK]),
               waits=(s_vc.now(),), incs=(s_vc.inc(),))
        sch.on("vector", lambda v, d=d: v.tensor_mul(
            rwkv_b[d][:, :], aa_d[:, 0:TOK], sr_d[:, :]),
            waits=(s_vc.now(),), incs=(s_vc.inc(),))
        prev_y_tok = s_vc.now()
    rwkv_done = s_vc.now()
    if DBG and PH >= 2:
        for dd in range(CG):
            ap = dbg(f"d_rw{dd}", [128, TOK], BF16)
            def fdr(sy, ap=ap, dd=dd):
                return sy.dma_start(out=ap, in_=rwkv_b[dd][:, :])
            sch.on("sync", fdr, incs=(s_out.inc(16),))
    if PH >= 2:
        last_done = rwkv_done

    segX.reset()
    x1 = [segX.alloc([128, TOK], F32, f"x1_{c}") for c in range(CG)]
    segY.reset()
    woa_blk = [segY.alloc([128, D], BF16, f"woab{i}") for i in range(2)]
    xi_blk = segY.alloc([128, 512], F32, "xi_blk")
    wblk_free[("woa", 0)] = rwkv_done
    wblk_free[("woa", 1)] = rwkv_done
    xi_free = rwkv_done
    for cg in range(CG if PH >= 3 else 0):
        buf = cg % 2
        wtok = load_wblk(woa_blk[buf], woTa, cg, ("woa", buf))
        for bi, (t0, W) in enumerate(TB4):
            bank = bi % 2
            mm_group(bank, [(woa_blk[buf][:, 128 * dd:128 * (dd + 1)],
                             rwkv_b[dd][:, t0:t0 + W], pb[bank][:, 0:W])
                            for dd in range(CG)], waits=(wtok, rwkv_done))
            ptok = t_fence()
            def fxl(g, cg=cg, t0=t0, W=W):
                return g.dma_start(out=xi_blk[:, 0:W],
                                   in_=xin[128 * cg:128 * (cg + 1),
                                           1 + t0:1 + t0 + W])
            sch.on("gpsimd", fxl, waits=(xi_free,), incs=(s_xld.inc(16),))
            xl_tok = s_xld.now()
            def fadd(v, cg=cg, t0=t0, W=W, bank=bank):
                return v.tensor_add(x1[cg][:, t0:t0 + W], pb[bank][:, 0:W],
                                    xi_blk[:, 0:W])
            sch.on("vector", fadd, waits=(ptok, xl_tok), incs=(s_vc.inc(),))
            bank_free[bank] = s_vc.now()
            xi_free = s_vc.now()
        wblk_free[("woa", buf)] = s_vc.now()
    x1_tok = s_vc.now()
    if PH >= 3:
        last_done = x1_tok

    def emit_ffn(ready_tok, tag):
        segY.reset()
        mark = segY.mark()
        lnb = [segY.alloc([128, 512], BF16, f"lnb{c}{tag}") for c in range(CG)]
        hbuf = [segY.alloc([128, 512], BF16, f"hb{h}{tag}") for h in range(HG)]
        gbuf = [segY.alloc([128, 512], BF16, f"gb{h}{tag}") for h in range(HG)]
        sA = segY.alloc([128, 512], F32, f"sA{tag}")
        w12 = [segY.alloc([128, D], BF16, f"w12_{i}{tag}") for i in range(8)]
        wbig = [segY.alloc([128, HID], BF16, f"wbig{i}{tag}") for i in range(4)]
        wbig_free = {i: ready_tok for i in range(4)}
        w12_free = {i: ready_tok for i in range(4)}
        last_tok = ready_tok
        for (t0, W) in TB4:
            ln_tok = emit_ln_block(x1, t0, W, lnb, 0, src_ready=(last_tok,))
            for h in range(HG):
                b0, b1 = (0, 1) if h % 2 == 0 else (2, 3)
                wb1, wb2 = w12[2 * (h % 4)], w12[2 * (h % 4) + 1]
                def fl1(g, wb1=wb1, h=h):
                    return g.dma_start(out=wb1[:, :], in_=w1T[h])
                def fl2(g, wb2=wb2, h=h):
                    return g.dma_start(out=wb2[:, :], in_=w2T[h])
                sem12 = wsem(("w12", h % 4))
                sch.on("gpsimd", fl1, waits=(w12_free.get(h % 4),),
                       incs=(sem12.inc(16),))
                sch.on("gpsimd", fl2, incs=(sem12.inc(16),))
                wl_tok = sem12.now()
                mm_group(b0, [(wb1[:, 128 * c:128 * (c + 1)], lnb[c][:, 0:W],
                               pb[b0][:, 0:W]) for c in range(CG)],
                         waits=(wl_tok, ln_tok))
                mm_group(b1, [(wb2[:, 128 * c:128 * (c + 1)], lnb[c][:, 0:W],
                               pb[b1][:, 0:W]) for c in range(CG)])
                ptok = t_fence()
                w12_free[h % 4] = ptok
                def fsilu(s, b0=b0, h=h, W=W):
                    return s.activation(sA[:, 0:W], pb[b0][:, 0:W], AF.Silu,
                                        scale=ffs_s[:, h:h + 1])
                sch.on("scalar", fsilu, waits=(ptok, s_vc.now()),
                       incs=(s_sc.inc(),))
                si_tok = s_sc.now()
                bank_free[b0] = si_tok
                def fh(v, b1=b1, h=h, W=W):
                    return v.scalar_tensor_tensor(
                        hbuf[h][:, 0:W], pb[b1][:, 0:W],
                        ffs_s[:, HG + h:HG + h + 1], sA[:, 0:W],
                        ALU.mult, ALU.mult)
                sch.on("vector", fh, waits=(si_tok,), incs=(s_vc.inc(),))
                bank_free[b1] = s_vc.now()
            stage_in_tok = s_vc.now()
            src, dst = hbuf, gbuf
            for (wT, scol) in ((h1T, 2 * HG), (h2T, 3 * HG)):
                for og in range(HG):
                    bank = og % 4
                    wb = wbig[og % 4]
                    def flw(g, wb=wb, wT=wT, og=og):
                        return g.dma_start(out=wb[:, :], in_=wT[og])
                    semb = wsem(("wbig", og % 4))
                    sch.on("gpsimd", flw, waits=(wbig_free.get(og % 4),),
                           incs=(semb.inc(16),))
                    wl_tok = semb.now()
                    mm_group(bank, [(wb[:, 128 * k:128 * (k + 1)],
                                     src[k][:, 0:W], pb[bank][:, 0:W])
                                    for k in range(HG)],
                             waits=(wl_tok, stage_in_tok))
                    ptok = t_fence()
                    wbig_free[og % 4] = ptok
                    def fg(s, bank=bank, og=og, W=W, scol=scol, dst=dst):
                        return s.activation(
                            dst[og][:, 0:W], pb[bank][:, 0:W], AF.Gelu,
                            scale=ffs_s[:, scol + og:scol + og + 1])
                    sch.on("scalar", fg, waits=(ptok,), incs=(s_sc.inc(),))
                    bank_free[bank] = s_sc.now()
                src, dst = dst, src
                stage_in_tok = s_sc.now()
            for cg in range(CG):
                bank = cg % 4
                wb = wbig[cg % 4]
                def flw3(g, wb=wb, cg=cg):
                    return g.dma_start(out=wb[:, :], in_=w3T[cg])
                semb = wsem(("wbig", cg % 4))
                sch.on("gpsimd", flw3, waits=(wbig_free.get(cg % 4),),
                       incs=(semb.inc(16),))
                wl_tok = semb.now()
                mm_group(bank, [(wb[:, 128 * k:128 * (k + 1)],
                                 src[k][:, 0:W], pb[bank][:, 0:W])
                                for k in range(HG)],
                         waits=(wl_tok, stage_in_tok))
                ptok = t_fence()
                wbig_free[cg % 4] = ptok
                def fx(v, bank=bank, cg=cg, t0=t0, W=W):
                    return v.scalar_tensor_tensor(
                        x1[cg][:, t0:t0 + W], pb[bank][:, 0:W],
                        ffs_s[:, 4 * HG + cg:4 * HG + cg + 1],
                        x1[cg][:, t0:t0 + W], ALU.mult, ALU.add)
                sch.on("vector", fx, waits=(ptok,), incs=(s_vc.inc(),))
                bank_free[bank] = s_vc.now()
            last_tok = s_vc.now()
        segY.reset(mark)
        return last_tok

    if PH >= 4:
        x2_tok = emit_ffn(x1_tok, "f1")
        last_done = x2_tok
    else:
        x2_tok = x1_tok

    segY.reset()
    markY0 = segY.mark()
    xn_loc = [segY.alloc([128, 512], BF16, f"xnl{c}") for c in range(CG)]
    t_f32 = [segY.alloc([128, 512], F32, f"t3_{c}") for c in range(CG)]
    last = x2_tok
    for bi, (t0, W) in enumerate(TB4 if PH >= 5 else []):
        ln_tok = emit_ln_block(x1, t0, W, t_f32, 0, src_ready=(last,))
        sq_guard = {}
        for c in range(CG):
            w = [ln_tok] if c == 0 else []
            if c >= 2:
                w.append((s_tn.h, sq_guard[c - 2]))
            def fsq(s, c=c, W=W):
                return s.activation(sq_t[c % 2][:, 0:W], t_f32[c][:, 0:W],
                                    AF.Square)
            sch.on("scalar", fsq, waits=w, incs=(s_sc.inc(),))
            sqv = s_sc.now()
            bw = [sqv]
            if c == 0 and bank_free[7] is not None:
                bw.append(bank_free[7])
            def fm(t, c=c, W=W):
                return t.matmul(pb[7][0:1, 0:W], ones_col[:, 0:1],
                                sq_t[c % 2][:, 0:W], start=(c == 0),
                                stop=(c == CG - 1))
            sch.on("tensor", fm, waits=bw)
            sch.on("tensor", lambda t: (t.drain(), t.nop())[1],
                   incs=(s_tn.inc(),))
            sq_guard[c] = s_tn.n
        rms_tok = (s_tn.h, s_tn.n)
        sch.on("scalar", lambda s, W=W: s.activation(
            srow[0:1, 0:W], pb[7][0:1, 0:W], AF.Sqrt, bias=eps2[0:1, 0:1],
            scale=1.0 / D), waits=(rms_tok,), incs=(s_sc.inc(),))
        sch.on("vector", lambda v, W=W: v.reciprocal(vrow[0:1, 0:W],
                                                     srow[0:1, 0:W]),
               waits=(s_sc.now(),), incs=(s_vc.inc(),))
        rr_tok = s_vc.now()
        bank_free[7] = rr_tok
        mm_group(7, [(ones_row[0:1, :], vrow[0:1, 0:W], pb[7][:, 0:W])],
                 waits=(rr_tok,))
        bc_tok = t_fence()
        for c in range(CG):
            def fa(v, c=c, W=W):
                return v.tensor_mul(sq_t[c % 2][:, 0:W], t_f32[c][:, 0:W],
                                    pb[7][:, 0:W])
            def fb(v, c=c, W=W):
                return v.tensor_scalar_mul(xn_loc[c][:, 0:W],
                                           sq_t[c % 2][:, 0:W],
                                           vec_b[:, 12 + c:13 + c])
            sch.on("vector", fa,
                   waits=(bc_tok,) if c == 0 else (s_vc.now(),),
                   incs=(s_vc.inc(),))
            sch.on("vector", fb, waits=(s_vc.now(),), incs=(s_vc.inc(),))
        xn_tok = s_vc.now()
        bank_free[7] = xn_tok
        def fship(g, t0=t0, W=W):
            pid = parity_open(g)
            with g.If_eq(pid, 0):
                exA[0] += CG * 16
                for c in range(CG):
                    g.dma_start(
                        out=sh_xn[0].ap()[128 * c:128 * (c + 1), t0:t0 + W],
                        in_=xn_loc[c][:, 0:W]).then_inc(s_exA.h, 16)
                g.wait_ge(s_exA.h, exA[0])
            with g.Else():
                exB[0] += CG * 16
                for c in range(CG):
                    g.dma_start(
                        out=sh_xn[1].ap()[128 * c:128 * (c + 1), t0:t0 + W],
                        in_=xn_loc[c][:, 0:W]).then_inc(s_exB.h, 16)
                g.wait_ge(s_exB.h, exB[0])
            return g.nop()
        sch.on("gpsimd", fship, waits=(xn_tok,), incs=(s_gp.inc(),))
        last = s_gp.now()

    def fxflag(g):
        pid = parity_open(g)
        with g.If_eq(pid, 0):
            exA[0] += 16
            g.dma_start(out=sh_fl.ap()[0:1, 8:9],
                        in_=flagv[:, :]).then_inc(s_exA.h, 16)
            emit_poll(g, 9, 0)
        with g.Else():
            exB[0] += 16
            g.dma_start(out=sh_fl.ap()[0:1, 9:10],
                        in_=flagv[:, :]).then_inc(s_exB.h, 16)
            emit_poll(g, 8, 1)
        return g.nop()
    if PH >= 5:
        sch.on("gpsimd", fxflag, incs=(s_gp.inc(),))
        last_done = s_gp.now()
    xn_ready = s_gp.now()

    segY.reset(markY0)
    o_fm = [segY.alloc([128, TFULL], BF16, f"ofm{p}") for p in range(4)]
    markY1 = segY.mark()
    q_fm = segY.alloc([128, TFULL], BF16, "q_fm")
    k_fm = segY.alloc([128, TFULL], BF16, "k_fm")
    v_fm = segY.alloc([128, TFULL], BF16, "v_fm")
    v_tok = segY.alloc([128, TFULL], BF16, "v_tok")
    xn_st = [segY.alloc([128, 512], BF16, f"xst{c}") for c in range(CG)]
    wq_blk = segY.alloc([128, D], BF16, "wq_blk")
    wkb_blk = segY.alloc([128, D], BF16, "wkb_blk")
    wvb_blk = segY.alloc([128, D], BF16, "wvb_blk")
    p_bf = [segY.alloc([128, 512], BF16, f"pbf{h}") for h in range(2)]
    pt_bf = [segY.alloc([128, 512], BF16, f"ptb{h}") for h in range(2)]
    mask01_s = segY.alloc([128, 4 * 512], BF16, "mask01_s")
    s_msk = wsem(("mask",))
    sch.on("gpsimd", lambda g: g.dma_start(out=mask01_s[:, :], in_=dmask01),
           waits=(x2_tok,), incs=(s_msk.inc(16),))
    mask_ld = s_msk.now()

    s_vtr = wsem(("vtr",))
    s_otr = wsem(("otr",))
    s_ptr = [wsem(("ptr", i)) for i in range(2)]
    xnst_free = None
    vt_free = None
    p_free = [None, None]
    pt_free = [None, None]
    obf_free = None
    ofm_tok = []
    wblk_free[("wq", 0)] = x2_tok
    wblk_free[("wkb", 0)] = x2_tok
    wblk_free[("wvb", 0)] = x2_tok
    for p in range(4 if PH >= 6 else 0):
        tq_tok = load_wblk(wq_blk, wqT, p, ("wq", 0))
        tk_tok = load_wblk(wkb_blk, wkTb, p, ("wkb", 0))
        tv_tok = load_wblk(wvb_blk, wvTb, p, ("wvb", 0))
        pw_toks = {"q": tq_tok, "k": tk_tok, "v": tv_tok}
        for bi, (t0, W) in enumerate(TB8):
            slot = 0 if t0 < TOK else 1
            c0 = t0 % TOK
            for c in range(CG):
                def fxn(g, c=c, slot=slot, c0=c0, W=W):
                    return g.dma_start(
                        out=xn_st[c][:, 0:W],
                        in_=sh_xn[slot].ap()[128 * c:128 * (c + 1),
                                             c0:c0 + W])
                w = (xn_ready, xnst_free) if c == 0 else ()
                sch.on("gpsimd", fxn, waits=w, incs=(s_xld.inc(16),))
            xl_tok = s_xld.now()
            for (bank, wb, dstt, kind) in ((0, wq_blk, q_fm, "q"),
                                           (1, wkb_blk, k_fm, "k"),
                                           (2, wvb_blk, v_fm, "v")):
                mm_group(bank, [(wb[:, 128 * c:128 * (c + 1)],
                                 xn_st[c][:, 0:W], pb[bank][:, 0:W])
                                for c in range(CG)],
                         waits=(pw_toks[kind], xl_tok))
                ptok = t_fence()
                if kind == "q":
                    def fe(s, bank=bank, dstt=dstt, t0=t0, W=W, p=p):
                        return s.activation(dstt[:, t0:t0 + W],
                                            pb[bank][:, 0:W], AF.Identity,
                                            bias=bq_st[p][:, 0:1],
                                            scale=0.125)
                elif kind == "k":
                    def fe(s, bank=bank, dstt=dstt, t0=t0, W=W, p=p):
                        return s.activation(dstt[:, t0:t0 + W],
                                            pb[bank][:, 0:W], AF.Identity,
                                            bias=bk_st[p][:, 0:1])
                else:
                    def fe(s, bank=bank, dstt=dstt, t0=t0, W=W, p=p):
                        return s.activation(dstt[:, t0:t0 + W],
                                            pb[bank][:, 0:W], AF.Identity,
                                            bias=bv_st[p][:, 0:1])
                sch.on("scalar", fe, waits=(ptok,), incs=(s_sc.inc(),))
                bank_free[bank] = s_sc.now()
            xnst_free = s_sc.now()
        proj_tok = s_sc.now()
        wblk_free[("wq", 0)] = proj_tok
        wblk_free[("wkb", 0)] = proj_tok
        wblk_free[("wvb", 0)] = proj_tok
        if DBG and p == 0:
            for nm, tt in (("d_qfm", q_fm), ("d_kfm", k_fm), ("d_vfm", v_fm),
                           ("d_bqst", bq_st[0]), ("d_bqsc", bq_sc)):
                a = dbg(nm, [128, tt.shape[1]],
                        BF16 if tt.dtype == BF16 else F32)
                def fdq(sy, a=a, tt=tt):
                    return sy.dma_start(out=a, in_=tt[:, :])
                sch.on("sync", fdq, waits=(proj_tok,), incs=(s_out.inc(16),))
        for tj in range(TFULL // 128):
            def ftr(s, tj=tj):
                return s.dma_start(out=v_tok[:, 128 * tj:128 * (tj + 1)],
                                   in_=v_fm[:, 128 * tj:128 * (tj + 1)],
                                   transpose=True)
            w = (proj_tok, vt_free) if tj == 0 else ()
            sch.on("sync", ftr, waits=w, incs=(s_vtr.inc(16),))
        vtr_tok = s_vtr.now()

        if PH >= 6 and PH < 7:
            last_done = vtr_tok
        for qi in range(NQT if PH >= 7 else 0):
            nblk = qi // 4 + 1
            for h2 in range(2):
                r0 = 64 * h2
                for j in range(nblk):
                    mm_group(j, [(q_fm[r0:r0 + 64, 128 * qi:128 * (qi + 1)],
                                  k_fm[r0:r0 + 64, 512 * j:512 * (j + 1)],
                                  pb[j][:, 0:512])],
                             waits=(proj_tok,) if j == 0 else ())
                sc_tok = t_fence()
                for j in range(nblk):
                    def fmx(v, h2=h2, j=j):
                        return v.reduce_max(mx8[h2][:, j:j + 1],
                                            pb[j][:, 0:512], axis=AX)
                    sch.on("vector", fmx, waits=(sc_tok,) if j == 0 else (),
                           incs=(s_vc.inc(),))
                def fcmb(v, h2=h2, nblk=nblk):
                    return v.reduce_max(mneg_t[h2][:, :],
                                        mx8[h2][:, 0:nblk], axis=AX,
                                        negate=True)
                sch.on("vector", fcmb, waits=(s_vc.now(),),
                       incs=(s_vc.inc(),))
                mstat_tok = s_vc.now()
                pv_started = False
                for j in range(nblk):
                    diag = (j == nblk - 1)
                    slot = j % 2
                    def fpexp(s, h2=h2, j=j, slot=slot):
                        return s.activation(p_bf[slot][:, 0:512],
                                            pb[j][:, 0:512], AF.Exp,
                                            bias=mneg_t[h2][:, 0:1],
                                            accum_out=lc8[h2][:, j:j + 1])
                    sch.on("scalar", fpexp,
                           waits=(mstat_tok, p_free[slot]),
                           incs=(s_sc.inc(),))
                    pexp_tok = s_sc.now()
                    bank_free[j] = pexp_tok
                    if diag:
                        mv = qi % 4
                        def fpm(v, slot=slot, mv=mv):
                            return v.tensor_mul(
                                p_bf[slot][:, 0:512], p_bf[slot][:, 0:512],
                                mask01_s[:, 512 * mv:512 * (mv + 1)])
                        sch.on("vector", fpm, waits=(pexp_tok, mask_ld),
                               incs=(s_vc.inc(),))
                        def fls(v, h2=h2, j=j, slot=slot):
                            return v.reduce_sum(lc8[h2][:, j:j + 1],
                                                p_bf[slot][:, 0:512], axis=AX)
                        sch.on("vector", fls, waits=(s_vc.now(),),
                               incs=(s_vc.inc(),))
                        pexp_tok = s_vc.now()
                    for u in range(4):
                        def ftp(sy, slot=slot, u=u):
                            return sy.dma_start(
                                out=pt_bf[slot][:, 128 * u:128 * (u + 1)],
                                in_=p_bf[slot][:, 128 * u:128 * (u + 1)],
                                transpose=True)
                        w = (pexp_tok, pt_free[slot]) if u == 0 else ()
                        sch.on("sync", ftp, waits=w,
                               incs=(s_ptr[slot].inc(16),))
                    p_free[slot] = s_ptr[slot].now()
                    ptr_tok = s_ptr[slot].now()
                    mms = [(pt_bf[slot][:, 128 * u:128 * (u + 1)],
                            v_tok[:, 128 * (4 * j + u) + r0:
                                  128 * (4 * j + u) + r0 + 64],
                            pb[0][:, 64 * h2:64 * h2 + 64]) for u in range(4)]
                    w = [ptr_tok, vtr_tok]
                    if j == 0:
                        w.append(pexp_tok if not diag else pexp_tok)
                    n = len(mms)
                    for i, (lhsT, rhs, outp) in enumerate(mms):
                        def fpv(t, lhsT=lhsT, rhs=rhs, outp=outp,
                                st=(j == 0 and i == 0),
                                sp=(j == nblk - 1 and i == n - 1)):
                            return t.matmul(outp, lhsT, rhs, start=st, stop=sp)
                        sch.on("tensor", fpv, waits=w if i == 0 else ())
                pv_tok = t_fence()
                pt_free[0] = pv_tok
                pt_free[1] = pv_tok
                def flsum(s, h2=h2, nblk=nblk):
                    return s.activation(lsc[h2][:, 0:nblk],
                                        lc8[h2][:, 0:nblk], AF.Identity,
                                        accum_out=lsum[h2][:, :])
                sch.on("scalar", flsum, waits=(pv_tok,), incs=(s_sc.inc(),))
                lsum_tok = s_sc.now()
                def frl(v, h2=h2):
                    return v.reciprocal(rl_t[h2][:, :], lsum[h2][:, :])
                sch.on("vector", frl, waits=(lsum_tok,), incs=(s_vc.inc(),))
                rl_tok = s_vc.now()
                def fOev(s, h2=h2):
                    return s.activation(o_bf[:, 64 * h2:64 * h2 + 64],
                                        pb[0][:, 64 * h2:64 * h2 + 64],
                                        AF.Identity, scale=rl_t[h2][:, 0:1])
                sch.on("scalar", fOev, waits=(rl_tok, obf_free),
                       incs=(s_sc.inc(),))
                bank_free[0] = s_sc.now()
            fin_tok = s_sc.now()
            def ftro(s, p=p, qi=qi):
                return s.dma_start(out=o_fm[p][:, 128 * qi:128 * (qi + 1)],
                                   in_=o_bf[:, :], transpose=True)
            sch.on("sync", ftro, waits=(fin_tok,), incs=(s_otr.inc(16),))
            obf_free = s_otr.now()
        vt_free = s_tn.now()
        if DBG and p == 0 and PH >= 7:
            for nm, tt in (("d_vtok", v_tok), ("d_ofm0", o_fm[0])):
                a = dbg(nm, [128, tt.shape[1]], BF16)
                def fdv(sy, a=a, tt=tt):
                    return sy.dma_start(out=a, in_=tt[:, :])
                sch.on("sync", fdv, waits=(s_otr.now(),),
                       incs=(s_out.inc(16),))
        ofm_tok.append(s_otr.now())
        if PH >= 7:
            last_done = s_otr.now()

    segY.reset(markY1)
    o_full = [segY.alloc([128, TOK], BF16, f"ofl{g}") for g in range(CG)]
    wob_blk = [segY.alloc([128, D], BF16, f"wob{i}") for i in range(2)]
    xnm_st = [segY.alloc([128, 512], BF16, f"xnm{c}") for c in range(CG)]
    osum = segY.alloc([128, 512], F32, "osum")

    flash_done = s_tn.now()

    def foship(g):
        pid = parity_open(g)
        with g.If_eq(pid, 0):
            exA[0] += 4 * 16
            for p in range(4):
                g.dma_start(out=sh_O[0].ap()[128 * p:128 * (p + 1), :],
                            in_=o_fm[p][:, :]).then_inc(s_exA.h, 16)
            g.wait_ge(s_exA.h, exA[0])
            exA[0] += 16
            g.dma_start(out=sh_fl.ap()[0:1, 10:11],
                        in_=flagv[:, :]).then_inc(s_exA.h, 16)
            emit_poll(g, 11, 0)
            exA[0] += CG * 16
            for gg in range(CG):
                g.dma_start(
                    out=o_full[gg][:, :],
                    in_=sh_O[gg // 4].ap()[128 * (gg % 4):128 * (gg % 4 + 1),
                                           0:TOK]).then_inc(s_exA.h, 16)
            g.wait_ge(s_exA.h, exA[0])
        with g.Else():
            exB[0] += 4 * 16
            for p in range(4):
                g.dma_start(out=sh_O[1].ap()[128 * p:128 * (p + 1), :],
                            in_=o_fm[p][:, :]).then_inc(s_exB.h, 16)
            g.wait_ge(s_exB.h, exB[0])
            exB[0] += 16
            g.dma_start(out=sh_fl.ap()[0:1, 11:12],
                        in_=flagv[:, :]).then_inc(s_exB.h, 16)
            emit_poll(g, 10, 1)
            exB[0] += CG * 16
            for gg in range(CG):
                g.dma_start(
                    out=o_full[gg][:, :],
                    in_=sh_O[gg // 4].ap()[128 * (gg % 4):128 * (gg % 4 + 1),
                                           TOK:TFULL]).then_inc(s_exB.h, 16)
            g.wait_ge(s_exB.h, exB[0])
        return g.nop()
    if PH >= 8:
        sch.on("gpsimd", foship, waits=tuple(ofm_tok) + (flash_done,),
               incs=(s_gp.inc(),))
    o_ready = s_gp.now()

    wblk_free[("wob", 0)] = flash_done
    wblk_free[("wob", 1)] = flash_done
    xnm_free = None
    for bi, (t0, W) in enumerate(TB4 if PH >= 8 else []):
        def fxnm(g, t0=t0, W=W):
            pid = parity_open(g)
            with g.If_eq(pid, 0):
                exA[0] += CG * 16
                for c in range(CG):
                    g.dma_start(
                        out=xnm_st[c][:, 0:W],
                        in_=sh_xn[0].ap()[128 * c:128 * (c + 1),
                                          t0:t0 + W]).then_inc(s_exA.h, 16)
                g.wait_ge(s_exA.h, exA[0])
            with g.Else():
                exB[0] += CG * 16
                for c in range(CG):
                    g.dma_start(
                        out=xnm_st[c][:, 0:W],
                        in_=sh_xn[1].ap()[128 * c:128 * (c + 1),
                                          t0:t0 + W]).then_inc(s_exB.h, 16)
                g.wait_ge(s_exB.h, exB[0])
            return g.nop()
        sch.on("gpsimd", fxnm, waits=(o_ready, xnm_free), incs=(s_gp.inc(),))
        xnm_tok = s_gp.now()
        for cg in range(CG):
            buf = cg % 2
            wtok = load_wblk(wob_blk[buf], woTb, cg, ("wob", buf))
            bank = cg % 4
            mm_group(bank, [(wob_blk[buf][:, 128 * dd:128 * (dd + 1)],
                             o_full[dd][:, t0:t0 + W], pb[bank][:, 0:W])
                            for dd in range(CG)], waits=(wtok, o_ready))
            ptok = t_fence()
            wblk_free[("wob", buf)] = ptok
            def fbo(s, bank=bank, cg=cg, W=W):
                return s.activation(osum[:, 0:W], pb[bank][:, 0:W], AF.Identity,
                                    bias=bo_st[cg][:, 0:1])
            sch.on("scalar", fbo, waits=(ptok, s_vc.now()),
                       incs=(s_sc.inc(),))
            bo_tok = s_sc.now()
            bank_free[bank] = bo_tok
            def fr1(v, cg=cg, W=W):
                return v.tensor_add(osum[:, 0:W], osum[:, 0:W],
                                    xnm_st[cg][:, 0:W])
            def fr2(v, cg=cg, t0=t0, W=W):
                return v.tensor_add(x1[cg][:, t0:t0 + W],
                                    x1[cg][:, t0:t0 + W], osum[:, 0:W])
            sch.on("vector", fr1, waits=(bo_tok, xnm_tok),
                   incs=(s_vc.inc(),))
            sch.on("vector", fr2, waits=(s_vc.now(),), incs=(s_vc.inc(),))
        xnm_free = s_vc.now()
    x3_tok = s_vc.now()
    if PH >= 8:
        last_done = x3_tok

    if PH >= 9:
        x4_tok = emit_ffn(x3_tok, "f2")
        last_done = x4_tok
    else:
        x4_tok = last_done

    for cg in range(CG):
        def fo2(s, cg=cg):
            return s.dma_start(out=out_fm[128 * cg:128 * (cg + 1), :],
                               in_=x1[cg][:, :])
        sch.on("sync", fo2, waits=(last_done,) if cg == 0 else (),
               incs=(s_out.inc(16),))
    sch.on("sync", lambda s: s.nop(), waits=(s_out.now(),))

    with nc.Block() as block:
        block.tensor(sch.runner("tensor"))
        block.vector(sch.runner("vector"))
        block.scalar(sch.runner("scalar"))
        block.gpsimd(sch.runner("gpsimd"))
        block.sync(sch.runner("sync"))

    return nc


def _blk(wt, og):
    i, o = wt.shape
    c = i // 128
    a = wt.reshape(i, og, 128).transpose(1, 0, 2)
    a = a.reshape(og, c, 128, 128).transpose(0, 2, 1, 3)
    return np.ascontiguousarray(a.reshape(og, 128, i))


def _col(v):
    return np.ascontiguousarray(v.reshape(-1, 128).T)


def kernel(**inp):
    x = np.asarray(inp["L0_attn_a_x"], np.float32)
    g = lambda n: np.asarray(inp[n], np.float32)

    base = dict(
        wkT=_blk(g("L0_attn_a_Wk").T, CG),
        wvT=_blk(g("L0_attn_a_Wv").T, CG),
        wrT=_blk(g("L0_attn_a_Wr").T, CG),
        woTa=_blk(g("L0_attn_a_Wo").T, CG),
        veca=np.concatenate([_col(g("L0_attn_a_time_decay")),
                             _col(g("L0_attn_a_time_first")),
                             _col(g("L0_attn_a_time_mix_k")),
                             _col(g("L0_attn_a_time_mix_v")),
                             _col(g("L0_attn_a_time_mix_r"))], axis=1),
        w1T=_blk(g("L0_ffn_w1_w").T, HG),
        w2T=_blk(g("L0_ffn_w2_w").T, HG),
        h1T=_blk(g("L0_ffn_h1_w").T, HG),
        h2T=_blk(g("L0_ffn_h2_w").T, HG),
        w3T=_blk(g("L0_ffn_w3_w").T, CG),
        ffs=np.concatenate([_col(g("L0_ffn_w1_s")), _col(g("L0_ffn_w2_s")),
                            _col(g("L0_ffn_h1_s")), _col(g("L0_ffn_h2_s")),
                            _col(g("L0_ffn_w3_s"))], axis=1),
        woTb=_blk(g("L0_attn_b_Wo").T, CG),
    )
    dmask = np.zeros((128, 4 * 512), np.float32)
    for mv in range(4):
        keep = np.arange(512)[None, :] <= (128 * mv + np.arange(128)[:, None])
        dmask[:, 512 * mv:512 * (mv + 1)] = np.where(keep, 0.0, -1e30)
    base["dmask"] = dmask
    base["dmask01"] = (dmask == 0.0).astype(np.float32)
    base["czero"] = np.zeros((128, 2), np.float32)
    base["nonce"] = np.array([[np.int64(np.random.randint(1, 2**31 - 1))]],
                             dtype=np.int32)

    in_maps = []
    for b in range(4):
        for hh in range(2):
            m = dict(base)
            t0 = TOK * hh
            xin = np.empty((D, TOK + 1), np.float32)
            if hh == 0:
                xin[:, 0] = 0.0
            else:
                xin[:, 0] = x[b, TOK - 1, :]
            xin[:, 1:] = x[b, t0:t0 + TOK, :].T
            m["xin"] = np.ascontiguousarray(xin)
            hs = slice(512 * hh, 512 * (hh + 1))
            m["wqT"] = _blk(g("L0_attn_b_Wq")[hs, :].T, 4)
            m["wkTb"] = _blk(g("L0_attn_b_Wk")[hs, :].T, 4)
            m["wvTb"] = _blk(g("L0_attn_b_Wv")[hs, :].T, 4)
            m["vecb"] = np.concatenate(
                [_col(g("L0_attn_b_bq")[hs]), _col(g("L0_attn_b_bk")[hs]),
                 _col(g("L0_attn_b_bv")[hs]), _col(g("L0_attn_b_norm_w")),
                 _col(g("L0_attn_b_bo"))], axis=1)
            in_maps.append(m)

    nc = build_program()
    res = run_bass_kernel_spmd(nc, in_maps, core_ids=list(range(8)))
    global LAST_RES
    LAST_RES = res

    out = np.empty((4, TFULL, D), np.float32)
    for b in range(4):
        for hh in range(2):
            out[b, TOK * hh:TOK * (hh + 1), :] = \
                res.results[2 * b + hh]["out_fm"].T
    return out
